# version 08
# baseline (speedup 1.0000x reference)
import numpy as np
import ml_dtypes
from contextlib import ExitStack

import concourse.bass as bass
import concourse.mybir as mybir
from concourse.bass_utils import run_bass_kernel_spmd

F32 = mybir.dt.float32
BF16 = mybir.dt.bfloat16
AF = mybir.ActivationFunctionType
ALU = mybir.AluOpType
AX = mybir.AxisListType

BF = ml_dtypes.bfloat16
NEG = np.float32(-1e30)

D = 2048
DK = 512
TQ = 2048
TKV = 4096
DFF = 1408
NFC = 11
NDC = 16
NEXP = 16
HALF = 1024
RSQ128 = float(1.0 / np.sqrt(128.0))


def own_tile_ids(parity):
    lo = [2 * i + parity for i in range(8)]
    hi = [31 - 2 * i - parity for i in range(8)]
    return sorted(lo + hi)


def build_masks(parity):
    J = own_tile_ids(parity)
    m = np.zeros((128, 16, 256), dtype=np.float32)
    for i in range(16):
        q = 128 * J[i] + np.arange(128)[:, None]
        k = 256 * i + np.arange(256)[None, :]
        m[:, i, :] = np.where(k <= q, 0.0, NEG)
    return m.astype(BF)


class Step:
    __slots__ = ("idx", "ch", "fn", "deps", "amt", "val", "key", "cond")

    def __init__(self, idx, ch, fn, deps, amt, key, cond=None):
        self.idx, self.ch, self.fn, self.deps, self.amt = idx, ch, fn, deps, amt
        self.val = None
        self.key = key
        self.cond = cond


class Sched:
    ENGS = ("T", "V", "S", "G", "Y")
    CHANNELS = ("T", "V", "S", "G", "Y", "dY", "dS", "dG", "dGc")
    QUEUE_OF = {"dY": "Y", "dS": "S", "dG": "G", "dGc": "G"}
    NOBAR = ("dGc",)
    NROT = 16

    def __init__(self, serialize=False):
        self.steps = []
        self.last = {}
        self.rot = {c: 0 for c in self.CHANNELS}
        self.serialize = serialize

    def _add(self, ch, fn, deps, amt, cond=None):
        deps = [d for d in deps if d is not None]
        if self.serialize and self.steps:
            deps = deps + [self.steps[-1]]
        if amt == 16:
            key = (ch, self.rot[ch] % self.NROT)
            self.rot[ch] += 1
        else:
            key = (ch, 0)
        st = Step(len(self.steps), ch, fn, deps, amt, key, cond)
        self.steps.append(st)
        self.last[key] = st
        return st

    def op(self, eng, fn, deps=(), cond=None):
        return self._add(eng, fn, list(deps), 1, cond)

    def dma(self, queue, fn, deps=()):
        return self._add(queue, fn, list(deps), 16)

    def snapshot(self):
        return [s for k, s in self.last.items() if k[0] not in self.NOBAR]

    def emit(self, nc, sems):
        referenced = set()
        for s in self.steps:
            for d in s.deps:
                referenced.add(d.idx)
        counts = {}
        for s in self.steps:
            if s.amt == 16 or s.idx in referenced:
                counts[s.key] = counts.get(s.key, 0) + s.amt
                s.val = counts[s.key]
        owner = {c: self.QUEUE_OF.get(c, c) for c in self.CHANNELS}

        with nc.Block() as blk:
            def emit_for(name, eng):
                water = {}
                state = {"cond": None, "guard": None, "incs": None, "reg": None,
                         "regsrc": None}

                def close_cond():
                    if state["cond"] is None:
                        return
                    state["guard"].__exit__(None, None, None)
                    if state["incs"]:
                        eg = eng.Else()
                        eg.__enter__()
                        waited = set()
                        for k, a, v in state["incs"]:
                            if k not in waited:
                                eng.wait_ge(sems[k], v - a)
                                waited.add(k)
                            eng.sem_inc(sems[k], a)
                        eg.__exit__(None, None, None)
                    state["cond"] = None
                    state["guard"] = None
                    state["incs"] = None

                def open_cond(cond):
                    ap_i32, thr = cond
                    if state["reg"] is None:
                        state["reg"] = eng.alloc_register("condreg")
                    if state["regsrc"] is not cond[0]:
                        eng.reg_load(state["reg"], ap_i32)
                        state["regsrc"] = cond[0]
                    g = eng.If_cmp(state["reg"], thr, "IS_GT")
                    g.__enter__()
                    state["cond"] = cond
                    state["guard"] = g
                    state["incs"] = []

                for s in self.steps:
                    if owner[s.ch] != name:
                        continue
                    if s.cond is not state["cond"]:
                        close_cond()
                    if s.cond is not None and state["cond"] is None:
                        for d in s.deps:
                            if s.ch == "T" and d.ch == "T":
                                continue
                            if water.get(d.key, 0) < d.val:
                                eng.wait_ge(sems[d.key], d.val)
                                water[d.key] = d.val
                        open_cond(s.cond)
                        instr = s.fn(eng)
                        if s.val is not None:
                            instr.then_inc(sems[s.key], s.amt)
                            state["incs"].append((s.key, s.amt, s.val))
                        continue
                    if s.cond is not None:
                        for d in s.deps:
                            if s.ch == "T" and d.ch == "T":
                                continue
                            if water.get(d.key, 0) < d.val:
                                eng.wait_ge(sems[d.key], d.val)
                        instr = s.fn(eng)
                        if s.val is not None:
                            instr.then_inc(sems[s.key], s.amt)
                            state["incs"].append((s.key, s.amt, s.val))
                        continue
                    for d in s.deps:
                        if s.ch == "T" and d.ch == "T":
                            continue
                        if water.get(d.key, 0) < d.val:
                            eng.wait_ge(sems[d.key], d.val)
                            water[d.key] = d.val
                    if s.amt == 16 and s.val > 16 and water.get(s.key, 0) < s.val - 16:
                        eng.wait_ge(sems[s.key], s.val - 16)
                        water[s.key] = s.val - 16
                    instr = s.fn(eng)
                    if s.val is not None:
                        instr.then_inc(sems[s.key], s.amt)
                close_cond()

            @blk.tensor
            def _(t):
                emit_for("T", t)

            @blk.vector
            def _(v):
                emit_for("V", v)

            @blk.scalar
            def _(s):
                emit_for("S", s)

            @blk.gpsimd
            def _(g):
                emit_for("G", g)

            @blk.sync
            def _(y):
                emit_for("Y", y)


def build_program(dbg=0, serialize=False):
    nc = bass.Bass("TRN2", target_bir_lowering=False)
    S = Sched(serialize=serialize)

    ap = lambda t: t.ap()
    xkv = ap(nc.dram_tensor("xkv", [TKV, D], F32, kind="ExternalInput"))
    xq = ap(nc.dram_tensor("xq", [TQ, D], F32, kind="ExternalInput"))
    masks_in = ap(nc.dram_tensor("masks", [128, 16, 256], BF16, kind="ExternalInput"))
    ident_in = ap(nc.dram_tensor("ident", [128, 128], BF16, kind="ExternalInput"))
    wq = ap(nc.dram_tensor("wq", [D, D], F32, kind="ExternalInput"))
    bq_r = ap(nc.dram_tensor("bq_r", [128, 16], F32, kind="ExternalInput"))
    wk = ap(nc.dram_tensor("wk", [DK, D], F32, kind="ExternalInput"))
    bk_r = ap(nc.dram_tensor("bk_r", [128, 4], F32, kind="ExternalInput"))
    wv = ap(nc.dram_tensor("wv", [DK, D], F32, kind="ExternalInput"))
    bv_in = ap(nc.dram_tensor("bv", [1, DK], F32, kind="ExternalInput"))
    proj = ap(nc.dram_tensor("proj", [D, D], F32, kind="ExternalInput"))
    sw1 = ap(nc.dram_tensor("sw1", [2, D, DFF], F32, kind="ExternalInput"))
    sb1_r = ap(nc.dram_tensor("sb1_r", [128, NFC, 2], F32, kind="ExternalInput"))
    sw2 = ap(nc.dram_tensor("sw2", [2, DFF, D], F32, kind="ExternalInput"))
    sb2 = ap(nc.dram_tensor("sb2", [2, D], F32, kind="ExternalInput"))
    rw1 = ap(nc.dram_tensor("rw1", [NEXP, D, DFF], F32, kind="ExternalInput"))
    rb1_r = ap(nc.dram_tensor("rb1_r", [128, NFC, NEXP], F32, kind="ExternalInput"))
    rw2 = ap(nc.dram_tensor("rw2", [NEXP, DFF, D], F32, kind="ExternalInput"))
    rb2 = ap(nc.dram_tensor("rb2", [NEXP, D], F32, kind="ExternalInput"))
    cent_r = ap(nc.dram_tensor("cent_r", [D, NEXP], F32, kind="ExternalInput"))
    cbias = ap(nc.dram_tensor("cbias", [NEXP, 1], F32, kind="ExternalInput"))
    iota_in = ap(nc.dram_tensor("iota_in", [128, 128], F32, kind="ExternalInput"))
    identf_in = ap(nc.dram_tensor("identf_in", [128, 128], F32, kind="ExternalInput"))

    dscr = "ExternalOutput" if dbg else "Internal"
    scr_kv = ap(nc.dram_tensor("scr_kv", [TKV, D], BF16, kind=dscr))
    scr_q = ap(nc.dram_tensor("scr_q", [TQ, D], BF16, kind=dscr))
    scr_qlo = ap(nc.dram_tensor("scr_qlo", [TQ, D], BF16, kind="Internal"))
    s_scr = ap(nc.dram_tensor("s_scr", [TQ, D], BF16, kind=dscr))
    scr_w1 = ap(nc.dram_tensor("scr_w1", [18, NFC, 128, NDC * 128], BF16, kind="Internal"))
    scr_w2 = ap(nc.dram_tensor("scr_w2", [18, 8, 128, NFC * 256], BF16, kind="Internal"))
    out = ap(nc.dram_tensor("out", [TQ, D], F32, kind="ExternalOutput"))
    if dbg:
        dbg_kt = ap(nc.dram_tensor("dbg_kt", [128, 4, TKV], BF16, kind="ExternalOutput"))
        dbg_v = ap(nc.dram_tensor("dbg_v", [128, 32, DK], BF16, kind="ExternalOutput"))
        dbg_qs = ap(nc.dram_tensor("dbg_qs", [128, 4, TQ], BF16, kind="ExternalOutput"))
        dbg_gat = ap(nc.dram_tensor("dbg_gat", [128, 16, NEXP], F32, kind="ExternalOutput"))
        dbg_acc = ap(nc.dram_tensor("dbg_acc", [128, 16, D], F32, kind="ExternalOutput"))

    def DMA(q, dst, src, deps=()):
        return S.dma(q, lambda e, a=dst, b=src: e.dma_start(a, b), deps)

    def DMT(q, dst, src, deps=()):
        return S.dma(q, lambda e, a=dst, b=src: e.dma_start_transpose(a, b), deps)

    with ExitStack() as CTX:
        sems = {}
        for c in Sched.CHANNELS:
            nrot = Sched.NROT if c.startswith("d") else 1
            for i in range(nrot):
                sems[(c, i)] = CTX.enter_context(nc.semaphore("ch_%s_%d" % (c, i)))
        A = lambda name, shape, dt=F32: CTX.enter_context(nc.sbuf_tensor(name, shape, dt))

        ident = A("ident_sb", [128, 128], BF16)
        bqs = A("bqs_sb", [128, 4])
        bks = A("bks_sb", [128, 4])
        bvrow = A("bvrow_sb", [1, DK], BF16)
        ones1 = A("ones1_sb", [1, 128], BF16)
        sb1s = A("sb1_sb", [128, NFC, 2])
        rb1s = A("rb1_sb", [128, NFC, NEXP])

        ones2 = A("ones2_sb", [2, 128], BF16)
        centhi = A("centhi_sb", [128, NDC, NEXP], BF16)
        centlo = A("centlo_sb", [128, NDC, NEXP], BF16)
        cbias_s = A("cbias_sb", [16, 1])

        ATT = ExitStack()
        AA = lambda name, shape, dt=F32: ATT.enter_context(nc.sbuf_tensor(name, shape, dt))
        Pfold = AA("pfold_sb", [128, 4, D], BF16)
        WKT = AA("wkt_sb", [128, NDC, DK], BF16)
        WVT = AA("wvt_sb", [128, NDC, DK], BF16)
        QsumT = AA("qsumt_sb", [128, 4, TQ], BF16)
        KT = AA("kt_sb", [128, 4, TKV], BF16)
        Vsb = AA("v_sb", [128, 32, DK], BF16)

        psA = CTX.enter_context(nc.psum_tensor("psA", [128, 512], F32))
        psB = CTX.enter_context(nc.psum_tensor("psB", [128, 512], F32))
        psC = CTX.enter_context(nc.psum_tensor("psC", [128, 512], F32))
        psD = CTX.enter_context(nc.psum_tensor("psD", [128, 512], F32))
        psE = CTX.enter_context(nc.psum_tensor("psE", [128, 512], F32))
        psF = CTX.enter_context(nc.psum_tensor("psF", [128, 512], F32))
        ptr1 = CTX.enter_context(nc.psum_tensor("ptr1", [128, 1024], BF16))
        ptr2 = CTX.enter_context(nc.psum_tensor("ptr2", [128, 1024], BF16))
        psY = [psC, psD, psE, psF]

        TP = {"k": 0, "ev": [None, None]}

        def TRP(src_ap, outP, outF, id_ap, evac_maker, deps, cond=None):
            k = TP["k"]
            TP["k"] += 1
            bank = [ptr1, ptr2][k % 2]
            pap = bank[0:outP, 0:outF]
            tst = S.op("T", lambda e, pap=pap, s=src_ap, ia=id_ap: e.transpose(pap, s, ia),
                       deps=list(deps) + [TP["ev"][k % 2]], cond=cond)
            ev = evac_maker(pap, tst)
            TP["ev"][k % 2] = ev
            return tst, ev

        P1wq = ExitStack()
        WQT = P1wq.enter_context(nc.sbuf_tensor("wqt_sb", [128, NDC, DK], BF16))
        P1a = ExitStack()
        A1 = lambda name, shape, dt=F32: P1a.enter_context(nc.sbuf_tensor(name, shape, dt))
        wstg = A1("wstg", [128, 2, D])
        wstg2 = A1("wstg2", [128, 2, D])
        wbfh = A1("wbfh", [128, 2, D], BF16)
        bqr_s = A1("bqr_s", [128, 16])
        centf = A1("centf", [128, NDC, NEXP])
        wbfh2 = A1("wbfh2", [128, 2, D], BF16)

        ld_ident = DMA("dY", ident[:, :], ident_in[:, :])
        ld_bqr = DMA("dY", bqr_s[:, :], bq_r[:, :])
        ld_bk = DMA("dY", bks[:, :], bk_r[:, :])
        ld_sb1 = DMA("dY", sb1s[:, :, :], sb1_r[:, :, :])
        ld_rb1 = DMA("dY", rb1s[:, :, :], rb1_r[:, :, :])
        ld_cb = DMA("dY", cbias_s[:, :], cbias[:, :])
        ld_centf = DMA("dY", centf[:, :, :], cent_r.rearrange("(dc p) j -> p dc j", p=128))
        ld_bv = DMA("dG", bvrow[:, :], bv_in[:, :])
        ld_centh = DMA("dG", centhi[:, :, :], cent_r.rearrange("(dc p) j -> p dc j", p=128))

        s1 = S.op("V", lambda e: e.tensor_add(bqs[:, :], bqr_s[:, 0:4], bqr_s[:, 4:8]), deps=[ld_bqr])
        s2 = S.op("V", lambda e: e.tensor_add(bqs[:, :], bqs[:, :], bqr_s[:, 8:12]), deps=[s1])
        bqs_done = S.op("V", lambda e: e.tensor_add(bqs[:, :], bqs[:, :], bqr_s[:, 12:16]), deps=[s2])
        clo = S.op("V", lambda e: e.tensor_sub(centlo[:, :, :], centf[:, :, :], centhi[:, :, :]),
                   deps=[ld_centf, ld_centh])
        ones_set = S.op("V", lambda e: e.memset(ones1[:, :], 1.0))
        ones2_set = S.op("V", lambda e: e.memset(ones2[:, :], 1.0))

        def fold_weight(src_dram, out_cvt_fn, prev_users):
            outs = []
            prevA = prev_users
            for ah in range(2):
                l0 = DMA("dG", wstg[:, :, :],
                         src_dram[256 * ah:256 * ah + 256, :].rearrange("(a p) d -> p a d", p=128),
                         deps=prevA)
                fold = l0
                addstep = None
                for r in range(1, 4):
                    lb = DMA("dG", wstg2[:, :, :],
                             src_dram[512 * r + 256 * ah:512 * r + 256 * ah + 256, :]
                             .rearrange("(a p) d -> p a d", p=128),
                             deps=[addstep] + prevA)
                    addstep = S.op("V", lambda e: e.tensor_add(wstg[:, :, :], wstg[:, :, :],
                                                              wstg2[:, :, :]),
                                   deps=[l0, lb, addstep])
                outs.append(out_cvt_fn(ah, addstep))
                prevA = [outs[-1][0]]
            return outs

        def emit_tr(srcbuf, a_local, ah, dst, dstcol_base, dep):
            evs = []
            for dc in range(NDC):
                def mk(pap, tst, d=dc, ds=dst, cb=dstcol_base):
                    return S.op("V", lambda e: e.tensor_copy(ds[:, d, cb:cb + 128], pap),
                                deps=[tst])
                tst, ev = TRP(srcbuf[:, a_local, dc * 128:(dc + 1) * 128], 128, 128,
                              ident[:, :], mk, [dep, ld_ident])
                evs.append(ev)
            return evs

        def wq_cvt(ah, fold_step):
            cvt = S.op("V", lambda e: e.tensor_copy(wbfh[:, :, :], wstg[:, :, :]), deps=[fold_step])
            evs = []
            for al in range(2):
                evs += emit_tr(wbfh, al, ah, WQT, (2 * ah + al) * 128, cvt)
            return [evs[-1], cvt] + evs

        wq_outs = fold_weight(wq, wq_cvt, [])
        wqt_done = wq_outs[-1][0]
        wq_users = [s for o in wq_outs for s in o]

        def pf_cvt(ah, fold_step):
            cvt = S.op("V", lambda e, ah=ah: e.tensor_scalar(
                Pfold[:, 2 * ah:2 * ah + 2, :], wstg[:, :, :], RSQ128, None, ALU.mult),
                deps=[fold_step])
            return [cvt]

        pf_outs = fold_weight(proj, pf_cvt, wq_users)
        cvt_pf = pf_outs[-1][0]


        kv_tr_last = {}
        prev_half_load = None
        for (wsrc, dst) in ((wk, WKT), (wv, WVT)):
            for ah in range(2):
                lh = DMA("dG", wbfh2[:, :, :],
                         wsrc[256 * ah:256 * ah + 256, :].rearrange("(a p) d -> p a d", p=128),
                         deps=[prev_half_load])
                evs = []
                for al in range(2):
                    evs += emit_tr(wbfh2, al, ah, dst, (2 * ah + al) * 128, lh)
                prev_half_load = evs[-1]
                kv_tr_last[id(dst)] = evs[-1]
        wkt_done = kv_tr_last[id(WKT)]
        wvt_done = kv_tr_last[id(WVT)]

        conv_w1_done = {}
        conv_w2_done = {}
        conv_order = [16, 17] + list(range(16))
        for j in conv_order:
            w1_ap = sw1[j - 16] if j >= 16 else rw1[j]
            w2_ap = sw2[j - 16] if j >= 16 else rw2[j]
            for fs in range(NFC):
                wr = S.dma("dGc", lambda e,
                           a=scr_w1[j, fs].rearrange("p (dc f) -> p dc f", f=128),
                           b=w1_ap[:, fs * 128:(fs + 1) * 128].rearrange("(dc p) f -> p dc f", p=128):
                           e.dma_start(a, b), deps=[])
                conv_w1_done[(j, fs)] = wr
            for hb in range(8):
                wr = S.dma("dGc", lambda e,
                           a=scr_w2[j, hb].rearrange("p (fc x) -> p fc x", x=256),
                           b=w2_ap[:, hb * 256:(hb + 1) * 256].rearrange("(fc p) d -> p fc d", p=128):
                           e.dma_start(a, b), deps=[])
                conv_w2_done[(j, hb)] = wr
        cv1_last = [conv_w1_done[(15, NFC - 1)], conv_w1_done[(15, NFC - 2)]]
        cv2_last = [conv_w2_done[(15, 7)], conv_w2_done[(15, 6)]]

        bar_snap = S.snapshot()
        bar_steps = [S.op(e, lambda eng: eng.nop(), deps=bar_snap) for e in ("T", "V", "S", "Y")]

        P1a.close()

        PHq = ExitStack()
        A2 = lambda name, shape, dt=F32: PHq.enter_context(nc.sbuf_tensor(name, shape, dt))
        xt = [A2("xt0", [128, D]), A2("xt1", [128, D])]
        stats = [A2("st0", [128, 8]), A2("st1", [128, 8])]
        hib = [A2("hib0", [128, D], BF16), A2("hib1", [128, D], BF16)]
        lob = A2("lob", [128, D], BF16)
        xfb = A2("xfb", [128, D])
        xnWq = A2("xnWq", [128, NDC, 1, 512], BF16)

        def ln_tile(bufs, src_ap, row0, b, want_lo, prevu, dst_scr, prev_lo):
            xt, stats, hib, lob, xfb = bufs
            xb = xt[b]
            hb = hib[b]
            dummy = xfb if xfb is not None else hb
            ld = DMA("dY", xb[:, :], src_ap[row0:row0 + 128, :], deps=prevu)
            st = stats[b]
            r1 = S.op("V", lambda e, xb=xb: e.reduce_sum(st[:, 0:1], xb[:, :], axis=AX.X), deps=[ld])
            sq = S.op("S", lambda e, xb=xb, dummy=dummy: e.activation(dummy[:, :], xb[:, :], AF.Square,
                                                                     accum_out=st[:, 4:5]),
                      deps=[ld] + prevu + prev_lo)
            m = S.op("V", lambda e: e.tensor_scalar(st[:, 1:2], st[:, 0:1], float(1.0 / D), None, ALU.mult),
                     deps=[r1])
            v2 = S.op("V", lambda e: e.tensor_scalar(st[:, 2:3], st[:, 4:5], float(1.0 / D), 1e-5,
                                                     ALU.mult, ALU.add), deps=[sq])
            mm = S.op("V", lambda e: e.tensor_mul(st[:, 3:4], st[:, 1:2], st[:, 1:2]), deps=[m])
            vv = S.op("V", lambda e: e.tensor_sub(st[:, 2:3], st[:, 2:3], st[:, 3:4]), deps=[v2, mm])
            lnv = S.op("S", lambda e: e.activation(st[:, 7:8], st[:, 2:3], AF.Ln), deps=[vv])
            rstd = S.op("S", lambda e: e.activation(st[:, 5:6], st[:, 7:8], AF.Exp, scale=-0.5), deps=[lnv])
            nmr = S.op("V", lambda e: e.tensor_scalar(st[:, 6:7], st[:, 1:2], st[:, 5:6], -1.0,
                                                      ALU.mult, ALU.mult), deps=[rstd])
            hi = S.op("S", lambda e, xb=xb, hb=hb: e.activation(hb[:, :], xb[:, :], AF.Identity,
                                                                bias=st[:, 6:7], scale=st[:, 5:6]),
                      deps=[nmr] + prevu)
            lo_store = None
            lo_users = []
            xf = lo = None
            if want_lo:
                xf = S.op("S", lambda e, xb=xb: e.activation(xfb[:, :], xb[:, :], AF.Identity,
                                                             bias=st[:, 6:7], scale=st[:, 5:6]),
                          deps=[nmr, sq] + prev_lo)
                lo = S.op("V", lambda e, hb=hb: e.tensor_sub(lob[:, :], xfb[:, :], hb[:, :]),
                          deps=[xf, hi] + prev_lo)
                lo_store = DMA("dY", scr_qlo[row0:row0 + 128, :], lob[:, :], deps=[lo])
                lo_users = [xf, lo, lo_store]
            hi_store = DMA("dY", dst_scr[row0:row0 + 128, :], hb[:, :], deps=[hi])
            last = [hi_store, sq, hi, r1]
            if want_lo:
                last += [xf, lo]
            return hi_store, lo_store, last, lo_users

        q_hi = []
        q_lo = []
        prev_lo = []
        prevu = [list(bar_steps), list(bar_steps)]
        qwin_users = [[], []]
        pq_ev = [None, None]
        qev_all = []
        for tc in range(4):
            for kk in range(4):
                k = tc * 4 + kk
                hs, ls, lu, lou = ln_tile((xt, stats, hib, lob, xfb), xq, 128 * k, k % 2, True,
                                          prevu[k % 2], scr_q, prev_lo)
                q_hi.append(hs)
                q_lo.append(ls)
                prevu[k % 2] = lu
                prev_lo = lou
            w = 0
            trs = []
            for dc in range(NDC):
                t = DMT("dS", xnWq[:, dc, w, :],
                        scr_q[tc * 512:(tc + 1) * 512, dc * 128:(dc + 1) * 128],
                        deps=q_hi[4 * tc:4 * tc + 4] + qwin_users[w] + bar_steps)
                trs.append(t)
            users = []
            for a in range(4):
                g = a % 2
                ps = [psA, psB][g]

                def qfn(e, a=a, w=w, ps=ps):
                    for dc in range(NDC):
                        ins = e.matmul(ps[:, :], WQT[:, dc, a * 128:(a + 1) * 128],
                                       xnWq[:, dc, w, :], start=(dc == 0), stop=(dc == NDC - 1))
                    return ins
                mmst = S.op("T", qfn, deps=trs + [wqt_done, pq_ev[g]])
                ev = S.op("S", lambda e, a=a, tc=tc, ps=ps: e.activation(
                    QsumT[:, a, tc * 512:(tc + 1) * 512], ps[:, :], AF.Identity,
                    bias=bqs[:, a:a + 1]), deps=[mmst, bqs_done])
                pq_ev[g] = ev
                qev_all.append(ev)
                users.append(mmst)
            qwin_users[w] = users

        bar_snap = S.snapshot()
        bar_steps = [S.op(e, lambda eng: eng.nop(), deps=bar_snap) for e in ("T", "V", "S", "Y")]
        PHq.close()
        P1wq.close()


        PHkv = ExitStack()
        A2 = lambda name, shape, dt=F32: PHkv.enter_context(nc.sbuf_tensor(name, shape, dt))
        xt = [A2("kxt0", [128, D]), A2("kxt1", [128, D])]
        stats = [A2("kst0", [128, 8]), A2("kst1", [128, 8])]
        hib = [A2("khib0", [128, D], BF16), A2("khib1", [128, D], BF16)]
        xnWk = A2("xnWk", [128, NDC, 1, 512], BF16)

        kv_hi = []
        prevu = [list(bar_steps), list(bar_steps)]
        prev_lo = []
        for k in range(32):
            hs, _, lu, _ = ln_tile((xt, stats, hib, None, None), xkv, 128 * k, k % 2, False,
                                   prevu[k % 2], scr_kv, prev_lo)
            kv_hi.append(hs)
            prevu[k % 2] = lu

        win_users = [bar_steps, bar_steps]
        pk_ev = [qev_all[-1], qev_all[-1]]
        pv_ev = [None, None]
        k_ev_all = []
        v_ev_all = []
        for tc in range(8):
            w = 0
            trs = []
            for dc in range(NDC):
                t = DMT("dS", xnWk[:, dc, w, :],
                        scr_kv[tc * 512:(tc + 1) * 512, dc * 128:(dc + 1) * 128],
                        deps=kv_hi[4 * tc:4 * tc + 4] + list(win_users[w]))
                trs.append(t)
            users = []
            for a in range(4):
                g = a % 2
                ps = [psA, psB][g]

                def kfn(e, a=a, w=w, ps=ps):
                    for dc in range(NDC):
                        ins = e.matmul(ps[:, :], WKT[:, dc, a * 128:(a + 1) * 128],
                                       xnWk[:, dc, w, :], start=(dc == 0), stop=(dc == NDC - 1))
                    return ins
                mmst = S.op("T", kfn, deps=trs + [wkt_done, pk_ev[g]])
                ev = S.op("S", lambda e, a=a, tc=tc, ps=ps: e.activation(
                    KT[:, a, tc * 512:(tc + 1) * 512], ps[:, :], AF.Identity,
                    bias=bks[:, a:a + 1]), deps=[mmst, ld_bk])
                pk_ev[g] = ev
                k_ev_all.append(ev)
                users.append(mmst)
            for sub in range(4):
                g = sub % 2
                ps = [psC, psD][g]
                kb = tc * 4 + sub

                def vfn(e, sub=sub, w=w, ps=ps):
                    for dc in range(NDC):
                        e.matmul(ps[:, :], xnWk[:, dc, w, sub * 128:(sub + 1) * 128],
                                 WVT[:, dc, :], start=(dc == 0), stop=False)
                    return e.matmul(ps[:, :], ones1[:, :], bvrow[:, :], start=False, stop=True)
                mmst = S.op("T", vfn, deps=trs + [wvt_done, ld_bv, ones_set, pv_ev[g]])
                ev = S.op("V", lambda e, kb=kb, ps=ps: e.tensor_copy(Vsb[:, kb, :], ps[:, :]),
                          deps=[mmst])
                pv_ev[g] = ev
                v_ev_all.append(ev)
                users.append(mmst)
            win_users[w] = users

        if dbg:
            d1 = DMA("dY", dbg_kt[:, :, :], KT[:, :, :], deps=k_ev_all)
            d2 = DMA("dY", dbg_v[:, :, :], Vsb[:, :, :], deps=v_ev_all)
            d3 = DMA("dY", dbg_qs[:, :, :], QsumT[:, :, :], deps=qev_all)
            dbg_proj = [d1, d2, d3]
        else:
            dbg_proj = []

        bar_snap = S.snapshot()
        bar_steps = [S.op(e, lambda eng: eng.nop(), deps=bar_snap) for e in ("T", "V", "S", "Y")]
        PHkv.close()


        P4 = ExitStack()
        A4 = lambda name, shape, dt=F32: P4.enter_context(nc.sbuf_tensor(name, shape, dt))
        masks = A4("masks_sb", [128, 16, 256], BF16)
        srow = A4("srow", [128, TKV])
        Pbf = A4("pbf", [128, TKV], BF16)
        PTb = A4("ptb", [128, 32, 128], BF16)
        Zbf = A4("zbf", [128, 512], BF16)
        ZTb = A4("ztb", [128, 4, 128], BF16)
        stile = [A4("stile0", [128, D], BF16), A4("stile1", [128, D], BF16)]
        smax = A4("smax", [128, 2])
        ssum = A4("ssum", [128, 2])

        ld_masks = DMA("dY", masks[:, :, :], masks_in[:, :, :], deps=bar_steps)

        psS_ev = [None, None]
        s_stores = [None, None]
        prev_srow_reader = None
        prev_pbf_reader = None
        prev_rcp = None
        prev_av = None
        prev_zread = None
        psZ_ev = None
        psSP_ev = [None, None]
        s_store_list = []

        for i in range(16):
            nkeys = 256 * (i + 1)
            nkb = nkeys // 128
            blocks = []
            off = 0
            while off < nkeys:
                ln_ = min(512, nkeys - off)
                blocks.append((off, ln_))
                off += ln_
            ev_blocks = []
            for bi, (off, ln_) in enumerate(blocks):
                g = bi % 2
                ps = [psA, psB][g]

                def scfn(e, i=i, off=off, ln_=ln_, ps=ps):
                    for a in range(4):
                        ins = e.matmul(ps[:, 0:ln_], QsumT[:, a, i * 128:(i + 1) * 128],
                                       KT[:, a, off:off + ln_], start=(a == 0), stop=(a == 3))
                    return ins
                tc_max = (off + ln_ - 1) // 512
                mmst = S.op("T", scfn,
                            deps=[psS_ev[g]] + qev_all[:4 * (i // 4 + 1)]
                            + k_ev_all[:4 * (tc_max + 1)] + dbg_proj)
                ev = S.op("V", lambda e, off=off, ln_=ln_, ps=ps: e.tensor_copy(
                    srow[:, off:off + ln_], ps[:, 0:ln_]),
                    deps=[mmst, prev_srow_reader])
                psS_ev[g] = ev
                ev_blocks.append(ev)
            mk = S.op("V", lambda e, i=i, nk=nkeys: e.tensor_add(
                srow[:, nk - 256:nk], srow[:, nk - 256:nk], masks[:, i, :]),
                deps=[ev_blocks[-1], ld_masks])
            nmx = S.op("V", lambda e, nk=nkeys: e.reduce_max(
                smax[:, 0:1], srow[:, 0:nk], axis=AX.X, negate=True),
                deps=[mk, prev_pbf_reader] + ev_blocks)
            exps = S.op("S", lambda e, nk=nkeys: e.activation(
                Pbf[:, 0:nk], srow[:, 0:nk], AF.Exp, bias=smax[:, 0:1],
                accum_out=ssum[:, 0:1]),
                deps=[nmx, prev_pbf_reader, prev_rcp])
            rcp = S.op("V", lambda e: e.reciprocal(ssum[:, 1:2], ssum[:, 0:1]),
                       deps=[exps, prev_zread])
            prev_rcp = rcp
            pt_evs = []
            last_pt_t = None
            for kb in range(nkb):
                def mk(pap, tst, kb=kb):
                    return S.op("V", lambda e: e.tensor_copy(PTb[:, kb, :], pap),
                                deps=[tst, prev_av])
                tst, evt = TRP(Pbf[:, kb * 128:(kb + 1) * 128], 128, 128, ident[:, :],
                               mk, [exps, prev_av])
                pt_evs.append(evt)
                last_pt_t = tst

            def avfn(e, nkb=nkb):
                for kb in range(nkb):
                    ins = e.matmul(psC[:, :], PTb[:, kb, :], Vsb[:, kb, :],
                                   start=(kb == 0), stop=(kb == nkb - 1))
                return ins
            av = S.op("T", avfn,
                      deps=pt_evs + v_ev_all[:4 * ((nkeys - 1) // 512 + 1)] + [psZ_ev])
            prev_av = av
            prev_pbf_reader = last_pt_t
            prev_srow_reader = exps
            zev = S.op("V", lambda e: e.tensor_scalar(
                Zbf[:, :], psC[:, :], ssum[:, 1:2], None, ALU.mult),
                deps=[av, rcp, prev_zread])
            psZ_ev = zev
            zt_evs = []
            zt_ts = []
            for c in range(4):
                def mk(pap, tst, c=c):
                    return S.op("V", lambda e: e.tensor_copy(ZTb[:, c, :], pap),
                                deps=[tst, prev_zread])
                tst, evt = TRP(Zbf[:, c * 128:(c + 1) * 128], 128, 128, ident[:, :],
                               mk, [zev, prev_zread])
                zt_evs.append(evt)
                zt_ts.append(tst)
            prev_zread = zt_ts[-1]
            sb = i % 2
            sp_evs = []
            for dg in range(4):
                g = dg % 2
                ps = [psE, psF][g]

                def spfn(e, dg=dg, ps=ps):
                    for c in range(4):
                        ins = e.matmul(ps[:, :], ZTb[:, c, :], Pfold[:, c, dg * 512:(dg + 1) * 512],
                                       start=(c == 0), stop=(c == 3))
                    return ins
                mmst = S.op("T", spfn, deps=zt_evs + [cvt_pf, psSP_ev[g]])
                ev = S.op("S", lambda e, dg=dg, ps=ps, sb=sb: e.activation(
                    stile[sb][:, dg * 512:(dg + 1) * 512], ps[:, :], AF.Copy),
                    deps=[mmst, s_stores[sb]])
                psSP_ev[g] = ev
                sp_evs.append(ev)
            sst = DMA("dY", s_scr[i * 128:(i + 1) * 128, :], stile[sb][:, :], deps=sp_evs)
            s_stores[sb] = sst
            s_store_list.append(sst)

        conv_tail = [s for k, s in S.last.items() if k[0] == "dGc"]
        bar_snap = S.snapshot() + conv_tail
        bar_steps = [S.op(e, lambda eng: eng.nop(), deps=bar_snap) for e in ("T", "V", "S", "Y")]
        P4.close()
        ATT.close()

        import struct as _struct

        def fbits(x):
            return _struct.unpack("<i", _struct.pack("<f", float(x)))[0]

        P5 = ExitStack()
        A5 = lambda name, shape, dt=F32: P5.enter_context(nc.sbuf_tensor(name, shape, dt))
        acc = A5("acc", [128, 8, D])
        w2b = A5("w2b", [128, NFC, 2, 512], BF16)
        w1b = [A5("w1b0", [128, NDC, 128], BF16), A5("w1b1", [128, NDC, 128], BF16)]
        gating = A5("gating", [128, 8, NEXP])
        gatb = A5("gatb", [128, 8, NEXP], BF16)
        posm_tm = A5("posm_tm", [128, 8, 16])
        Lrow = A5("lrow", [1, 16])
        ggath = A5("ggath", [128, 8])
        iota128 = A5("iota128_sb", [128, 128])
        identf = A5("identf_sb", [128, 128])
        onescol = A5("onescol_sb", [128, 1], BF16)
        zcol = A5("zcol_sb", [1, 128], BF16)
        zrow = A5("zrow_sb", [1, 1], BF16)

        ld_iota = DMA("dY", iota128[:, :], iota_in[:, :])
        ld_idf = DMA("dY", identf[:, :], identf_in[:, :])
        oc_set = S.op("V", lambda e: e.memset(onescol[:, :], 1.0))
        zc_set = S.op("V", lambda e: e.memset(zcol[:, :], 0.0))
        zr_set = S.op("V", lambda e: e.memset(zrow[:, :], 0.0))

        b0 = bar_steps[0]
        U = {
            "sct": [b0], "gat": [b0] * 8, "gT": [b0] * 8, "gatb": b0,
            "acc": {}, "hT": [b0], "w2b": [[b0], [b0]], "w1b": [[b0], [b0]],
            "xqrl": b0, "srl": b0, "gsil": b0, "fin": b0,
            "psH": [bar_steps[1], bar_steps[1]], "psY": [bar_steps[1]] * 4,
            "psR": [bar_steps[1], bar_steps[1]],
            "posm_tm": [b0] * 8, "Lrow": b0, "ggath": b0,
        }
        out_stores = []
        half_bar = list(bar_steps)

        def emit_half(half, half_bar):
            hoff = half * HALF
            U["gT"] = [half_bar[0]] * 8
            PA = ExitStack()
            AA5 = lambda name, shape, dt=F32, _h=half: PA.enter_context(
                nc.sbuf_tensor(name + "_h%d" % _h, shape, dt))
            xnqTh = AA5("xnqTh", [128, NDC, HALF], BF16)
            lowin = AA5("lowin", [128, 2, HALF], BF16)
            hT = AA5("hTs", [128, NFC, HALF], BF16)
            xnqrl = [AA5("xnqrl0", [128, D], BF16), AA5("xnqrl1", [128, D], BF16)]
            xlorl = AA5("xlorl", [128, D], BF16)
            gT = AA5("gt", [16, 8, 128], BF16)
            b2ext = AA5("b2ext", [16, D], BF16)
            sb2n = AA5("sb2n", [2, D], BF16)
            sct = AA5("sct", [16, HALF])
            schi = AA5("schi", [16, 128], BF16)
            sclo = AA5("sclo", [16, 128], BF16)
            schiT = AA5("schiT", [128, 16])
            sc32 = AA5("sc32", [128, 16])
            rnk = AA5("rnk", [128, 16])
            mskb = AA5("mskb", [128, 16])
            mskbf = AA5("mskbf", [128, 16], BF16)
            graw = AA5("graw", [128, 16])
            den = AA5("den", [128, 2])
            gbf = AA5("gbf", [128, NEXP], BF16)
            mT = AA5("mT", [16, HALF], BF16)
            scanA = AA5("scanA", [16, HALF])
            scanB = AA5("scanB", [16, HALF])
            posm = AA5("posm", [16, HALF])

            ld_rb2 = DMA("dG", b2ext[:, :], rb2[:, :], deps=half_bar)
            ld_sb2 = DMA("dG", sb2n[:, :], sb2[:, :], deps=half_bar)
            trh = []
            for dc in range(NDC):
                t = DMT("dS", xnqTh[:, dc, :],
                        scr_q[hoff:hoff + HALF, dc * 128:(dc + 1) * 128],
                        deps=q_hi + half_bar)
                trh.append(t)

            rt_steps = []
            lw_last = [half_bar[0], half_bar[0]]
            for dc in range(NDC):
                tl = DMT("dS", lowin[:, dc % 2, :],
                         scr_qlo[hoff:hoff + HALF, dc * 128:(dc + 1) * 128],
                         deps=q_lo + [lw_last[dc % 2]] + half_bar)

                def rtfn(e, dc=dc):
                    for sub in range(2):
                        ps = [psE, psF][sub]
                        e.matmul(ps[0:16, :], centhi[:, dc, :],
                                 xnqTh[:, dc, sub * 512:(sub + 1) * 512],
                                 start=(dc == 0), stop=False)
                        e.matmul(ps[0:16, :], centlo[:, dc, :],
                                 xnqTh[:, dc, sub * 512:(sub + 1) * 512],
                                 start=False, stop=False)
                        ins = e.matmul(ps[0:16, :], centhi[:, dc, :],
                                       lowin[:, dc % 2, sub * 512:(sub + 1) * 512],
                                       start=False, stop=(dc == NDC - 1))
                    return ins
                st = S.op("T", rtfn, deps=[trh[dc], tl, clo, ld_centh] + U["psR"])
                lw_last[dc % 2] = st
                rt_steps.append(st)
            rt_evs = []
            for sub in range(2):
                ps = [psE, psF][sub]
                ev = S.op("S", lambda e, sub=sub, ps=ps: e.activation(
                    sct[:, sub * 512:(sub + 1) * 512], ps[0:16, :], AF.Sigmoid),
                    deps=[rt_steps[-1]] + U["sct"])
                rt_evs.append(ev)
            badd = S.op("V", lambda e: e.tensor_scalar_add(sct[:, :], sct[:, :], cbias_s[:, 0:1]),
                        deps=rt_evs + [ld_cb])
            U["psY"][2] = rt_evs[0]
            U["psY"][3] = rt_evs[1]

            gat_steps = []
            gT_steps = []
            c2 = None
            prev_lmm = None
            uu = {"schi": half_bar[0], "sclo": half_bar[0], "schiT": half_bar[0],
                  "sc32": half_bar[0], "rnk": half_bar[0], "mskb": half_bar[0],
                  "graw": half_bar[0], "den": half_bar[0], "gbf": half_bar[0],
                  "mskbf": half_bar[0], "mT": [half_bar[0]] * 8}
            for tt in range(8):
                c1 = S.op("V", lambda e, tt=tt: e.tensor_copy(schi[:, :], sct[:, tt * 128:(tt + 1) * 128]),
                          deps=[badd, uu["schi"], c2])
                c2 = S.op("V", lambda e, tt=tt: e.tensor_sub(
                    sclo[:, :], sct[:, tt * 128:(tt + 1) * 128], schi[:, :]),
                    deps=[c1, uu["sclo"]])

                def mk1(pap, tst):
                    return S.op("V", lambda e, pap=pap: e.tensor_copy(schiT[:, :], pap),
                                deps=[tst, uu["schiT"]])
                t1, e1 = TRP(schi[:, :], 128, 16, ident[0:16, 0:16], mk1, [c1])

                def mk2(pap, tst):
                    return S.op("V", lambda e, pap=pap: e.tensor_add(sc32[:, :], schiT[:, :], pap),
                                deps=[tst, e1, uu["sc32"]])
                t2, e2 = TRP(sclo[:, :], 128, 16, ident[0:16, 0:16], mk2, [c2])
                uu["schiT"] = e2
                uu["schi"] = t1
                uu["sclo"] = t2
                prev = S.op("V", lambda e: e.tensor_scalar(rnk[:, :], sc32[:, :], sc32[:, 0:1],
                                                           None, ALU.is_lt),
                            deps=[e2, uu["rnk"]])
                for j in range(1, NEXP):
                    prev = S.op("V", lambda e, j=j: e.scalar_tensor_tensor(
                        rnk[:, :], sc32[:, :], sc32[:, j:j + 1], rnk[:, :], ALU.is_lt, ALU.add),
                        deps=[prev])
                mk1s = S.op("V", lambda e: e.tensor_scalar(mskb[:, :], rnk[:, :], 3.5, None, ALU.is_lt),
                            deps=[prev, uu["mskb"]])
                mbf = S.op("V", lambda e: e.tensor_copy(mskbf[:, :], mskb[:, :]),
                           deps=[mk1s, uu["mskbf"]])

                def mkm(pap, tst, tt=tt):
                    return S.op("V", lambda e, pap=pap, tt=tt: e.tensor_copy(
                        mT[:, tt * 128:(tt + 1) * 128], pap), deps=[tst, uu["mT"][tt]])
                tm, em = TRP(mskbf[:, :], 16, 128, ident[:, :], mkm, [mbf])
                uu["mT"][tt] = em
                lmm = S.op("T", lambda e, tt=tt: e.matmul(
                    psB[0:1, 0:16], onescol[:, :], mskbf[:, :],
                    start=(tt == 0), stop=(tt == 7)),
                    deps=[mbf, oc_set, prev_lmm] + ([U["psH"][1]] if tt == 0 else []))
                prev_lmm = lmm
                uu["mskbf"] = lmm
                gr = S.op("V", lambda e: e.tensor_mul(graw[:, :], sc32[:, :], mskb[:, :]),
                          deps=[mk1s, uu["graw"], mbf])
                uu["sc32"] = gr
                uu["rnk"] = mk1s
                ds = S.op("V", lambda e: e.reduce_sum(den[:, 0:1], graw[:, :], axis=AX.X),
                          deps=[gr, uu["den"]])
                dm = S.op("V", lambda e: e.tensor_scalar_max(den[:, 1:2], den[:, 0:1], 1e-8),
                          deps=[ds])
                dr = S.op("V", lambda e: e.reciprocal(den[:, 0:1], den[:, 1:2]), deps=[dm])
                uu["den"] = dr
                ga = S.op("V", lambda e, tt=tt: e.tensor_scalar(
                    gating[:, tt, :], graw[:, :], den[:, 0:1], None, ALU.mult),
                    deps=[dr, U["gat"][tt]])
                uu["mskb"] = mbf if False else ga
                uu["graw"] = ga
                gb = S.op("V", lambda e, tt=tt: e.tensor_copy(gbf[:, :], gating[:, tt, :]),
                          deps=[ga, uu["gbf"]])
                gab = S.op("V", lambda e, tt=tt: e.tensor_copy(gatb[:, tt, :], gating[:, tt, :]),
                           deps=[ga, U["gatb"]])

                def mk3(pap, tst, tt=tt):
                    return S.op("V", lambda e, pap=pap: e.tensor_copy(gT[:, tt, :], pap),
                                deps=[tst, U["gT"][tt]])
                t3, e3 = TRP(gbf[:, :], 16, 128, ident[:, :], mk3, [gb])
                uu["gbf"] = t3
                gat_steps.append(ga)
                gT_steps.append(e3)
                U["gat"][tt] = ga
                U["gatb"] = gab
                U["gT"][tt] = e3

            U["sct"] = [c2]
            lev = S.op("V", lambda e: e.tensor_copy(Lrow[:, :], psB[0:1, 0:16]),
                       deps=[prev_lmm, U["Lrow"]])
            U["Lrow"] = lev
            U["psH"][1] = lev
            cpA = S.op("V", lambda e: e.tensor_copy(scanA[:, :], mT[:, :]),
                       deps=list(uu["mT"]) + [half_bar[0]])
            cur, oth = scanA, scanB
            prevscan = cpA
            for k in (1, 2, 4, 8, 16, 32, 64, 128, 256, 512):
                s1 = S.op("V", lambda e, k=k, cur=cur, oth=oth: e.tensor_add(
                    oth[:, k:HALF], cur[:, k:HALF], cur[:, 0:HALF - k]), deps=[prevscan])
                s2 = S.op("V", lambda e, k=k, cur=cur, oth=oth: e.tensor_copy(
                    oth[:, 0:k], cur[:, 0:k]), deps=[prevscan, s1])
                prevscan = s2
                cur, oth = oth, cur
            pm1 = S.op("V", lambda e, cur=cur: e.tensor_mul(posm[:, :], cur[:, :], mT[:, :]),
                       deps=[prevscan])
            pm2 = S.op("V", lambda e: e.tensor_scalar_add(posm[:, :], posm[:, :], -1.0),
                       deps=[pm1])
            ptm_steps = []
            prev_ptm_ev = [lev, lev]
            for tt in range(8):
                bk = [psA, psB][tt % 2]
                tpt = S.op("T", lambda e, tt=tt, bk=bk: e.transpose(
                    bk[:, 0:16], posm[0:16, tt * 128:(tt + 1) * 128], identf[0:16, 0:16]),
                    deps=[pm2, ld_idf, prev_ptm_ev[tt % 2], U["psH"][0], prev_lmm])
                ept = S.op("V", lambda e, tt=tt, bk=bk: e.tensor_copy(
                    posm_tm[:, tt, :], bk[:, 0:16]), deps=[tpt, U["posm_tm"][tt]])
                prev_ptm_ev[tt % 2] = ept
                ptm_steps.append(ept)
                U["posm_tm"][tt] = ept
            U["psH"] = [prev_ptm_ev[0], prev_ptm_ev[1]]

            if dbg:
                dgat = DMA("dY", dbg_gat[:, half * 8:(half + 1) * 8, :], gating[:, :, :],
                           deps=gat_steps)
                for tt in range(8):
                    U["gat"][tt] = dgat

            def expert_shared(jconv, b1col, mode):
                h_evs = []
                for fs in range(NFC):
                    wb = w1b[fs % 2]
                    w1sl = scr_w1[jconv, fs].rearrange("p (dc f) -> p dc f", f=128)
                    q = "dG" if fs % 2 == 0 else "dY"
                    dma1 = DMA(q, wb[:, :, :], w1sl,
                               deps=U["w1b"][fs % 2] + [conv_w1_done[(jconv, fs)]])
                    musers = []
                    for sub in range(2):
                        g = (fs * 2 + sub) % 2
                        ps = [psA, psB][g]

                        def hfn(e, wb=wb, sub=sub, ps=ps):
                            for dc in range(NDC):
                                ins = e.matmul(ps[:, :], wb[:, dc, :],
                                               xnqTh[:, dc, sub * 512:(sub + 1) * 512],
                                               start=(dc == 0), stop=(dc == NDC - 1))
                            return ins
                        mmst = S.op("T", hfn,
                                    deps=[dma1, U["psH"][g]] + trh + U["hT"])
                        ev = S.op("S", lambda e, fs=fs, sub=sub, ps=ps, b1col=b1col: e.activation(
                            hT[:, fs, sub * 512:(sub + 1) * 512], ps[:, :], AF.Relu,
                            bias=sb1s[:, fs, b1col:b1col + 1]),
                            deps=[mmst, ld_sb1])
                        U["psH"][g] = ev
                        h_evs.append(ev)
                        musers.append(mmst)
                    U["w1b"][fs % 2] = musers
                y_last_mm = None
                for qd in range(4):
                    qb = qd % 2
                    wd1 = DMA("dG", w2b[:, :, qb, 0:256],
                              scr_w2[jconv, 2 * qd].rearrange("p (fc x) -> p fc x", x=256),
                              deps=U["w2b"][qb] + [conv_w2_done[(jconv, 2 * qd)]])
                    wd2 = DMA("dY", w2b[:, :, qb, 256:512],
                              scr_w2[jconv, 2 * qd + 1].rearrange("p (fc x) -> p fc x", x=256),
                              deps=U["w2b"][qb] + [conv_w2_done[(jconv, 2 * qd + 1)]])
                    yusers = []
                    for tt in range(8):
                        bk = (qd * 8 + tt) % 4
                        ps = psY[bk]
                        extras = (mode == "init")
                        if extras:
                            rl1 = DMA("dY", xnqrl[tt % 2][:, :],
                                      scr_q[hoff + tt * 128:hoff + (tt + 1) * 128, :],
                                      deps=[U.get("xnqrl_%d" % (tt % 2), half_bar[0])] + q_hi)
                            rl2 = DMA("dY", xlorl[:, :],
                                      scr_qlo[hoff + tt * 128:hoff + (tt + 1) * 128, :],
                                      deps=[U.get("xlorl", half_bar[0])] + q_lo)

                        def yfn(e, qd=qd, qb=qb, tt=tt, ps=ps, extras=extras):
                            for fs in range(NFC):
                                ins = e.matmul(ps[:, :], hT[:, fs, tt * 128:(tt + 1) * 128],
                                               w2b[:, fs, qb, :],
                                               start=(fs == 0), stop=(fs == NFC - 1 and not extras))
                            if extras:
                                e.matmul(ps[:, :], gT[:, tt, :], b2ext[:, qd * 512:(qd + 1) * 512],
                                         start=False, stop=False)
                                e.matmul(ps[:, :], ones2[:, :], sb2n[:, qd * 512:(qd + 1) * 512],
                                         start=False, stop=False)
                                e.matmul(ps[:, :], ident[:, :], xnqrl[tt % 2][:, qd * 512:(qd + 1) * 512],
                                         start=False, stop=False)
                                ins = e.matmul(ps[:, :], ident[:, :], xlorl[:, qd * 512:(qd + 1) * 512],
                                               start=False, stop=True)
                            return ins
                        deps = h_evs + [wd1, wd2, U["psY"][bk]]
                        if extras:
                            deps += [gT_steps[tt], ld_rb2, ld_sb2, ones2_set, rl1, rl2, ld_ident]
                        mmst = S.op("T", yfn, deps=deps)
                        if extras:
                            U["xnqrl_%d" % (tt % 2)] = mmst
                            U["xlorl"] = mmst
                        accdep = U["acc"].get((tt, qd))
                        if mode == "init":
                            ev = S.op("V", lambda e, tt=tt, qd=qd, ps=ps: e.tensor_copy(
                                acc[:, tt, qd * 512:(qd + 1) * 512], ps[:, :]),
                                deps=[mmst, accdep])
                        else:
                            ev = S.op("V", lambda e, tt=tt, qd=qd, ps=ps: e.scalar_tensor_tensor(
                                acc[:, tt, qd * 512:(qd + 1) * 512], ps[:, :], 1.0,
                                acc[:, tt, qd * 512:(qd + 1) * 512], ALU.mult, ALU.add),
                                deps=[mmst, accdep])
                        U["acc"][(tt, qd)] = ev
                        U["psY"][bk] = ev
                        yusers.append(mmst)
                        y_last_mm = mmst
                    U["w2b"][qb] = yusers
                U["hT"] = [y_last_mm]
                return y_last_mm

            init_last = expert_shared(16, 0, "init")
            for tt in range(8):
                U["gT"][tt] = init_last
            shared_last = expert_shared(17, 1, "add")

            snapA = S.snapshot()
            barA = [S.op(e, lambda eng: eng.nop(), deps=snapA) for e in ("T", "V", "S", "Y")]
            PA.close()

            PB = ExitStack()
            AB5 = lambda name, shape, dt=F32, _h=half: PB.enter_context(
                nc.sbuf_tensor(name + "_h%d" % _h, shape, dt))
            xnq_tm = AB5("xnq_tm", [128, 8, D], BF16)
            hTg = AB5("hTg", [128, NFC, HALF], BF16)
            STall = AB5("STall", [128, 8, HALF], BF16)
            PBi = ExitStack()
            ABi = lambda name, shape, dt=F32, _h=half: PBi.enter_context(
                nc.sbuf_tensor(name + "_h%d" % _h, shape, dt))
            flatG = ABi("flatG", [128, NDC, 512], BF16)
            S_sb = ABi("S_sb", [128, 8, 512], BF16)
            Y_all = ABi("Y_all", [128, 8, 512], BF16)

            xtm_loads = []
            for tt in range(8):
                lx = DMA("dY", xnq_tm[:, tt, :],
                         scr_q[hoff + tt * 128:hoff + (tt + 1) * 128, :],
                         deps=q_hi + barA)
                xtm_loads.append(lx)

            ub = {"S_sb": [barA[0]], "ST": [barA[0]] * 8, "flatG": [barA[0]] * NDC,
                  "Y": [barA[0]] * 8, "hTg": [barA[0]], "ggath": [barA[0]] * 8,
                  "psG": [barA[0], barA[0]]}
            U["psH"] = [barA[0], barA[0]]

            def expert_routed(j):
                gather_evs_holder = [None, None]
                conds = {}
                lap = Lrow[0:1, j:j + 1].bitcast(mybir.dt.int32)
                for st in range(1, 8):
                    conds[st] = (lap, fbits(128 * st))
                conds["sb1"] = (lap, fbits(512))

                def C(st):
                    if st == 0 or st is None:
                        return None
                    return conds[st]

                s_steps = [None] * 8
                st_tr = [None] * 8
                gg_steps = [None] * 8
                prev_gather_mms = ub["S_sb"]
                for sb in range(2):
                    smem = S.op("V", lambda e: e.memset(S_sb[:, :, :], 0.0),
                                deps=prev_gather_mms)
                    for st in range(4 * sb, 4 * sb + 4):
                        cond = C(st)
                        bdeps = [pm2, U["Lrow"], smem] + ptm_steps
                        bs = []
                        for tt in range(8):
                            bs.append(S.op("V", lambda e, st=st, tt=tt: e.tensor_scalar(
                                S_sb[:, tt, (st % 4) * 128:(st % 4 + 1) * 128],
                                iota128[:, :], float(128 * st), posm_tm[:, tt, j:j + 1],
                                ALU.add, ALU.is_equal),
                                deps=bdeps + [ld_iota], cond=cond))
                        s_steps[st] = bs
                        evs = []
                        for tt in range(8):
                            def mkst(pap, tst, st=st, tt=tt):
                                return S.op("V", lambda e, pap=pap, st=st, tt=tt: e.tensor_copy(
                                    STall[:, st, tt * 128:(tt + 1) * 128], pap),
                                    deps=[tst, ub["ST"][st]], cond=C(st))
                            tst, ev = TRP(S_sb[:, tt, (st % 4) * 128:(st % 4 + 1) * 128],
                                          128, 128, ident[:, :], mkst, [bs[tt]], cond=cond)
                            evs.append(ev)
                        st_tr[st] = evs
                        gps = [psA, psB][st % 2]

                        def ggfn(e, st=st, gps=gps):
                            for tt in range(8):
                                ins = e.matmul(gps[:, 0:1],
                                               S_sb[:, tt, (st % 4) * 128:(st % 4 + 1) * 128],
                                               gatb[:, tt, j:j + 1], start=(tt == 0), stop=(tt == 7))
                            return ins
                        ggm = S.op("T", ggfn, deps=bs + [U["gatb"], U["psH"][st % 2]], cond=cond)
                        gge = S.op("S", lambda e, st=st, gps=gps: e.activation(
                            ggath[:, st:st + 1], gps[:, 0:1], AF.Identity),
                            deps=[ggm, ub["ggath"][st]], cond=cond)
                        U["psH"][st % 2] = gge
                        gg_steps[st] = gge
                        ub["ggath"][st] = gge
                    cond = conds["sb1"] if sb else None
                    sdeps = [s for st in range(4 * sb, 4 * sb + 4) for s in s_steps[st]] + [smem]
                    evs = []
                    gmms = []
                    for dc in range(NDC):
                        g = dc % 2
                        gps = [psA, psB][g]

                        def gfn(e, dc=dc, gps=gps):
                            for tt in range(8):
                                ins = e.matmul(gps[:, :], xnq_tm[:, tt, dc * 128:(dc + 1) * 128],
                                               S_sb[:, tt, :], start=(tt == 0), stop=(tt == 7))
                            return ins
                        gmm = S.op("T", gfn, deps=sdeps + xtm_loads + [U["psH"][g]], cond=cond)
                        gev = S.op("V", lambda e, dc=dc, gps=gps: e.tensor_copy(
                            flatG[:, dc, :], gps[:, :]), deps=[gmm, ub["flatG"][dc]], cond=cond)
                        U["psH"][g] = gev
                        ub["flatG"][dc] = gev
                        evs.append(gev)
                        gmms.append(gmm)
                    prev_gather_mms = gmms
                    for fs in range(NFC):
                        g = fs % 2
                        ps = [psA, psB][g]
                        wb = w1b[fs % 2]
                        w1sl = scr_w1[j, fs].rearrange("p (dc f) -> p dc f", f=128)
                        q = "dG" if fs % 2 == 0 else "dY"
                        dma1 = DMA(q, wb[:, :, :], w1sl,
                                   deps=U["w1b"][fs % 2] + [conv_w1_done[(j, fs)]])

                        def hfn(e, wb=wb, ps=ps):
                            for dc in range(NDC):
                                ins = e.matmul(ps[:, :], wb[:, dc, :],
                                               flatG[:, dc, :], start=(dc == 0), stop=(dc == NDC - 1))
                            return ins
                        mmst = S.op("T", hfn,
                                    deps=evs + [dma1, U["psH"][g]] + ub["hTg"], cond=cond)
                        ev = S.op("S", lambda e, fs=fs, sb=sb, ps=ps: e.activation(
                            hTg[:, fs, sb * 512:(sb + 1) * 512], ps[:, :], AF.Relu,
                            bias=rb1s[:, fs, j:j + 1]),
                            deps=[mmst, ld_rb1], cond=cond)
                        U["psH"][g] = ev
                        U["w1b"][fs % 2] = [mmst, dma1]
                ub["S_sb"] = prev_gather_mms
                h_done = [U["psH"][0], U["psH"][1]]

                y_last = None
                for qd in range(4):
                    qb = qd % 2
                    wd1 = DMA("dG", w2b[:, :, qb, 0:256],
                              scr_w2[j, 2 * qd].rearrange("p (fc x) -> p fc x", x=256),
                              deps=U["w2b"][qb] + [conv_w2_done[(j, 2 * qd)]])
                    wd2 = DMA("dY", w2b[:, :, qb, 256:512],
                              scr_w2[j, 2 * qd + 1].rearrange("p (fc x) -> p fc x", x=256),
                              deps=U["w2b"][qb] + [conv_w2_done[(j, 2 * qd + 1)]])
                    yusers = []
                    for st in range(8):
                        cond = C(st)
                        bk = st % 2
                        ps = [psC, psD][bk]

                        def yfn(e, st=st, qb=qb, ps=ps):
                            for fs in range(NFC):
                                ins = e.matmul(ps[:, :], hTg[:, fs, st * 128:(st + 1) * 128],
                                               w2b[:, fs, qb, :],
                                               start=(fs == 0), stop=(fs == NFC - 1))
                            return ins
                        mmst = S.op("T", yfn, deps=h_done + [wd1, wd2, U["psY"][bk]], cond=cond)
                        yev = S.op("S", lambda e, st=st, ps=ps: e.activation(
                            Y_all[:, st, :], ps[:, :], AF.Identity, scale=ggath[:, st:st + 1]),
                            deps=[mmst, gg_steps[st], ub["Y"][st]], cond=cond)
                        U["psY"][bk] = yev
                        ub["Y"][st] = yev
                        yusers.append(mmst)
                        y_last = mmst
                    yevs = [ub["Y"][st] for st in range(8)]
                    for tt in range(8):
                        bk2 = 2 + (tt % 2)
                        ps = psY[bk2]

                        def sc0(e, tt=tt, ps=ps):
                            return e.matmul(ps[:, :], STall[:, 0, tt * 128:(tt + 1) * 128],
                                            Y_all[:, 0, :], start=True, stop=False)
                        m0 = S.op("T", sc0, deps=yevs + [st_tr[0][tt], U["psY"][bk2]])
                        for st in range(1, 8):
                            m0 = S.op("T", lambda e, st=st, tt=tt, ps=ps: e.matmul(
                                ps[:, :], STall[:, st, tt * 128:(tt + 1) * 128],
                                Y_all[:, st, :], start=False, stop=False),
                                deps=[m0], cond=C(st))
                        mcl = S.op("T", lambda e, ps=ps: e.matmul(
                            ps[:, 0:1], zcol[:, :], zrow[:, :], start=False, stop=True),
                            deps=[m0, zc_set, zr_set])
                        accdep = U["acc"].get((tt, qd))
                        ev = S.op("V", lambda e, tt=tt, qd=qd, ps=ps: e.scalar_tensor_tensor(
                            acc[:, tt, qd * 512:(qd + 1) * 512], ps[:, :], 1.0,
                            acc[:, tt, qd * 512:(qd + 1) * 512], ALU.mult, ALU.add),
                            deps=[mcl, accdep])
                        U["acc"][(tt, qd)] = ev
                        U["psY"][bk2] = ev
                    U["w2b"][qb] = yusers
                ub["hTg"] = [y_last]
                ub["S_sb"] = [y_last]

            for j in range(NEXP):
                expert_routed(j)
            U["psR"] = [U["psY"][2], U["psY"][3]]

            if dbg:
                dacc = DMA("dY", dbg_acc[:, half * 8:(half + 1) * 8, :], acc[:, :, :],
                           deps=[U["acc"][(tt, qd)] for tt in range(8) for qd in range(4)])
                dbg_acc_dep = [dacc]
            else:
                dbg_acc_dep = []

            snapB = S.snapshot()
            barB = [S.op(e, lambda eng: eng.nop(), deps=snapB) for e in ("T", "V", "S", "Y")]
            PBi.close()
            AF5 = lambda name, shape, dt=F32, _h=half: PB.enter_context(
                nc.sbuf_tensor(name + "f_h%d" % _h, shape, dt))
            xqrl = AF5("xqrl", [128, D])
            srl = AF5("srl", [128, D], BF16)
            gsil = AF5("gsil", [128, D], BF16)
            fin = AF5("fin", [128, D])
            U["xqrl"] = barB[0]
            U["srl"] = barB[0]
            U["gsil"] = barB[0]
            U["fin"] = barB[0]

            for tt in range(8):
                acc_done = [U["acc"][(tt, qd)] for qd in range(4)] + dbg_acc_dep
                sil = S.op("S", lambda e, tt=tt: e.activation(gsil[:, :], acc[:, tt, :], AF.Silu),
                           deps=acc_done + [U["gsil"]] + barB)
                srload = DMA("dY", srl[:, :], s_scr[hoff + tt * 128:hoff + (tt + 1) * 128, :],
                             deps=[s_store_list[half * 8 + tt], U["srl"]] + barB)
                xqload = DMA("dY", xqrl[:, :], xq[hoff + tt * 128:hoff + (tt + 1) * 128, :],
                             deps=[U["xqrl"]] + barB)
                mu = S.op("V", lambda e: e.tensor_mul(fin[:, :], srl[:, :], gsil[:, :]),
                          deps=[sil, srload, U["fin"]])
                ad = S.op("V", lambda e: e.tensor_add(xqrl[:, :], fin[:, :], xqrl[:, :]),
                          deps=[mu, xqload])
                ost = DMA("dY", out[hoff + tt * 128:hoff + (tt + 1) * 128, :], xqrl[:, :],
                          deps=[ad])
                out_stores.append(ost)
                U["gsil"] = mu
                U["fin"] = ad
                U["srl"] = mu
                U["xqrl"] = ost
                for qd in range(4):
                    U["acc"][(tt, qd)] = sil

            snapH = S.snapshot()
            half_bar = [S.op(e, lambda eng: eng.nop(), deps=snapH) for e in ("T", "V", "S", "Y")]
            PB.close()
            return half_bar

        for half in range(2):
            half_bar = emit_half(half, half_bar)

        S.op("Y", lambda e: e.nop(), deps=out_stores)

        S.emit(nc, sems)
        P5.close()
    return nc


_PROG_CACHE = {}


def _get_prog(dbg=0, serialize=False):
    key = (dbg, serialize)
    if key not in _PROG_CACHE:
        _PROG_CACHE[key] = build_program(dbg=dbg, serialize=serialize)
    return _PROG_CACHE[key]


def make_in_maps(inputs, dbg=0):
    f32 = np.float32
    g = lambda k: np.ascontiguousarray(np.asarray(inputs[k], dtype=f32))
    x = g("L0_attn_x")
    WQ = g("L0_attn_WQ"); bQ = g("L0_attn_bQ")
    WK = g("L0_attn_WK"); bK = g("L0_attn_bK")
    WV = g("L0_attn_WV"); bV = g("L0_attn_bV")
    PJ = g("L0_attn_proj")
    sw1 = g("L0_ffn_shared_w1"); sb1 = g("L0_ffn_shared_b1")
    sw2 = g("L0_ffn_shared_w2"); sb2 = g("L0_ffn_shared_b2")
    rw1 = g("L0_ffn_routed_w1"); rb1 = g("L0_ffn_routed_b1")
    rw2 = g("L0_ffn_routed_w2"); rb2 = g("L0_ffn_routed_b2")
    cent = g("L0_ffn_centroids"); cb = g("L0_ffn_bias")

    common = {
        "ident": np.eye(128, dtype=BF),
        "iota_in": np.broadcast_to(np.arange(128, dtype=np.float32), (128, 128)).copy(),
        "identf_in": np.eye(128, dtype=np.float32),
        "wq": WQ,
        "bq_r": np.ascontiguousarray(bQ.reshape(16, 128).T),
        "wk": WK,
        "bk_r": np.ascontiguousarray(bK.reshape(4, 128).T),
        "wv": WV,
        "bv": bV.reshape(1, DK),
        "proj": PJ,
        "sw1": sw1,
        "sb1_r": np.ascontiguousarray(sb1.reshape(2, NFC, 128).transpose(2, 1, 0)),
        "sw2": sw2,
        "sb2": sb2,
        "rw1": rw1,
        "rb1_r": np.ascontiguousarray(rb1.reshape(NEXP, NFC, 128).transpose(2, 1, 0)),
        "rw2": rw2,
        "rb2": rb2,
        "cent_r": np.ascontiguousarray(cent.T),
        "cbias": cb.reshape(NEXP, 1),
    }
    masks = [build_masks(0), build_masks(1)]
    in_maps = []
    for core in range(8):
        b, p = core // 2, core % 2
        J = own_tile_ids(p)
        xq_host = np.ascontiguousarray(
            np.concatenate([x[b, 128 * j:128 * (j + 1), :] for j in J], axis=0))
        m = dict(common)
        m["xkv"] = x[b]
        m["xq"] = xq_host
        m["masks"] = masks[p]
        in_maps.append(m)
    return in_maps


def assemble(results):
    y = np.empty((4, TKV, D), dtype=np.float32)
    for core in range(8):
        b, p = core // 2, core % 2
        J = own_tile_ids(p)
        o = results[core]["out"]
        for i, j in enumerate(J):
            y[b, 128 * j:128 * (j + 1), :] = o[128 * i:128 * (i + 1), :]
    return y


def kernel(**inputs):
    nc = _get_prog(dbg=0)
    in_maps = make_in_maps(inputs)
    res = run_bass_kernel_spmd(nc, in_maps, core_ids=list(range(8)))
    return assemble(res.results)


# version 09
# speedup vs baseline: 1.7292x; 1.7292x over previous
import numpy as np
import ml_dtypes
from contextlib import ExitStack

import concourse.bass as bass
import concourse.mybir as mybir
from concourse.bass_utils import run_bass_kernel_spmd

F32 = mybir.dt.float32
BF16 = mybir.dt.bfloat16
AF = mybir.ActivationFunctionType
ALU = mybir.AluOpType
AX = mybir.AxisListType

BF = ml_dtypes.bfloat16
NEG = np.float32(-1e30)

D = 2048
DK = 512
TQ = 2048
TKV = 4096
DFF = 1408
NFC = 11
NDC = 16
NEXP = 16
HALF = 1024
RSQ128 = float(1.0 / np.sqrt(128.0))


def own_tile_ids(parity):
    lo = [2 * i + parity for i in range(8)]
    hi = [31 - 2 * i - parity for i in range(8)]
    return sorted(lo + hi)


def build_masks(parity):
    J = own_tile_ids(parity)
    m = np.zeros((128, 16, 256), dtype=np.float32)
    for i in range(16):
        q = 128 * J[i] + np.arange(128)[:, None]
        k = 256 * i + np.arange(256)[None, :]
        m[:, i, :] = np.where(k <= q, 0.0, NEG)
    return m.astype(BF)


class Step:
    __slots__ = ("idx", "ch", "fn", "deps", "amt", "val", "key")

    def __init__(self, idx, ch, fn, deps, amt, key):
        self.idx, self.ch, self.fn, self.deps, self.amt = idx, ch, fn, deps, amt
        self.val = None
        self.key = key


class Sched:
    ENGS = ("T", "V", "S", "G", "Y")
    CHANNELS = ("T", "V", "S", "G", "Y", "dY", "dS", "dG", "dGc")
    QUEUE_OF = {"dY": "Y", "dS": "S", "dG": "G", "dGc": "G"}
    NOBAR = ("dGc",)
    NROT = 16

    def __init__(self, serialize=False):
        self.steps = []
        self.last = {}
        self.rot = {c: 0 for c in self.CHANNELS}
        self.serialize = serialize

    def _add(self, ch, fn, deps, amt):
        deps = [d for d in deps if d is not None]
        if self.serialize and self.steps:
            deps = deps + [self.steps[-1]]
        if amt == 16:
            key = (ch, self.rot[ch] % self.NROT)
            self.rot[ch] += 1
        else:
            key = (ch, 0)
        st = Step(len(self.steps), ch, fn, deps, amt, key)
        self.steps.append(st)
        self.last[key] = st
        return st

    def op(self, eng, fn, deps=()):
        return self._add(eng, fn, list(deps), 1)

    def dma(self, queue, fn, deps=()):
        return self._add(queue, fn, list(deps), 16)

    def snapshot(self):
        return [s for k, s in self.last.items() if k[0] not in self.NOBAR]

    def emit(self, nc, sems):
        referenced = set()
        for s in self.steps:
            for d in s.deps:
                referenced.add(d.idx)
        counts = {}
        for s in self.steps:
            if s.amt == 16 or s.idx in referenced:
                counts[s.key] = counts.get(s.key, 0) + s.amt
                s.val = counts[s.key]
        owner = {c: self.QUEUE_OF.get(c, c) for c in self.CHANNELS}

        with nc.Block() as blk:
            def emit_for(name, eng):
                water = {}
                for s in self.steps:
                    if owner[s.ch] != name:
                        continue
                    for d in s.deps:
                        if s.ch == "T" and d.ch == "T":
                            continue
                        if water.get(d.key, 0) < d.val:
                            eng.wait_ge(sems[d.key], d.val)
                            water[d.key] = d.val
                    if s.amt == 16 and s.val > 16 and water.get(s.key, 0) < s.val - 16:
                        eng.wait_ge(sems[s.key], s.val - 16)
                        water[s.key] = s.val - 16
                    instr = s.fn(eng)
                    if s.val is not None:
                        instr.then_inc(sems[s.key], s.amt)

            @blk.tensor
            def _(t):
                emit_for("T", t)

            @blk.vector
            def _(v):
                emit_for("V", v)

            @blk.scalar
            def _(s):
                emit_for("S", s)

            @blk.gpsimd
            def _(g):
                emit_for("G", g)

            @blk.sync
            def _(y):
                emit_for("Y", y)


def build_program(dbg=0, serialize=False):
    nc = bass.Bass("TRN2", target_bir_lowering=False)
    S = Sched(serialize=serialize)

    ap = lambda t: t.ap()
    xkv = ap(nc.dram_tensor("xkv", [TKV, D], F32, kind="ExternalInput"))
    xq = ap(nc.dram_tensor("xq", [TQ, D], F32, kind="ExternalInput"))
    masks_in = ap(nc.dram_tensor("masks", [128, 16, 256], BF16, kind="ExternalInput"))
    ident_in = ap(nc.dram_tensor("ident", [128, 128], BF16, kind="ExternalInput"))
    wq = ap(nc.dram_tensor("wq", [D, D], F32, kind="ExternalInput"))
    bq_r = ap(nc.dram_tensor("bq_r", [128, 16], F32, kind="ExternalInput"))
    wk = ap(nc.dram_tensor("wk", [DK, D], F32, kind="ExternalInput"))
    bk_r = ap(nc.dram_tensor("bk_r", [128, 4], F32, kind="ExternalInput"))
    wv = ap(nc.dram_tensor("wv", [DK, D], F32, kind="ExternalInput"))
    bv_in = ap(nc.dram_tensor("bv", [1, DK], F32, kind="ExternalInput"))
    proj = ap(nc.dram_tensor("proj", [D, D], F32, kind="ExternalInput"))
    sw1 = ap(nc.dram_tensor("sw1", [2, D, DFF], F32, kind="ExternalInput"))
    sb1_r = ap(nc.dram_tensor("sb1_r", [128, NFC, 2], F32, kind="ExternalInput"))
    sw2 = ap(nc.dram_tensor("sw2", [2, DFF, D], F32, kind="ExternalInput"))
    sb2 = ap(nc.dram_tensor("sb2", [2, D], F32, kind="ExternalInput"))
    rw1 = ap(nc.dram_tensor("rw1", [NEXP, D, DFF], F32, kind="ExternalInput"))
    rb1_r = ap(nc.dram_tensor("rb1_r", [128, NFC, NEXP], F32, kind="ExternalInput"))
    rw2 = ap(nc.dram_tensor("rw2", [NEXP, DFF, D], F32, kind="ExternalInput"))
    rb2 = ap(nc.dram_tensor("rb2", [NEXP, D], F32, kind="ExternalInput"))
    cent_r = ap(nc.dram_tensor("cent_r", [D, NEXP], F32, kind="ExternalInput"))
    cbias = ap(nc.dram_tensor("cbias", [NEXP, 1], F32, kind="ExternalInput"))

    dscr = "ExternalOutput" if dbg else "Internal"
    scr_kv = ap(nc.dram_tensor("scr_kv", [TKV, D], BF16, kind=dscr))
    scr_q = ap(nc.dram_tensor("scr_q", [TQ, D], BF16, kind=dscr))
    scr_qlo = ap(nc.dram_tensor("scr_qlo", [TQ, D], BF16, kind="Internal"))
    s_scr = ap(nc.dram_tensor("s_scr", [TQ, D], BF16, kind=dscr))
    scr_w1 = ap(nc.dram_tensor("scr_w1", [18, NFC, 128, NDC * 128], BF16, kind="Internal"))
    scr_w2 = ap(nc.dram_tensor("scr_w2", [18, 8, 128, NFC * 256], BF16, kind="Internal"))
    out = ap(nc.dram_tensor("out", [TQ, D], F32, kind="ExternalOutput"))
    if dbg:
        dbg_kt = ap(nc.dram_tensor("dbg_kt", [128, 4, TKV], BF16, kind="ExternalOutput"))
        dbg_v = ap(nc.dram_tensor("dbg_v", [128, 32, DK], BF16, kind="ExternalOutput"))
        dbg_qs = ap(nc.dram_tensor("dbg_qs", [128, 4, TQ], BF16, kind="ExternalOutput"))
        dbg_gat = ap(nc.dram_tensor("dbg_gat", [128, 16, NEXP], F32, kind="ExternalOutput"))
        dbg_acc = ap(nc.dram_tensor("dbg_acc", [128, 16, D], F32, kind="ExternalOutput"))

    def DMA(q, dst, src, deps=()):
        return S.dma(q, lambda e, a=dst, b=src: e.dma_start(a, b), deps)

    def DMT(q, dst, src, deps=()):
        return S.dma(q, lambda e, a=dst, b=src: e.dma_start_transpose(a, b), deps)

    with ExitStack() as CTX:
        sems = {}
        for c in Sched.CHANNELS:
            nrot = Sched.NROT if c.startswith("d") else 1
            for i in range(nrot):
                sems[(c, i)] = CTX.enter_context(nc.semaphore("ch_%s_%d" % (c, i)))
        A = lambda name, shape, dt=F32: CTX.enter_context(nc.sbuf_tensor(name, shape, dt))

        ident = A("ident_sb", [128, 128], BF16)
        bqs = A("bqs_sb", [128, 4])
        bks = A("bks_sb", [128, 4])
        bvrow = A("bvrow_sb", [1, DK], BF16)
        ones1 = A("ones1_sb", [1, 128], BF16)
        sb1s = A("sb1_sb", [128, NFC, 2])
        rb1s = A("rb1_sb", [128, NFC, NEXP])
        b2ext = A("b2ext_sb", [16, D], BF16)
        sb2n = A("sb2n_sb", [2, D], BF16)
        ones2 = A("ones2_sb", [2, 128], BF16)
        centhi = A("centhi_sb", [128, NDC, NEXP], BF16)
        centlo = A("centlo_sb", [128, NDC, NEXP], BF16)
        cbias_s = A("cbias_sb", [16, 1])

        ATT = ExitStack()
        AA = lambda name, shape, dt=F32: ATT.enter_context(nc.sbuf_tensor(name, shape, dt))
        Pfold = AA("pfold_sb", [128, 4, D], BF16)
        WKT = AA("wkt_sb", [128, NDC, DK], BF16)
        WVT = AA("wvt_sb", [128, NDC, DK], BF16)
        QsumT = AA("qsumt_sb", [128, 4, TQ], BF16)
        KT = AA("kt_sb", [128, 4, TKV], BF16)
        Vsb = AA("v_sb", [128, 32, DK], BF16)

        psA = CTX.enter_context(nc.psum_tensor("psA", [128, 512], F32))
        psB = CTX.enter_context(nc.psum_tensor("psB", [128, 512], F32))
        psC = CTX.enter_context(nc.psum_tensor("psC", [128, 512], F32))
        psD = CTX.enter_context(nc.psum_tensor("psD", [128, 512], F32))
        psE = CTX.enter_context(nc.psum_tensor("psE", [128, 512], F32))
        psF = CTX.enter_context(nc.psum_tensor("psF", [128, 512], F32))
        ptr1 = CTX.enter_context(nc.psum_tensor("ptr1", [128, 1024], BF16))
        ptr2 = CTX.enter_context(nc.psum_tensor("ptr2", [128, 1024], BF16))
        psY = [psC, psD, psE, psF]

        TP = {"k": 0, "ev": [None, None]}

        def TRP(src_ap, outP, outF, id_ap, evac_maker, deps):
            k = TP["k"]
            TP["k"] += 1
            bank = [ptr1, ptr2][k % 2]
            pap = bank[0:outP, 0:outF]
            tst = S.op("T", lambda e, pap=pap, s=src_ap, ia=id_ap: e.transpose(pap, s, ia),
                       deps=list(deps) + [TP["ev"][k % 2]])
            ev = evac_maker(pap, tst)
            TP["ev"][k % 2] = ev
            return tst, ev

        P1wq = ExitStack()
        WQT = P1wq.enter_context(nc.sbuf_tensor("wqt_sb", [128, NDC, DK], BF16))
        P1a = ExitStack()
        A1 = lambda name, shape, dt=F32: P1a.enter_context(nc.sbuf_tensor(name, shape, dt))
        wstg = A1("wstg", [128, 2, D])
        wstg2 = A1("wstg2", [128, 2, D])
        wbfh = A1("wbfh", [128, 2, D], BF16)
        bqr_s = A1("bqr_s", [128, 16])
        centf = A1("centf", [128, NDC, NEXP])
        wbfh2 = A1("wbfh2", [128, 2, D], BF16)

        ld_ident = DMA("dY", ident[:, :], ident_in[:, :])
        ld_bqr = DMA("dY", bqr_s[:, :], bq_r[:, :])
        ld_bk = DMA("dY", bks[:, :], bk_r[:, :])
        ld_sb1 = DMA("dY", sb1s[:, :, :], sb1_r[:, :, :])
        ld_rb1 = DMA("dY", rb1s[:, :, :], rb1_r[:, :, :])
        ld_cb = DMA("dY", cbias_s[:, :], cbias[:, :])
        ld_centf = DMA("dY", centf[:, :, :], cent_r.rearrange("(dc p) j -> p dc j", p=128))
        ld_bv = DMA("dG", bvrow[:, :], bv_in[:, :])
        ld_rb2 = DMA("dG", b2ext[:, :], rb2[:, :])
        ld_sb2 = DMA("dG", sb2n[:, :], sb2[:, :])
        ld_centh = DMA("dG", centhi[:, :, :], cent_r.rearrange("(dc p) j -> p dc j", p=128))

        s1 = S.op("V", lambda e: e.tensor_add(bqs[:, :], bqr_s[:, 0:4], bqr_s[:, 4:8]), deps=[ld_bqr])
        s2 = S.op("V", lambda e: e.tensor_add(bqs[:, :], bqs[:, :], bqr_s[:, 8:12]), deps=[s1])
        bqs_done = S.op("V", lambda e: e.tensor_add(bqs[:, :], bqs[:, :], bqr_s[:, 12:16]), deps=[s2])
        clo = S.op("V", lambda e: e.tensor_sub(centlo[:, :, :], centf[:, :, :], centhi[:, :, :]),
                   deps=[ld_centf, ld_centh])
        ones_set = S.op("V", lambda e: e.memset(ones1[:, :], 1.0))
        ones2_set = S.op("V", lambda e: e.memset(ones2[:, :], 1.0))

        def fold_weight(src_dram, out_cvt_fn, prev_users):
            outs = []
            prevA = prev_users
            for ah in range(2):
                l0 = DMA("dG", wstg[:, :, :],
                         src_dram[256 * ah:256 * ah + 256, :].rearrange("(a p) d -> p a d", p=128),
                         deps=prevA)
                fold = l0
                addstep = None
                for r in range(1, 4):
                    lb = DMA("dG", wstg2[:, :, :],
                             src_dram[512 * r + 256 * ah:512 * r + 256 * ah + 256, :]
                             .rearrange("(a p) d -> p a d", p=128),
                             deps=[addstep] + prevA)
                    addstep = S.op("V", lambda e: e.tensor_add(wstg[:, :, :], wstg[:, :, :],
                                                              wstg2[:, :, :]),
                                   deps=[l0, lb, addstep])
                outs.append(out_cvt_fn(ah, addstep))
                prevA = [outs[-1][0]]
            return outs

        def emit_tr(srcbuf, a_local, ah, dst, dstcol_base, dep):
            evs = []
            for dc in range(NDC):
                def mk(pap, tst, d=dc, ds=dst, cb=dstcol_base):
                    return S.op("V", lambda e: e.tensor_copy(ds[:, d, cb:cb + 128], pap),
                                deps=[tst])
                tst, ev = TRP(srcbuf[:, a_local, dc * 128:(dc + 1) * 128], 128, 128,
                              ident[:, :], mk, [dep, ld_ident])
                evs.append(ev)
            return evs

        def wq_cvt(ah, fold_step):
            cvt = S.op("V", lambda e: e.tensor_copy(wbfh[:, :, :], wstg[:, :, :]), deps=[fold_step])
            evs = []
            for al in range(2):
                evs += emit_tr(wbfh, al, ah, WQT, (2 * ah + al) * 128, cvt)
            return [evs[-1], cvt] + evs

        wq_outs = fold_weight(wq, wq_cvt, [])
        wqt_done = wq_outs[-1][0]
        wq_users = [s for o in wq_outs for s in o]

        def pf_cvt(ah, fold_step):
            cvt = S.op("V", lambda e, ah=ah: e.tensor_scalar(
                Pfold[:, 2 * ah:2 * ah + 2, :], wstg[:, :, :], RSQ128, None, ALU.mult),
                deps=[fold_step])
            return [cvt]

        pf_outs = fold_weight(proj, pf_cvt, wq_users)
        cvt_pf = pf_outs[-1][0]


        kv_tr_last = {}
        prev_half_load = None
        for (wsrc, dst) in ((wk, WKT), (wv, WVT)):
            for ah in range(2):
                lh = DMA("dG", wbfh2[:, :, :],
                         wsrc[256 * ah:256 * ah + 256, :].rearrange("(a p) d -> p a d", p=128),
                         deps=[prev_half_load])
                evs = []
                for al in range(2):
                    evs += emit_tr(wbfh2, al, ah, dst, (2 * ah + al) * 128, lh)
                prev_half_load = evs[-1]
                kv_tr_last[id(dst)] = evs[-1]
        wkt_done = kv_tr_last[id(WKT)]
        wvt_done = kv_tr_last[id(WVT)]

        conv_w1_done = {}
        conv_w2_done = {}
        conv_order = [16, 17] + list(range(16))
        for j in conv_order:
            w1_ap = sw1[j - 16] if j >= 16 else rw1[j]
            w2_ap = sw2[j - 16] if j >= 16 else rw2[j]
            for fs in range(NFC):
                wr = S.dma("dGc", lambda e,
                           a=scr_w1[j, fs].rearrange("p (dc f) -> p dc f", f=128),
                           b=w1_ap[:, fs * 128:(fs + 1) * 128].rearrange("(dc p) f -> p dc f", p=128):
                           e.dma_start(a, b), deps=[])
                conv_w1_done[(j, fs)] = wr
            for hb in range(8):
                wr = S.dma("dGc", lambda e,
                           a=scr_w2[j, hb].rearrange("p (fc x) -> p fc x", x=256),
                           b=w2_ap[:, hb * 256:(hb + 1) * 256].rearrange("(fc p) d -> p fc d", p=128):
                           e.dma_start(a, b), deps=[])
                conv_w2_done[(j, hb)] = wr
        cv1_last = [conv_w1_done[(15, NFC - 1)], conv_w1_done[(15, NFC - 2)]]
        cv2_last = [conv_w2_done[(15, 7)], conv_w2_done[(15, 6)]]

        bar_snap = S.snapshot()
        bar_steps = [S.op(e, lambda eng: eng.nop(), deps=bar_snap) for e in ("T", "V", "S", "Y")]

        P1a.close()

        PHq = ExitStack()
        A2 = lambda name, shape, dt=F32: PHq.enter_context(nc.sbuf_tensor(name, shape, dt))
        xt = [A2("xt0", [128, D]), A2("xt1", [128, D])]
        stats = [A2("st0", [128, 8]), A2("st1", [128, 8])]
        hib = [A2("hib0", [128, D], BF16), A2("hib1", [128, D], BF16)]
        lob = A2("lob", [128, D], BF16)
        xfb = A2("xfb", [128, D])
        xnWq = A2("xnWq", [128, NDC, 1, 512], BF16)

        def ln_tile(bufs, src_ap, row0, b, want_lo, prevu, dst_scr, prev_lo):
            xt, stats, hib, lob, xfb = bufs
            xb = xt[b]
            hb = hib[b]
            dummy = xfb if xfb is not None else hb
            ld = DMA("dY", xb[:, :], src_ap[row0:row0 + 128, :], deps=prevu)
            st = stats[b]
            r1 = S.op("V", lambda e, xb=xb: e.reduce_sum(st[:, 0:1], xb[:, :], axis=AX.X), deps=[ld])
            sq = S.op("S", lambda e, xb=xb, dummy=dummy: e.activation(dummy[:, :], xb[:, :], AF.Square,
                                                                     accum_out=st[:, 4:5]),
                      deps=[ld] + prevu + prev_lo)
            m = S.op("V", lambda e: e.tensor_scalar(st[:, 1:2], st[:, 0:1], float(1.0 / D), None, ALU.mult),
                     deps=[r1])
            v2 = S.op("V", lambda e: e.tensor_scalar(st[:, 2:3], st[:, 4:5], float(1.0 / D), 1e-5,
                                                     ALU.mult, ALU.add), deps=[sq])
            mm = S.op("V", lambda e: e.tensor_mul(st[:, 3:4], st[:, 1:2], st[:, 1:2]), deps=[m])
            vv = S.op("V", lambda e: e.tensor_sub(st[:, 2:3], st[:, 2:3], st[:, 3:4]), deps=[v2, mm])
            lnv = S.op("S", lambda e: e.activation(st[:, 7:8], st[:, 2:3], AF.Ln), deps=[vv])
            rstd = S.op("S", lambda e: e.activation(st[:, 5:6], st[:, 7:8], AF.Exp, scale=-0.5), deps=[lnv])
            nmr = S.op("V", lambda e: e.tensor_scalar(st[:, 6:7], st[:, 1:2], st[:, 5:6], -1.0,
                                                      ALU.mult, ALU.mult), deps=[rstd])
            hi = S.op("S", lambda e, xb=xb, hb=hb: e.activation(hb[:, :], xb[:, :], AF.Identity,
                                                                bias=st[:, 6:7], scale=st[:, 5:6]),
                      deps=[nmr] + prevu)
            lo_store = None
            lo_users = []
            xf = lo = None
            if want_lo:
                xf = S.op("S", lambda e, xb=xb: e.activation(xfb[:, :], xb[:, :], AF.Identity,
                                                             bias=st[:, 6:7], scale=st[:, 5:6]),
                          deps=[nmr, sq] + prev_lo)
                lo = S.op("V", lambda e, hb=hb: e.tensor_sub(lob[:, :], xfb[:, :], hb[:, :]),
                          deps=[xf, hi] + prev_lo)
                lo_store = DMA("dY", scr_qlo[row0:row0 + 128, :], lob[:, :], deps=[lo])
                lo_users = [xf, lo, lo_store]
            hi_store = DMA("dY", dst_scr[row0:row0 + 128, :], hb[:, :], deps=[hi])
            last = [hi_store, sq, hi, r1]
            if want_lo:
                last += [xf, lo]
            return hi_store, lo_store, last, lo_users

        q_hi = []
        q_lo = []
        prev_lo = []
        prevu = [list(bar_steps), list(bar_steps)]
        qwin_users = [[], []]
        pq_ev = [None, None]
        qev_all = []
        for tc in range(4):
            for kk in range(4):
                k = tc * 4 + kk
                hs, ls, lu, lou = ln_tile((xt, stats, hib, lob, xfb), xq, 128 * k, k % 2, True,
                                          prevu[k % 2], scr_q, prev_lo)
                q_hi.append(hs)
                q_lo.append(ls)
                prevu[k % 2] = lu
                prev_lo = lou
            w = 0
            trs = []
            for dc in range(NDC):
                t = DMT("dS", xnWq[:, dc, w, :],
                        scr_q[tc * 512:(tc + 1) * 512, dc * 128:(dc + 1) * 128],
                        deps=q_hi[4 * tc:4 * tc + 4] + qwin_users[w] + bar_steps)
                trs.append(t)
            users = []
            for a in range(4):
                g = a % 2
                ps = [psA, psB][g]

                def qfn(e, a=a, w=w, ps=ps):
                    for dc in range(NDC):
                        ins = e.matmul(ps[:, :], WQT[:, dc, a * 128:(a + 1) * 128],
                                       xnWq[:, dc, w, :], start=(dc == 0), stop=(dc == NDC - 1))
                    return ins
                mmst = S.op("T", qfn, deps=trs + [wqt_done, pq_ev[g]])
                ev = S.op("S", lambda e, a=a, tc=tc, ps=ps: e.activation(
                    QsumT[:, a, tc * 512:(tc + 1) * 512], ps[:, :], AF.Identity,
                    bias=bqs[:, a:a + 1]), deps=[mmst, bqs_done])
                pq_ev[g] = ev
                qev_all.append(ev)
                users.append(mmst)
            qwin_users[w] = users

        bar_snap = S.snapshot()
        bar_steps = [S.op(e, lambda eng: eng.nop(), deps=bar_snap) for e in ("T", "V", "S", "Y")]
        PHq.close()
        P1wq.close()


        PHkv = ExitStack()
        A2 = lambda name, shape, dt=F32: PHkv.enter_context(nc.sbuf_tensor(name, shape, dt))
        xt = [A2("kxt0", [128, D]), A2("kxt1", [128, D])]
        stats = [A2("kst0", [128, 8]), A2("kst1", [128, 8])]
        hib = [A2("khib0", [128, D], BF16), A2("khib1", [128, D], BF16)]
        xnWk = A2("xnWk", [128, NDC, 1, 512], BF16)

        kv_hi = []
        prevu = [list(bar_steps), list(bar_steps)]
        prev_lo = []
        for k in range(32):
            hs, _, lu, _ = ln_tile((xt, stats, hib, None, None), xkv, 128 * k, k % 2, False,
                                   prevu[k % 2], scr_kv, prev_lo)
            kv_hi.append(hs)
            prevu[k % 2] = lu

        win_users = [bar_steps, bar_steps]
        pk_ev = [qev_all[-1], qev_all[-1]]
        pv_ev = [None, None]
        k_ev_all = []
        v_ev_all = []
        for tc in range(8):
            w = 0
            trs = []
            for dc in range(NDC):
                t = DMT("dS", xnWk[:, dc, w, :],
                        scr_kv[tc * 512:(tc + 1) * 512, dc * 128:(dc + 1) * 128],
                        deps=kv_hi[4 * tc:4 * tc + 4] + list(win_users[w]))
                trs.append(t)
            users = []
            for a in range(4):
                g = a % 2
                ps = [psA, psB][g]

                def kfn(e, a=a, w=w, ps=ps):
                    for dc in range(NDC):
                        ins = e.matmul(ps[:, :], WKT[:, dc, a * 128:(a + 1) * 128],
                                       xnWk[:, dc, w, :], start=(dc == 0), stop=(dc == NDC - 1))
                    return ins
                mmst = S.op("T", kfn, deps=trs + [wkt_done, pk_ev[g]])
                ev = S.op("S", lambda e, a=a, tc=tc, ps=ps: e.activation(
                    KT[:, a, tc * 512:(tc + 1) * 512], ps[:, :], AF.Identity,
                    bias=bks[:, a:a + 1]), deps=[mmst, ld_bk])
                pk_ev[g] = ev
                k_ev_all.append(ev)
                users.append(mmst)
            for sub in range(4):
                g = sub % 2
                ps = [psC, psD][g]
                kb = tc * 4 + sub

                def vfn(e, sub=sub, w=w, ps=ps):
                    for dc in range(NDC):
                        e.matmul(ps[:, :], xnWk[:, dc, w, sub * 128:(sub + 1) * 128],
                                 WVT[:, dc, :], start=(dc == 0), stop=False)
                    return e.matmul(ps[:, :], ones1[:, :], bvrow[:, :], start=False, stop=True)
                mmst = S.op("T", vfn, deps=trs + [wvt_done, ld_bv, ones_set, pv_ev[g]])
                ev = S.op("V", lambda e, kb=kb, ps=ps: e.tensor_copy(Vsb[:, kb, :], ps[:, :]),
                          deps=[mmst])
                pv_ev[g] = ev
                v_ev_all.append(ev)
                users.append(mmst)
            win_users[w] = users

        if dbg:
            d1 = DMA("dY", dbg_kt[:, :, :], KT[:, :, :], deps=k_ev_all)
            d2 = DMA("dY", dbg_v[:, :, :], Vsb[:, :, :], deps=v_ev_all)
            d3 = DMA("dY", dbg_qs[:, :, :], QsumT[:, :, :], deps=qev_all)
            dbg_proj = [d1, d2, d3]
        else:
            dbg_proj = []

        bar_snap = S.snapshot()
        bar_steps = [S.op(e, lambda eng: eng.nop(), deps=bar_snap) for e in ("T", "V", "S", "Y")]
        PHkv.close()


        P4 = ExitStack()
        A4 = lambda name, shape, dt=F32: P4.enter_context(nc.sbuf_tensor(name, shape, dt))
        masks = A4("masks_sb", [128, 16, 256], BF16)
        srow = A4("srow", [128, TKV])
        Pbf = A4("pbf", [128, TKV], BF16)
        PTb = A4("ptb", [128, 32, 128], BF16)
        Zbf = A4("zbf", [128, 512], BF16)
        ZTb = A4("ztb", [128, 4, 128], BF16)
        stile = [A4("stile0", [128, D], BF16), A4("stile1", [128, D], BF16)]
        smax = A4("smax", [128, 2])
        ssum = A4("ssum", [128, 2])

        ld_masks = DMA("dY", masks[:, :, :], masks_in[:, :, :], deps=bar_steps)

        psS_ev = [None, None]
        s_stores = [None, None]
        prev_srow_reader = None
        prev_pbf_reader = None
        prev_rcp = None
        prev_av = None
        prev_zread = None
        psZ_ev = None
        psSP_ev = [None, None]
        s_store_list = []

        for i in range(16):
            nkeys = 256 * (i + 1)
            nkb = nkeys // 128
            blocks = []
            off = 0
            while off < nkeys:
                ln_ = min(512, nkeys - off)
                blocks.append((off, ln_))
                off += ln_
            ev_blocks = []
            for bi, (off, ln_) in enumerate(blocks):
                g = bi % 2
                ps = [psA, psB][g]

                def scfn(e, i=i, off=off, ln_=ln_, ps=ps):
                    for a in range(4):
                        ins = e.matmul(ps[:, 0:ln_], QsumT[:, a, i * 128:(i + 1) * 128],
                                       KT[:, a, off:off + ln_], start=(a == 0), stop=(a == 3))
                    return ins
                tc_max = (off + ln_ - 1) // 512
                mmst = S.op("T", scfn,
                            deps=[psS_ev[g]] + qev_all[:4 * (i // 4 + 1)]
                            + k_ev_all[:4 * (tc_max + 1)] + dbg_proj)
                ev = S.op("V", lambda e, off=off, ln_=ln_, ps=ps: e.tensor_copy(
                    srow[:, off:off + ln_], ps[:, 0:ln_]),
                    deps=[mmst, prev_srow_reader])
                psS_ev[g] = ev
                ev_blocks.append(ev)
            mk = S.op("V", lambda e, i=i, nk=nkeys: e.tensor_add(
                srow[:, nk - 256:nk], srow[:, nk - 256:nk], masks[:, i, :]),
                deps=[ev_blocks[-1], ld_masks])
            nmx = S.op("V", lambda e, nk=nkeys: e.reduce_max(
                smax[:, 0:1], srow[:, 0:nk], axis=AX.X, negate=True),
                deps=[mk, prev_pbf_reader] + ev_blocks)
            exps = S.op("S", lambda e, nk=nkeys: e.activation(
                Pbf[:, 0:nk], srow[:, 0:nk], AF.Exp, bias=smax[:, 0:1],
                accum_out=ssum[:, 0:1]),
                deps=[nmx, prev_pbf_reader, prev_rcp])
            rcp = S.op("V", lambda e: e.reciprocal(ssum[:, 1:2], ssum[:, 0:1]),
                       deps=[exps, prev_zread])
            prev_rcp = rcp
            pt_evs = []
            last_pt_t = None
            for kb in range(nkb):
                def mk(pap, tst, kb=kb):
                    return S.op("V", lambda e: e.tensor_copy(PTb[:, kb, :], pap),
                                deps=[tst, prev_av])
                tst, evt = TRP(Pbf[:, kb * 128:(kb + 1) * 128], 128, 128, ident[:, :],
                               mk, [exps, prev_av])
                pt_evs.append(evt)
                last_pt_t = tst

            def avfn(e, nkb=nkb):
                for kb in range(nkb):
                    ins = e.matmul(psC[:, :], PTb[:, kb, :], Vsb[:, kb, :],
                                   start=(kb == 0), stop=(kb == nkb - 1))
                return ins
            av = S.op("T", avfn,
                      deps=pt_evs + v_ev_all[:4 * ((nkeys - 1) // 512 + 1)] + [psZ_ev])
            prev_av = av
            prev_pbf_reader = last_pt_t
            prev_srow_reader = exps
            zev = S.op("V", lambda e: e.tensor_scalar(
                Zbf[:, :], psC[:, :], ssum[:, 1:2], None, ALU.mult),
                deps=[av, rcp, prev_zread])
            psZ_ev = zev
            zt_evs = []
            zt_ts = []
            for c in range(4):
                def mk(pap, tst, c=c):
                    return S.op("V", lambda e: e.tensor_copy(ZTb[:, c, :], pap),
                                deps=[tst, prev_zread])
                tst, evt = TRP(Zbf[:, c * 128:(c + 1) * 128], 128, 128, ident[:, :],
                               mk, [zev, prev_zread])
                zt_evs.append(evt)
                zt_ts.append(tst)
            prev_zread = zt_ts[-1]
            sb = i % 2
            sp_evs = []
            for dg in range(4):
                g = dg % 2
                ps = [psE, psF][g]

                def spfn(e, dg=dg, ps=ps):
                    for c in range(4):
                        ins = e.matmul(ps[:, :], ZTb[:, c, :], Pfold[:, c, dg * 512:(dg + 1) * 512],
                                       start=(c == 0), stop=(c == 3))
                    return ins
                mmst = S.op("T", spfn, deps=zt_evs + [cvt_pf, psSP_ev[g]])
                ev = S.op("S", lambda e, dg=dg, ps=ps, sb=sb: e.activation(
                    stile[sb][:, dg * 512:(dg + 1) * 512], ps[:, :], AF.Copy),
                    deps=[mmst, s_stores[sb]])
                psSP_ev[g] = ev
                sp_evs.append(ev)
            sst = DMA("dY", s_scr[i * 128:(i + 1) * 128, :], stile[sb][:, :], deps=sp_evs)
            s_stores[sb] = sst
            s_store_list.append(sst)

        bar_snap = S.snapshot()
        bar_steps = [S.op(e, lambda eng: eng.nop(), deps=bar_snap) for e in ("T", "V", "S", "Y")]
        P4.close()
        ATT.close()

        P5 = ExitStack()
        A5 = lambda name, shape, dt=F32: P5.enter_context(nc.sbuf_tensor(name, shape, dt))
        xnqTh = A5("xnqTh", [128, NDC, HALF], BF16)
        lowin = A5("lowin", [128, 2, HALF], BF16)
        acc = A5("acc", [128, 8, D])
        hT = A5("hT", [128, NFC, HALF], BF16)
        w2b = A5("w2b", [128, NFC, 2, 512], BF16)
        w1b = [A5("w1b0", [128, NDC, 128], BF16), A5("w1b1", [128, NDC, 128], BF16)]
        sct = A5("sct", [16, HALF])
        schi = A5("schi", [16, 128], BF16)
        sclo = A5("sclo", [16, 128], BF16)
        schiT = A5("schiT", [128, 16])
        sc32 = A5("sc32", [128, 16])
        rnk = A5("rnk", [128, 16])
        mskb = A5("mskb", [128, 16])
        graw = A5("graw", [128, 16])
        den = A5("den", [128, 2])
        gating = A5("gating", [128, 8, NEXP])
        gbf = A5("gbf", [128, NEXP], BF16)
        gT = A5("gt", [16, 8, 128], BF16)
        xnqrl = [A5("xnqrl0", [128, D], BF16), A5("xnqrl1", [128, D], BF16)]
        xlorl = A5("xlorl", [128, D], BF16)
        xqrl = A5("xqrl", [128, D])
        srl = A5("srl", [128, D], BF16)
        gsil = A5("gsil", [128, D], BF16)
        fin = A5("fin", [128, D])

        b0 = bar_steps[0]
        U = {
            "xnqTh": list(bar_steps), "lowin": [b0, b0], "sct": [b0],
            "schi": b0, "sclo": b0, "schiT": b0, "sc32": b0, "rnk": b0,
            "mskb": b0, "graw": b0, "den": b0, "gbf": b0,
            "gat": [b0] * 8, "gT": [b0] * 8,
            "acc": {}, "hT": [b0], "w2b": [[b0], [b0]], "w1b": [[b0], [b0]],
            "xnqrl": [b0, b0], "xlorl": b0, "xqrl": b0,
            "srl": b0, "gsil": b0, "fin": b0,
            "psH": [bar_steps[1], bar_steps[1]], "psY": [bar_steps[1]] * 4,
            "psR": [bar_steps[1], bar_steps[1]],
        }
        out_stores = []

        for half in range(2):
            hoff = half * HALF
            trh = []
            for dc in range(NDC):
                t = DMT("dS", xnqTh[:, dc, :],
                        scr_q[hoff:hoff + HALF, dc * 128:(dc + 1) * 128],
                        deps=q_hi + U["xnqTh"])
                trh.append(t)

            rt_steps = []
            for dc in range(NDC):
                tl = DMT("dS", lowin[:, dc % 2, :],
                         scr_qlo[hoff:hoff + HALF, dc * 128:(dc + 1) * 128],
                         deps=q_lo + [U["lowin"][dc % 2]])

                def rtfn(e, dc=dc):
                    for sub in range(2):
                        ps = [psE, psF][sub]
                        e.matmul(ps[0:16, :], centhi[:, dc, :],
                                 xnqTh[:, dc, sub * 512:(sub + 1) * 512],
                                 start=(dc == 0), stop=False)
                        e.matmul(ps[0:16, :], centlo[:, dc, :],
                                 xnqTh[:, dc, sub * 512:(sub + 1) * 512],
                                 start=False, stop=False)
                        ins = e.matmul(ps[0:16, :], centhi[:, dc, :],
                                       lowin[:, dc % 2, sub * 512:(sub + 1) * 512],
                                       start=False, stop=(dc == NDC - 1))
                    return ins
                st = S.op("T", rtfn, deps=[trh[dc], tl, clo, ld_centh] + U["psR"])
                U["lowin"][dc % 2] = st
                rt_steps.append(st)
            rt_evs = []
            for sub in range(2):
                ps = [psE, psF][sub]
                ev = S.op("S", lambda e, sub=sub, ps=ps: e.activation(
                    sct[:, sub * 512:(sub + 1) * 512], ps[0:16, :], AF.Sigmoid),
                    deps=[rt_steps[-1]] + U["sct"])
                rt_evs.append(ev)
            badd = S.op("V", lambda e: e.tensor_scalar_add(sct[:, :], sct[:, :], cbias_s[:, 0:1]),
                        deps=rt_evs + [ld_cb])
            U["psY"][2] = rt_evs[0]
            U["psY"][3] = rt_evs[1]

            gat_steps = []
            gT_steps = []
            c2 = None
            for tt in range(8):
                c1 = S.op("V", lambda e, tt=tt: e.tensor_copy(schi[:, :], sct[:, tt * 128:(tt + 1) * 128]),
                          deps=[badd, U["schi"], c2])
                c2 = S.op("V", lambda e, tt=tt: e.tensor_sub(
                    sclo[:, :], sct[:, tt * 128:(tt + 1) * 128], schi[:, :]),
                    deps=[c1, U["sclo"]])
                def mk1(pap, tst):
                    return S.op("V", lambda e, pap=pap: e.tensor_copy(schiT[:, :], pap),
                                deps=[tst, U["schiT"]])
                t1, e1 = TRP(schi[:, :], 128, 16, ident[0:16, 0:16], mk1, [c1])

                def mk2(pap, tst):
                    return S.op("V", lambda e, pap=pap: e.tensor_add(sc32[:, :], schiT[:, :], pap),
                                deps=[tst, e1, U["sc32"]])
                t2, e2 = TRP(sclo[:, :], 128, 16, ident[0:16, 0:16], mk2, [c2])
                U["schiT"] = e2
                U["schi"] = t1
                U["sclo"] = t2
                prev = S.op("V", lambda e: e.tensor_scalar(rnk[:, :], sc32[:, :], sc32[:, 0:1],
                                                           None, ALU.is_lt),
                            deps=[e2, U["rnk"]])
                for j in range(1, NEXP):
                    prev = S.op("V", lambda e, j=j: e.scalar_tensor_tensor(
                        rnk[:, :], sc32[:, :], sc32[:, j:j + 1], rnk[:, :], ALU.is_lt, ALU.add),
                        deps=[prev])
                mk1 = S.op("V", lambda e: e.tensor_scalar(mskb[:, :], rnk[:, :], 3.5, None, ALU.is_lt),
                           deps=[prev, U["mskb"]])
                gr = S.op("V", lambda e: e.tensor_mul(graw[:, :], sc32[:, :], mskb[:, :]),
                          deps=[mk1, U["graw"]])
                U["sc32"] = gr
                U["rnk"] = mk1
                ds = S.op("V", lambda e: e.reduce_sum(den[:, 0:1], graw[:, :], axis=AX.X),
                          deps=[gr, U["den"]])
                dm = S.op("V", lambda e: e.tensor_scalar_max(den[:, 1:2], den[:, 0:1], 1e-8),
                          deps=[ds])
                dr = S.op("V", lambda e: e.reciprocal(den[:, 0:1], den[:, 1:2]), deps=[dm])
                U["den"] = dr
                ga = S.op("V", lambda e, tt=tt: e.tensor_scalar(
                    gating[:, tt, :], graw[:, :], den[:, 0:1], None, ALU.mult),
                    deps=[dr, U["gat"][tt]])
                U["mskb"] = ga
                U["graw"] = ga
                gb = S.op("V", lambda e, tt=tt: e.tensor_copy(gbf[:, :], gating[:, tt, :]),
                          deps=[ga, U["gbf"]])
                def mk3(pap, tst, tt=tt):
                    return S.op("V", lambda e, pap=pap: e.tensor_copy(gT[:, tt, :], pap),
                                deps=[tst, U["gT"][tt]])
                t3, e3 = TRP(gbf[:, :], 16, 128, ident[:, :], mk3, [gb])
                U["gbf"] = t3
                gat_steps.append(ga)
                gT_steps.append(e3)
                U["gat"][tt] = ga
                U["gT"][tt] = e3

            U["sct"] = [c2]
            if dbg:
                dgat = DMA("dY", dbg_gat[:, half * 8:(half + 1) * 8, :], gating[:, :, :],
                           deps=gat_steps)
                for tt in range(8):
                    U["gat"][tt] = dgat

            def expert_pass(jconv, b1s, b1col, mode, gcol):
                h_evs = []
                for fs in range(NFC):
                    wb = w1b[fs % 2]
                    w1sl = scr_w1[jconv, fs].rearrange("p (dc f) -> p dc f", f=128)
                    q = "dG" if fs % 2 == 0 else "dY"
                    dma1 = DMA(q, wb[:, :, :], w1sl,
                               deps=U["w1b"][fs % 2] + [conv_w1_done[(jconv, fs)]])
                    dma2 = None
                    musers = []
                    for sub in range(2):
                        g = (fs * 2 + sub) % 2
                        ps = [psA, psB][g]

                        def hfn(e, wb=wb, sub=sub, ps=ps):
                            for dc in range(NDC):
                                ins = e.matmul(ps[:, :], wb[:, dc, :],
                                               xnqTh[:, dc, sub * 512:(sub + 1) * 512],
                                               start=(dc == 0), stop=(dc == NDC - 1))
                            return ins
                        mmst = S.op("T", hfn,
                                    deps=[dma1, U["psH"][g]] + trh + U["hT"])
                        ev = S.op("S", lambda e, fs=fs, sub=sub, ps=ps, b1s=b1s, b1col=b1col: e.activation(
                            hT[:, fs, sub * 512:(sub + 1) * 512], ps[:, :], AF.Relu,
                            bias=b1s[:, fs, b1col:b1col + 1]),
                            deps=[mmst, ld_sb1, ld_rb1])
                        U["psH"][g] = ev
                        h_evs.append(ev)
                        musers.append(mmst)
                    U["w1b"][fs % 2] = musers
                y_last_mm = None
                for qd in range(4):
                    qb = qd % 2
                    wd1 = DMA("dG", w2b[:, :, qb, 0:256],
                              scr_w2[jconv, 2 * qd].rearrange("p (fc x) -> p fc x", x=256),
                              deps=U["w2b"][qb] + [conv_w2_done[(jconv, 2 * qd)]])
                    wd2 = DMA("dY", w2b[:, :, qb, 256:512],
                              scr_w2[jconv, 2 * qd + 1].rearrange("p (fc x) -> p fc x", x=256),
                              deps=U["w2b"][qb] + [conv_w2_done[(jconv, 2 * qd + 1)]])
                    yusers = []
                    for tt in range(8):
                        bk = (qd * 8 + tt) % 4
                        ps = psY[bk]
                        extras = (mode == "init")
                        if extras:
                            rl1 = DMA("dY", xnqrl[tt % 2][:, :],
                                      scr_q[hoff + tt * 128:hoff + (tt + 1) * 128, :],
                                      deps=[U["xnqrl"][tt % 2]] + q_hi)
                            rl2 = DMA("dY", xlorl[:, :],
                                      scr_qlo[hoff + tt * 128:hoff + (tt + 1) * 128, :],
                                      deps=[U["xlorl"]] + q_lo)

                        def yfn(e, qd=qd, qb=qb, tt=tt, ps=ps, extras=extras):
                            for fs in range(NFC):
                                ins = e.matmul(ps[:, :], hT[:, fs, tt * 128:(tt + 1) * 128],
                                               w2b[:, fs, qb, :],
                                               start=(fs == 0), stop=(fs == NFC - 1 and not extras))
                            if extras:
                                e.matmul(ps[:, :], gT[:, tt, :], b2ext[:, qd * 512:(qd + 1) * 512],
                                         start=False, stop=False)
                                e.matmul(ps[:, :], ones2[:, :], sb2n[:, qd * 512:(qd + 1) * 512],
                                         start=False, stop=False)
                                e.matmul(ps[:, :], ident[:, :], xnqrl[tt % 2][:, qd * 512:(qd + 1) * 512],
                                         start=False, stop=False)
                                ins = e.matmul(ps[:, :], ident[:, :], xlorl[:, qd * 512:(qd + 1) * 512],
                                               start=False, stop=True)
                            return ins
                        deps = h_evs + [wd1, wd2, U["psY"][bk]]
                        if extras:
                            deps += [gT_steps[tt], ld_rb2, ld_sb2, ones2_set, rl1, rl2, ld_ident]
                        mmst = S.op("T", yfn, deps=deps)
                        if extras:
                            U["xnqrl"][tt % 2] = mmst
                            U["xlorl"] = mmst
                        accdep = U["acc"].get((tt, qd))
                        if mode == "init":
                            ev = S.op("V", lambda e, tt=tt, qd=qd, ps=ps: e.tensor_copy(
                                acc[:, tt, qd * 512:(qd + 1) * 512], ps[:, :]),
                                deps=[mmst, accdep])
                        elif mode == "add":
                            ev = S.op("V", lambda e, tt=tt, qd=qd, ps=ps: e.scalar_tensor_tensor(
                                acc[:, tt, qd * 512:(qd + 1) * 512], ps[:, :], 1.0,
                                acc[:, tt, qd * 512:(qd + 1) * 512], ALU.mult, ALU.add),
                                deps=[mmst, accdep])
                        else:
                            ev = S.op("V", lambda e, tt=tt, qd=qd, ps=ps, gcol=gcol: e.scalar_tensor_tensor(
                                acc[:, tt, qd * 512:(qd + 1) * 512], ps[:, :],
                                gating[:, tt, gcol:gcol + 1],
                                acc[:, tt, qd * 512:(qd + 1) * 512], ALU.mult, ALU.add),
                                deps=[mmst, accdep, gat_steps[tt]])
                        U["acc"][(tt, qd)] = ev
                        U["psY"][bk] = ev
                        yusers.append(mmst)
                        y_last_mm = mmst
                    U["w2b"][qb] = yusers
                U["hT"] = [y_last_mm]
                return y_last_mm

            init_last = expert_pass(16, sb1s, 0, "init", None)
            for tt in range(8):
                U["gT"][tt] = init_last
            expert_pass(17, sb1s, 1, "add", None)
            for j in range(NEXP):
                expert_pass(j, rb1s, j, "gated", j)
            U["psR"] = [U["psY"][2], U["psY"][3]]

            if dbg:
                dacc = DMA("dY", dbg_acc[:, half * 8:(half + 1) * 8, :], acc[:, :, :],
                           deps=[U["acc"][(tt, qd)] for tt in range(8) for qd in range(4)])
                dbg_acc_dep = [dacc]
            else:
                dbg_acc_dep = []

            for tt in range(8):
                acc_done = [U["acc"][(tt, qd)] for qd in range(4)] + dbg_acc_dep
                sil = S.op("S", lambda e, tt=tt: e.activation(gsil[:, :], acc[:, tt, :], AF.Silu),
                           deps=acc_done + [U["gsil"]])
                srload = DMA("dY", srl[:, :], s_scr[hoff + tt * 128:hoff + (tt + 1) * 128, :],
                             deps=[s_store_list[half * 8 + tt], U["srl"]])
                xqload = DMA("dY", xqrl[:, :], xq[hoff + tt * 128:hoff + (tt + 1) * 128, :],
                             deps=[U["xqrl"]])
                mu = S.op("V", lambda e: e.tensor_mul(fin[:, :], srl[:, :], gsil[:, :]),
                          deps=[sil, srload, U["fin"]])
                ad = S.op("V", lambda e: e.tensor_add(xqrl[:, :], fin[:, :], xqrl[:, :]),
                          deps=[mu, xqload])
                ost = DMA("dY", out[hoff + tt * 128:hoff + (tt + 1) * 128, :], xqrl[:, :],
                          deps=[ad])
                out_stores.append(ost)
                U["gsil"] = mu
                U["fin"] = ad
                U["srl"] = mu
                U["xqrl"] = ost
                for qd in range(4):
                    U["acc"][(tt, qd)] = sil

            U["xnqTh"] = [U["hT"][0], rt_steps[-1]]

        S.op("Y", lambda e: e.nop(), deps=out_stores)

        S.emit(nc, sems)
        P5.close()
    return nc


_PROG_CACHE = {}


def _get_prog(dbg=0, serialize=False):
    key = (dbg, serialize)
    if key not in _PROG_CACHE:
        _PROG_CACHE[key] = build_program(dbg=dbg, serialize=serialize)
    return _PROG_CACHE[key]


def make_in_maps(inputs, dbg=0):
    f32 = np.float32
    g = lambda k: np.ascontiguousarray(np.asarray(inputs[k], dtype=f32))
    x = g("L0_attn_x")
    WQ = g("L0_attn_WQ"); bQ = g("L0_attn_bQ")
    WK = g("L0_attn_WK"); bK = g("L0_attn_bK")
    WV = g("L0_attn_WV"); bV = g("L0_attn_bV")
    PJ = g("L0_attn_proj")
    sw1 = g("L0_ffn_shared_w1"); sb1 = g("L0_ffn_shared_b1")
    sw2 = g("L0_ffn_shared_w2"); sb2 = g("L0_ffn_shared_b2")
    rw1 = g("L0_ffn_routed_w1"); rb1 = g("L0_ffn_routed_b1")
    rw2 = g("L0_ffn_routed_w2"); rb2 = g("L0_ffn_routed_b2")
    cent = g("L0_ffn_centroids"); cb = g("L0_ffn_bias")

    common = {
        "ident": np.eye(128, dtype=BF),
        "wq": WQ,
        "bq_r": np.ascontiguousarray(bQ.reshape(16, 128).T),
        "wk": WK,
        "bk_r": np.ascontiguousarray(bK.reshape(4, 128).T),
        "wv": WV,
        "bv": bV.reshape(1, DK),
        "proj": PJ,
        "sw1": sw1,
        "sb1_r": np.ascontiguousarray(sb1.reshape(2, NFC, 128).transpose(2, 1, 0)),
        "sw2": sw2,
        "sb2": sb2,
        "rw1": rw1,
        "rb1_r": np.ascontiguousarray(rb1.reshape(NEXP, NFC, 128).transpose(2, 1, 0)),
        "rw2": rw2,
        "rb2": rb2,
        "cent_r": np.ascontiguousarray(cent.T),
        "cbias": cb.reshape(NEXP, 1),
    }
    masks = [build_masks(0), build_masks(1)]
    in_maps = []
    for core in range(8):
        b, p = core // 2, core % 2
        J = own_tile_ids(p)
        xq_host = np.ascontiguousarray(
            np.concatenate([x[b, 128 * j:128 * (j + 1), :] for j in J], axis=0))
        m = dict(common)
        m["xkv"] = x[b]
        m["xq"] = xq_host
        m["masks"] = masks[p]
        in_maps.append(m)
    return in_maps


def assemble(results):
    y = np.empty((4, TKV, D), dtype=np.float32)
    for core in range(8):
        b, p = core // 2, core % 2
        J = own_tile_ids(p)
        o = results[core]["out"]
        for i, j in enumerate(J):
            y[b, 128 * j:128 * (j + 1), :] = o[128 * i:128 * (i + 1), :]
    return y


def kernel(**inputs):
    nc = _get_prog(dbg=0)
    in_maps = make_in_maps(inputs)
    res = run_bass_kernel_spmd(nc, in_maps, core_ids=list(range(8)))
    return assemble(res.results)


# version 10
# speedup vs baseline: 1.7664x; 1.0215x over previous
import numpy as np
import ml_dtypes
from contextlib import ExitStack

import concourse.bass as bass
import concourse.mybir as mybir
from concourse.bass_utils import run_bass_kernel_spmd

F32 = mybir.dt.float32
BF16 = mybir.dt.bfloat16
AF = mybir.ActivationFunctionType
ALU = mybir.AluOpType
AX = mybir.AxisListType

BF = ml_dtypes.bfloat16
NEG = np.float32(-1e30)

D = 2048
DK = 512
TQ = 2048
TKV = 4096
DFF = 1408
NFC = 11
NDC = 16
NEXP = 16
HALF = 1024
RSQ128 = float(1.0 / np.sqrt(128.0))


def own_tile_ids(parity):
    lo = [2 * i + parity for i in range(8)]
    hi = [31 - 2 * i - parity for i in range(8)]
    return sorted(lo + hi)


def build_masks(parity):
    J = own_tile_ids(parity)
    m = np.zeros((128, 16, 256), dtype=np.float32)
    for i in range(16):
        q = 128 * J[i] + np.arange(128)[:, None]
        k = 256 * i + np.arange(256)[None, :]
        m[:, i, :] = np.where(k <= q, 0.0, NEG)
    return m.astype(BF)


class Step:
    __slots__ = ("idx", "ch", "fn", "deps", "amt", "val", "key")

    def __init__(self, idx, ch, fn, deps, amt, key):
        self.idx, self.ch, self.fn, self.deps, self.amt = idx, ch, fn, deps, amt
        self.val = None
        self.key = key


class Sched:
    ENGS = ("T", "V", "S", "G", "Y")
    CHANNELS = ("T", "V", "S", "G", "Y", "dY", "dS", "dG", "dGc")
    QUEUE_OF = {"dY": "Y", "dS": "S", "dG": "G", "dGc": "G"}
    NOBAR = ("dGc",)
    NROT = 16

    def __init__(self, serialize=False):
        self.steps = []
        self.last = {}
        self.rot = {c: 0 for c in self.CHANNELS}
        self.serialize = serialize

    def _add(self, ch, fn, deps, amt):
        deps = [d for d in deps if d is not None]
        if self.serialize and self.steps:
            deps = deps + [self.steps[-1]]
        if amt == 16:
            key = (ch, self.rot[ch] % self.NROT)
            self.rot[ch] += 1
        else:
            key = (ch, 0)
        st = Step(len(self.steps), ch, fn, deps, amt, key)
        self.steps.append(st)
        self.last[key] = st
        return st

    def op(self, eng, fn, deps=()):
        return self._add(eng, fn, list(deps), 1)

    def dma(self, queue, fn, deps=()):
        return self._add(queue, fn, list(deps), 16)

    def snapshot(self):
        return [s for k, s in self.last.items() if k[0] not in self.NOBAR]

    def emit(self, nc, sems):
        referenced = set()
        for s in self.steps:
            for d in s.deps:
                referenced.add(d.idx)
        counts = {}
        for s in self.steps:
            if s.amt == 16 or s.idx in referenced:
                counts[s.key] = counts.get(s.key, 0) + s.amt
                s.val = counts[s.key]
        owner = {c: self.QUEUE_OF.get(c, c) for c in self.CHANNELS}

        with nc.Block() as blk:
            def emit_for(name, eng):
                water = {}
                for s in self.steps:
                    if owner[s.ch] != name:
                        continue
                    for d in s.deps:
                        if s.ch == "T" and d.ch == "T":
                            continue
                        if water.get(d.key, 0) < d.val:
                            eng.wait_ge(sems[d.key], d.val)
                            water[d.key] = d.val
                    if s.amt == 16 and s.val > 16 and water.get(s.key, 0) < s.val - 16:
                        eng.wait_ge(sems[s.key], s.val - 16)
                        water[s.key] = s.val - 16
                    instr = s.fn(eng)
                    if s.val is not None:
                        instr.then_inc(sems[s.key], s.amt)

            @blk.tensor
            def _(t):
                emit_for("T", t)

            @blk.vector
            def _(v):
                emit_for("V", v)

            @blk.scalar
            def _(s):
                emit_for("S", s)

            @blk.gpsimd
            def _(g):
                emit_for("G", g)

            @blk.sync
            def _(y):
                emit_for("Y", y)


def build_program(dbg=0, serialize=False):
    nc = bass.Bass("TRN2", target_bir_lowering=False)
    S = Sched(serialize=serialize)

    ap = lambda t: t.ap()
    xkv = ap(nc.dram_tensor("xkv", [TKV, D], F32, kind="ExternalInput"))
    xq = ap(nc.dram_tensor("xq", [TQ, D], F32, kind="ExternalInput"))
    masks_in = ap(nc.dram_tensor("masks", [128, 16, 256], BF16, kind="ExternalInput"))
    ident_in = ap(nc.dram_tensor("ident", [128, 128], BF16, kind="ExternalInput"))
    wq = ap(nc.dram_tensor("wq", [D, D], F32, kind="ExternalInput"))
    bq_r = ap(nc.dram_tensor("bq_r", [128, 16], F32, kind="ExternalInput"))
    wk = ap(nc.dram_tensor("wk", [DK, D], F32, kind="ExternalInput"))
    bk_r = ap(nc.dram_tensor("bk_r", [128, 4], F32, kind="ExternalInput"))
    wv = ap(nc.dram_tensor("wv", [DK, D], F32, kind="ExternalInput"))
    bv_in = ap(nc.dram_tensor("bv", [1, DK], F32, kind="ExternalInput"))
    proj = ap(nc.dram_tensor("proj", [D, D], F32, kind="ExternalInput"))
    sw1 = ap(nc.dram_tensor("sw1", [2, D, DFF], F32, kind="ExternalInput"))
    sb1_r = ap(nc.dram_tensor("sb1_r", [128, NFC, 2], F32, kind="ExternalInput"))
    sw2 = ap(nc.dram_tensor("sw2", [2, DFF, D], F32, kind="ExternalInput"))
    sb2 = ap(nc.dram_tensor("sb2", [2, D], F32, kind="ExternalInput"))
    rw1 = ap(nc.dram_tensor("rw1", [NEXP, D, DFF], F32, kind="ExternalInput"))
    rb1_r = ap(nc.dram_tensor("rb1_r", [128, NFC, NEXP], F32, kind="ExternalInput"))
    rw2 = ap(nc.dram_tensor("rw2", [NEXP, DFF, D], F32, kind="ExternalInput"))
    rb2 = ap(nc.dram_tensor("rb2", [NEXP, D], F32, kind="ExternalInput"))
    cent_r = ap(nc.dram_tensor("cent_r", [D, NEXP], F32, kind="ExternalInput"))
    cbias = ap(nc.dram_tensor("cbias", [NEXP, 1], F32, kind="ExternalInput"))

    dscr = "ExternalOutput" if dbg else "Internal"
    scr_kv = ap(nc.dram_tensor("scr_kv", [TKV, D], BF16, kind=dscr))
    scr_q = ap(nc.dram_tensor("scr_q", [TQ, D], BF16, kind=dscr))
    scr_qlo = ap(nc.dram_tensor("scr_qlo", [TQ, D], BF16, kind="Internal"))
    s_scr = ap(nc.dram_tensor("s_scr", [TQ, D], BF16, kind=dscr))
    scr_w1 = ap(nc.dram_tensor("scr_w1", [18, NFC, 128, NDC * 128], BF16, kind="Internal"))
    scr_w2 = ap(nc.dram_tensor("scr_w2", [18, 8, 128, NFC * 256], BF16, kind="Internal"))
    out = ap(nc.dram_tensor("out", [TQ, D], F32, kind="ExternalOutput"))
    if dbg:
        dbg_kt = ap(nc.dram_tensor("dbg_kt", [128, 4, TKV], BF16, kind="ExternalOutput"))
        dbg_v = ap(nc.dram_tensor("dbg_v", [128, 32, DK], BF16, kind="ExternalOutput"))
        dbg_qs = ap(nc.dram_tensor("dbg_qs", [128, 4, TQ], BF16, kind="ExternalOutput"))
        dbg_gat = ap(nc.dram_tensor("dbg_gat", [128, 16, NEXP], F32, kind="ExternalOutput"))
        dbg_acc = ap(nc.dram_tensor("dbg_acc", [128, 16, D], F32, kind="ExternalOutput"))

    def DMA(q, dst, src, deps=()):
        return S.dma(q, lambda e, a=dst, b=src: e.dma_start(a, b), deps)

    def DMT(q, dst, src, deps=()):
        return S.dma(q, lambda e, a=dst, b=src: e.dma_start_transpose(a, b), deps)

    with ExitStack() as CTX:
        sems = {}
        for c in Sched.CHANNELS:
            nrot = Sched.NROT if c.startswith("d") else 1
            for i in range(nrot):
                sems[(c, i)] = CTX.enter_context(nc.semaphore("ch_%s_%d" % (c, i)))
        A = lambda name, shape, dt=F32: CTX.enter_context(nc.sbuf_tensor(name, shape, dt))

        ident = A("ident_sb", [128, 128], BF16)
        bqs = A("bqs_sb", [128, 4])
        bks = A("bks_sb", [128, 4])
        bvrow = A("bvrow_sb", [1, DK], BF16)
        ones1 = A("ones1_sb", [1, 128], BF16)
        sb1s = A("sb1_sb", [128, NFC, 2])
        rb1s = A("rb1_sb", [128, NFC, NEXP])
        b2ext = A("b2ext_sb", [16, D], BF16)
        sb2n = A("sb2n_sb", [2, D], BF16)
        ones2 = A("ones2_sb", [2, 128], BF16)
        centhi = A("centhi_sb", [128, NDC, NEXP], BF16)
        centlo = A("centlo_sb", [128, NDC, NEXP], BF16)
        cbias_s = A("cbias_sb", [16, 1])

        ATT = ExitStack()
        AA = lambda name, shape, dt=F32: ATT.enter_context(nc.sbuf_tensor(name, shape, dt))
        Pfold = AA("pfold_sb", [128, 4, D], BF16)
        WKT = AA("wkt_sb", [128, NDC, DK], BF16)
        WVT = AA("wvt_sb", [128, NDC, DK], BF16)
        QsumT = AA("qsumt_sb", [128, 4, TQ], BF16)
        KT = AA("kt_sb", [128, 4, TKV], BF16)
        Vsb = AA("v_sb", [128, 32, DK], BF16)

        psA = CTX.enter_context(nc.psum_tensor("psA", [128, 512], F32))
        psB = CTX.enter_context(nc.psum_tensor("psB", [128, 512], F32))
        psC = CTX.enter_context(nc.psum_tensor("psC", [128, 512], F32))
        psD = CTX.enter_context(nc.psum_tensor("psD", [128, 512], F32))
        psE = CTX.enter_context(nc.psum_tensor("psE", [128, 512], F32))
        psF = CTX.enter_context(nc.psum_tensor("psF", [128, 512], F32))
        ptr1 = CTX.enter_context(nc.psum_tensor("ptr1", [128, 1024], BF16))
        ptr2 = CTX.enter_context(nc.psum_tensor("ptr2", [128, 1024], BF16))
        psY = [psC, psD, psE, psF]

        TP = {"k": 0, "ev": [None, None]}

        def TRP(src_ap, outP, outF, id_ap, evac_maker, deps):
            k = TP["k"]
            TP["k"] += 1
            bank = [ptr1, ptr2][k % 2]
            pap = bank[0:outP, 0:outF]
            tst = S.op("T", lambda e, pap=pap, s=src_ap, ia=id_ap: e.transpose(pap, s, ia),
                       deps=list(deps) + [TP["ev"][k % 2]])
            ev = evac_maker(pap, tst)
            TP["ev"][k % 2] = ev
            return tst, ev

        P1wq = ExitStack()
        WQT = P1wq.enter_context(nc.sbuf_tensor("wqt_sb", [128, NDC, DK], BF16))
        P1a = ExitStack()
        A1 = lambda name, shape, dt=F32: P1a.enter_context(nc.sbuf_tensor(name, shape, dt))
        wstg = A1("wstg", [128, 2, D])
        wstg2 = A1("wstg2", [128, 2, D])
        wbfh = A1("wbfh", [128, 2, D], BF16)
        bqr_s = A1("bqr_s", [128, 16])
        centf = A1("centf", [128, NDC, NEXP])
        wbfh2 = A1("wbfh2", [128, 2, D], BF16)

        ld_ident = DMA("dY", ident[:, :], ident_in[:, :])
        ld_bqr = DMA("dY", bqr_s[:, :], bq_r[:, :])
        ld_bk = DMA("dY", bks[:, :], bk_r[:, :])
        ld_sb1 = DMA("dY", sb1s[:, :, :], sb1_r[:, :, :])
        ld_rb1 = DMA("dY", rb1s[:, :, :], rb1_r[:, :, :])
        ld_cb = DMA("dY", cbias_s[:, :], cbias[:, :])
        ld_centf = DMA("dY", centf[:, :, :], cent_r.rearrange("(dc p) j -> p dc j", p=128))
        ld_bv = DMA("dG", bvrow[:, :], bv_in[:, :])
        ld_rb2 = DMA("dG", b2ext[:, :], rb2[:, :])
        ld_sb2 = DMA("dG", sb2n[:, :], sb2[:, :])
        ld_centh = DMA("dG", centhi[:, :, :], cent_r.rearrange("(dc p) j -> p dc j", p=128))

        s1 = S.op("V", lambda e: e.tensor_add(bqs[:, :], bqr_s[:, 0:4], bqr_s[:, 4:8]), deps=[ld_bqr])
        s2 = S.op("V", lambda e: e.tensor_add(bqs[:, :], bqs[:, :], bqr_s[:, 8:12]), deps=[s1])
        bqs_done = S.op("V", lambda e: e.tensor_add(bqs[:, :], bqs[:, :], bqr_s[:, 12:16]), deps=[s2])
        clo = S.op("V", lambda e: e.tensor_sub(centlo[:, :, :], centf[:, :, :], centhi[:, :, :]),
                   deps=[ld_centf, ld_centh])
        ones_set = S.op("V", lambda e: e.memset(ones1[:, :], 1.0))
        ones2_set = S.op("V", lambda e: e.memset(ones2[:, :], 1.0))

        def fold_weight(src_dram, out_cvt_fn, prev_users):
            outs = []
            prevA = prev_users
            for ah in range(2):
                l0 = DMA("dG", wstg[:, :, :],
                         src_dram[256 * ah:256 * ah + 256, :].rearrange("(a p) d -> p a d", p=128),
                         deps=prevA)
                fold = l0
                addstep = None
                for r in range(1, 4):
                    lb = DMA("dG", wstg2[:, :, :],
                             src_dram[512 * r + 256 * ah:512 * r + 256 * ah + 256, :]
                             .rearrange("(a p) d -> p a d", p=128),
                             deps=[addstep] + prevA)
                    addstep = S.op("V", lambda e: e.tensor_add(wstg[:, :, :], wstg[:, :, :],
                                                              wstg2[:, :, :]),
                                   deps=[l0, lb, addstep])
                outs.append(out_cvt_fn(ah, addstep))
                prevA = [outs[-1][0]]
            return outs

        def emit_tr(srcbuf, a_local, ah, dst, dstcol_base, dep):
            evs = []
            for dc in range(NDC):
                def mk(pap, tst, d=dc, ds=dst, cb=dstcol_base):
                    return S.op("V", lambda e: e.tensor_copy(ds[:, d, cb:cb + 128], pap),
                                deps=[tst])
                tst, ev = TRP(srcbuf[:, a_local, dc * 128:(dc + 1) * 128], 128, 128,
                              ident[:, :], mk, [dep, ld_ident])
                evs.append(ev)
            return evs

        def wq_cvt(ah, fold_step):
            cvt = S.op("V", lambda e: e.tensor_copy(wbfh[:, :, :], wstg[:, :, :]), deps=[fold_step])
            evs = []
            for al in range(2):
                evs += emit_tr(wbfh, al, ah, WQT, (2 * ah + al) * 128, cvt)
            return [evs[-1], cvt] + evs

        wq_outs = fold_weight(wq, wq_cvt, [])
        wqt_done = wq_outs[-1][0]
        wq_users = [s for o in wq_outs for s in o]

        def pf_cvt(ah, fold_step):
            cvt = S.op("V", lambda e, ah=ah: e.tensor_scalar(
                Pfold[:, 2 * ah:2 * ah + 2, :], wstg[:, :, :], RSQ128, None, ALU.mult),
                deps=[fold_step])
            return [cvt]

        pf_outs = fold_weight(proj, pf_cvt, wq_users)
        cvt_pf = pf_outs[-1][0]


        kv_tr_last = {}
        prev_half_load = None
        for (wsrc, dst) in ((wk, WKT), (wv, WVT)):
            for ah in range(2):
                lh = DMA("dG", wbfh2[:, :, :],
                         wsrc[256 * ah:256 * ah + 256, :].rearrange("(a p) d -> p a d", p=128),
                         deps=[prev_half_load])
                evs = []
                for al in range(2):
                    evs += emit_tr(wbfh2, al, ah, dst, (2 * ah + al) * 128, lh)
                prev_half_load = evs[-1]
                kv_tr_last[id(dst)] = evs[-1]
        wkt_done = kv_tr_last[id(WKT)]
        wvt_done = kv_tr_last[id(WVT)]

        scrw1_done = {}
        scrw2_done = {}

        bar_snap = S.snapshot()
        bar_steps = [S.op(e, lambda eng: eng.nop(), deps=bar_snap) for e in ("T", "V", "S", "Y")]

        P1a.close()

        PHq = ExitStack()
        A2 = lambda name, shape, dt=F32: PHq.enter_context(nc.sbuf_tensor(name, shape, dt))
        xt = [A2("xt0", [128, D]), A2("xt1", [128, D])]
        stats = [A2("st0", [128, 8]), A2("st1", [128, 8])]
        hib = [A2("hib0", [128, D], BF16), A2("hib1", [128, D], BF16)]
        lob = A2("lob", [128, D], BF16)
        xfb = A2("xfb", [128, D])
        xnWq = A2("xnWq", [128, NDC, 1, 512], BF16)

        def ln_tile(bufs, src_ap, row0, b, want_lo, prevu, dst_scr, prev_lo):
            xt, stats, hib, lob, xfb = bufs
            xb = xt[b]
            hb = hib[b]
            dummy = xfb if xfb is not None else hb
            ld = DMA("dY", xb[:, :], src_ap[row0:row0 + 128, :], deps=prevu)
            st = stats[b]
            r1 = S.op("V", lambda e, xb=xb: e.reduce_sum(st[:, 0:1], xb[:, :], axis=AX.X), deps=[ld])
            sq = S.op("S", lambda e, xb=xb, dummy=dummy: e.activation(dummy[:, :], xb[:, :], AF.Square,
                                                                     accum_out=st[:, 4:5]),
                      deps=[ld] + prevu + prev_lo)
            m = S.op("V", lambda e: e.tensor_scalar(st[:, 1:2], st[:, 0:1], float(1.0 / D), None, ALU.mult),
                     deps=[r1])
            v2 = S.op("V", lambda e: e.tensor_scalar(st[:, 2:3], st[:, 4:5], float(1.0 / D), 1e-5,
                                                     ALU.mult, ALU.add), deps=[sq])
            mm = S.op("V", lambda e: e.tensor_mul(st[:, 3:4], st[:, 1:2], st[:, 1:2]), deps=[m])
            vv = S.op("V", lambda e: e.tensor_sub(st[:, 2:3], st[:, 2:3], st[:, 3:4]), deps=[v2, mm])
            lnv = S.op("S", lambda e: e.activation(st[:, 7:8], st[:, 2:3], AF.Ln), deps=[vv])
            rstd = S.op("S", lambda e: e.activation(st[:, 5:6], st[:, 7:8], AF.Exp, scale=-0.5), deps=[lnv])
            nmr = S.op("V", lambda e: e.tensor_scalar(st[:, 6:7], st[:, 1:2], st[:, 5:6], -1.0,
                                                      ALU.mult, ALU.mult), deps=[rstd])
            hi = S.op("S", lambda e, xb=xb, hb=hb: e.activation(hb[:, :], xb[:, :], AF.Identity,
                                                                bias=st[:, 6:7], scale=st[:, 5:6]),
                      deps=[nmr] + prevu)
            lo_store = None
            lo_users = []
            xf = lo = None
            if want_lo:
                xf = S.op("S", lambda e, xb=xb: e.activation(xfb[:, :], xb[:, :], AF.Identity,
                                                             bias=st[:, 6:7], scale=st[:, 5:6]),
                          deps=[nmr, sq] + prev_lo)
                lo = S.op("V", lambda e, hb=hb: e.tensor_sub(lob[:, :], xfb[:, :], hb[:, :]),
                          deps=[xf, hi] + prev_lo)
                lo_store = DMA("dY", scr_qlo[row0:row0 + 128, :], lob[:, :], deps=[lo])
                lo_users = [xf, lo, lo_store]
            hi_store = DMA("dY", dst_scr[row0:row0 + 128, :], hb[:, :], deps=[hi])
            last = [hi_store, sq, hi, r1]
            if want_lo:
                last += [xf, lo]
            return hi_store, lo_store, last, lo_users

        q_hi = []
        q_lo = []
        prev_lo = []
        prevu = [list(bar_steps), list(bar_steps)]
        qwin_users = [[], []]
        pq_ev = [None, None]
        qev_all = []
        for tc in range(4):
            for kk in range(4):
                k = tc * 4 + kk
                hs, ls, lu, lou = ln_tile((xt, stats, hib, lob, xfb), xq, 128 * k, k % 2, True,
                                          prevu[k % 2], scr_q, prev_lo)
                q_hi.append(hs)
                q_lo.append(ls)
                prevu[k % 2] = lu
                prev_lo = lou
            w = 0
            trs = []
            for dc in range(NDC):
                t = DMT("dS", xnWq[:, dc, w, :],
                        scr_q[tc * 512:(tc + 1) * 512, dc * 128:(dc + 1) * 128],
                        deps=q_hi[4 * tc:4 * tc + 4] + qwin_users[w] + bar_steps)
                trs.append(t)
            users = []
            for a in range(4):
                g = a % 2
                ps = [psA, psB][g]

                def qfn(e, a=a, w=w, ps=ps):
                    for dc in range(NDC):
                        ins = e.matmul(ps[:, :], WQT[:, dc, a * 128:(a + 1) * 128],
                                       xnWq[:, dc, w, :], start=(dc == 0), stop=(dc == NDC - 1))
                    return ins
                mmst = S.op("T", qfn, deps=trs + [wqt_done, pq_ev[g]])
                ev = S.op("S", lambda e, a=a, tc=tc, ps=ps: e.activation(
                    QsumT[:, a, tc * 512:(tc + 1) * 512], ps[:, :], AF.Identity,
                    bias=bqs[:, a:a + 1]), deps=[mmst, bqs_done])
                pq_ev[g] = ev
                qev_all.append(ev)
                users.append(mmst)
            qwin_users[w] = users

        bar_snap = S.snapshot()
        bar_steps = [S.op(e, lambda eng: eng.nop(), deps=bar_snap) for e in ("T", "V", "S", "Y")]
        PHq.close()
        P1wq.close()


        PHkv = ExitStack()
        A2 = lambda name, shape, dt=F32: PHkv.enter_context(nc.sbuf_tensor(name, shape, dt))
        xt = [A2("kxt0", [128, D]), A2("kxt1", [128, D])]
        stats = [A2("kst0", [128, 8]), A2("kst1", [128, 8])]
        hib = [A2("khib0", [128, D], BF16), A2("khib1", [128, D], BF16)]
        xnWk = A2("xnWk", [128, NDC, 1, 512], BF16)

        kv_hi = []
        prevu = [list(bar_steps), list(bar_steps)]
        prev_lo = []
        for k in range(32):
            hs, _, lu, _ = ln_tile((xt, stats, hib, None, None), xkv, 128 * k, k % 2, False,
                                   prevu[k % 2], scr_kv, prev_lo)
            kv_hi.append(hs)
            prevu[k % 2] = lu

        win_users = [bar_steps, bar_steps]
        pk_ev = [qev_all[-1], qev_all[-1]]
        pv_ev = [None, None]
        k_ev_all = []
        v_ev_all = []
        for tc in range(8):
            w = 0
            trs = []
            for dc in range(NDC):
                t = DMT("dS", xnWk[:, dc, w, :],
                        scr_kv[tc * 512:(tc + 1) * 512, dc * 128:(dc + 1) * 128],
                        deps=kv_hi[4 * tc:4 * tc + 4] + list(win_users[w]))
                trs.append(t)
            users = []
            for a in range(4):
                g = a % 2
                ps = [psA, psB][g]

                def kfn(e, a=a, w=w, ps=ps):
                    for dc in range(NDC):
                        ins = e.matmul(ps[:, :], WKT[:, dc, a * 128:(a + 1) * 128],
                                       xnWk[:, dc, w, :], start=(dc == 0), stop=(dc == NDC - 1))
                    return ins
                mmst = S.op("T", kfn, deps=trs + [wkt_done, pk_ev[g]])
                ev = S.op("S", lambda e, a=a, tc=tc, ps=ps: e.activation(
                    KT[:, a, tc * 512:(tc + 1) * 512], ps[:, :], AF.Identity,
                    bias=bks[:, a:a + 1]), deps=[mmst, ld_bk])
                pk_ev[g] = ev
                k_ev_all.append(ev)
                users.append(mmst)
            for sub in range(4):
                g = sub % 2
                ps = [psC, psD][g]
                kb = tc * 4 + sub

                def vfn(e, sub=sub, w=w, ps=ps):
                    for dc in range(NDC):
                        e.matmul(ps[:, :], xnWk[:, dc, w, sub * 128:(sub + 1) * 128],
                                 WVT[:, dc, :], start=(dc == 0), stop=False)
                    return e.matmul(ps[:, :], ones1[:, :], bvrow[:, :], start=False, stop=True)
                mmst = S.op("T", vfn, deps=trs + [wvt_done, ld_bv, ones_set, pv_ev[g]])
                ev = S.op("V", lambda e, kb=kb, ps=ps: e.tensor_copy(Vsb[:, kb, :], ps[:, :]),
                          deps=[mmst])
                pv_ev[g] = ev
                v_ev_all.append(ev)
                users.append(mmst)
            win_users[w] = users

        if dbg:
            d1 = DMA("dY", dbg_kt[:, :, :], KT[:, :, :], deps=k_ev_all)
            d2 = DMA("dY", dbg_v[:, :, :], Vsb[:, :, :], deps=v_ev_all)
            d3 = DMA("dY", dbg_qs[:, :, :], QsumT[:, :, :], deps=qev_all)
            dbg_proj = [d1, d2, d3]
        else:
            dbg_proj = []

        bar_snap = S.snapshot()
        bar_steps = [S.op(e, lambda eng: eng.nop(), deps=bar_snap) for e in ("T", "V", "S", "Y")]
        PHkv.close()


        P4 = ExitStack()
        A4 = lambda name, shape, dt=F32: P4.enter_context(nc.sbuf_tensor(name, shape, dt))
        masks = A4("masks_sb", [128, 16, 256], BF16)
        srow = A4("srow", [128, TKV])
        Pbf = A4("pbf", [128, TKV], BF16)
        PTb = A4("ptb", [128, 32, 128], BF16)
        Zbf = A4("zbf", [128, 512], BF16)
        ZTb = A4("ztb", [128, 4, 128], BF16)
        stile = [A4("stile0", [128, D], BF16), A4("stile1", [128, D], BF16)]
        smax = A4("smax", [128, 2])
        ssum = A4("ssum", [128, 2])

        ld_masks = DMA("dY", masks[:, :, :], masks_in[:, :, :], deps=bar_steps)

        psS_ev = [None, None]
        s_stores = [None, None]
        prev_srow_reader = None
        prev_pbf_reader = None
        prev_rcp = None
        prev_av = None
        prev_zread = None
        psZ_ev = None
        psSP_ev = [None, None]
        s_store_list = []

        for i in range(16):
            nkeys = 256 * (i + 1)
            nkb = nkeys // 128
            blocks = []
            off = 0
            while off < nkeys:
                ln_ = min(512, nkeys - off)
                blocks.append((off, ln_))
                off += ln_
            ev_blocks = []
            for bi, (off, ln_) in enumerate(blocks):
                g = bi % 2
                ps = [psA, psB][g]

                def scfn(e, i=i, off=off, ln_=ln_, ps=ps):
                    for a in range(4):
                        ins = e.matmul(ps[:, 0:ln_], QsumT[:, a, i * 128:(i + 1) * 128],
                                       KT[:, a, off:off + ln_], start=(a == 0), stop=(a == 3))
                    return ins
                tc_max = (off + ln_ - 1) // 512
                mmst = S.op("T", scfn,
                            deps=[psS_ev[g]] + qev_all[:4 * (i // 4 + 1)]
                            + k_ev_all[:4 * (tc_max + 1)] + dbg_proj)
                ev = S.op("V", lambda e, off=off, ln_=ln_, ps=ps: e.tensor_copy(
                    srow[:, off:off + ln_], ps[:, 0:ln_]),
                    deps=[mmst, prev_srow_reader])
                psS_ev[g] = ev
                ev_blocks.append(ev)
            mk = S.op("V", lambda e, i=i, nk=nkeys: e.tensor_add(
                srow[:, nk - 256:nk], srow[:, nk - 256:nk], masks[:, i, :]),
                deps=[ev_blocks[-1], ld_masks])
            nmx = S.op("V", lambda e, nk=nkeys: e.reduce_max(
                smax[:, 0:1], srow[:, 0:nk], axis=AX.X, negate=True),
                deps=[mk, prev_pbf_reader] + ev_blocks)
            exps = S.op("S", lambda e, nk=nkeys: e.activation(
                Pbf[:, 0:nk], srow[:, 0:nk], AF.Exp, bias=smax[:, 0:1],
                accum_out=ssum[:, 0:1]),
                deps=[nmx, prev_pbf_reader, prev_rcp])
            rcp = S.op("V", lambda e: e.reciprocal(ssum[:, 1:2], ssum[:, 0:1]),
                       deps=[exps, prev_zread])
            prev_rcp = rcp
            pt_evs = []
            last_pt_t = None
            for kb in range(nkb):
                def mk(pap, tst, kb=kb):
                    return S.op("V", lambda e: e.tensor_copy(PTb[:, kb, :], pap),
                                deps=[tst, prev_av])
                tst, evt = TRP(Pbf[:, kb * 128:(kb + 1) * 128], 128, 128, ident[:, :],
                               mk, [exps, prev_av])
                pt_evs.append(evt)
                last_pt_t = tst

            def avfn(e, nkb=nkb):
                for kb in range(nkb):
                    ins = e.matmul(psC[:, :], PTb[:, kb, :], Vsb[:, kb, :],
                                   start=(kb == 0), stop=(kb == nkb - 1))
                return ins
            av = S.op("T", avfn,
                      deps=pt_evs + v_ev_all[:4 * ((nkeys - 1) // 512 + 1)] + [psZ_ev])
            prev_av = av
            prev_pbf_reader = last_pt_t
            prev_srow_reader = exps
            zev = S.op("V", lambda e: e.tensor_scalar(
                Zbf[:, :], psC[:, :], ssum[:, 1:2], None, ALU.mult),
                deps=[av, rcp, prev_zread])
            psZ_ev = zev
            zt_evs = []
            zt_ts = []
            for c in range(4):
                def mk(pap, tst, c=c):
                    return S.op("V", lambda e: e.tensor_copy(ZTb[:, c, :], pap),
                                deps=[tst, prev_zread])
                tst, evt = TRP(Zbf[:, c * 128:(c + 1) * 128], 128, 128, ident[:, :],
                               mk, [zev, prev_zread])
                zt_evs.append(evt)
                zt_ts.append(tst)
            prev_zread = zt_ts[-1]
            sb = i % 2
            sp_evs = []
            for dg in range(4):
                g = dg % 2
                ps = [psE, psF][g]

                def spfn(e, dg=dg, ps=ps):
                    for c in range(4):
                        ins = e.matmul(ps[:, :], ZTb[:, c, :], Pfold[:, c, dg * 512:(dg + 1) * 512],
                                       start=(c == 0), stop=(c == 3))
                    return ins
                mmst = S.op("T", spfn, deps=zt_evs + [cvt_pf, psSP_ev[g]])
                ev = S.op("S", lambda e, dg=dg, ps=ps, sb=sb: e.activation(
                    stile[sb][:, dg * 512:(dg + 1) * 512], ps[:, :], AF.Copy),
                    deps=[mmst, s_stores[sb]])
                psSP_ev[g] = ev
                sp_evs.append(ev)
            sst = DMA("dY", s_scr[i * 128:(i + 1) * 128, :], stile[sb][:, :], deps=sp_evs)
            s_stores[sb] = sst
            s_store_list.append(sst)

        bar_snap = S.snapshot()
        bar_steps = [S.op(e, lambda eng: eng.nop(), deps=bar_snap) for e in ("T", "V", "S", "Y")]
        P4.close()
        ATT.close()

        P5 = ExitStack()
        A5 = lambda name, shape, dt=F32: P5.enter_context(nc.sbuf_tensor(name, shape, dt))
        xnqTh = A5("xnqTh", [128, NDC, HALF], BF16)
        lowin = A5("lowin", [128, 2, HALF], BF16)
        acc = A5("acc", [128, 8, D])
        hT = A5("hT", [128, NFC, HALF], BF16)
        w2b = A5("w2b", [128, NFC, 2, 512], BF16)
        w1b = [A5("w1b0", [128, NDC, 128], BF16), A5("w1b1", [128, NDC, 128], BF16)]
        sct = A5("sct", [16, HALF])
        schi = A5("schi", [16, 128], BF16)
        sclo = A5("sclo", [16, 128], BF16)
        schiT = A5("schiT", [128, 16])
        sc32 = A5("sc32", [128, 16])
        rnk = A5("rnk", [128, 16])
        mskb = A5("mskb", [128, 16])
        graw = A5("graw", [128, 16])
        den = A5("den", [128, 2])
        gating = A5("gating", [128, 8, NEXP])
        gbf = A5("gbf", [128, NEXP], BF16)
        gT = A5("gt", [16, 8, 128], BF16)
        xnqrl = [A5("xnqrl0", [128, D], BF16), A5("xnqrl1", [128, D], BF16)]
        xlorl = A5("xlorl", [128, D], BF16)
        xqrl = A5("xqrl", [128, D])
        srl = A5("srl", [128, D], BF16)
        gsil = A5("gsil", [128, D], BF16)
        fin = A5("fin", [128, D])

        b0 = bar_steps[0]
        U = {
            "xnqTh": list(bar_steps), "lowin": [b0, b0], "sct": [b0],
            "schi": b0, "sclo": b0, "schiT": b0, "sc32": b0, "rnk": b0,
            "mskb": b0, "graw": b0, "den": b0, "gbf": b0,
            "gat": [b0] * 8, "gT": [b0] * 8,
            "acc": {}, "hT": [b0], "w2b": [[b0], [b0]], "w1b": [[b0], [b0]],
            "xnqrl": [b0, b0], "xlorl": b0, "xqrl": b0,
            "srl": b0, "gsil": b0, "fin": b0,
            "psH": [bar_steps[1], bar_steps[1]], "psY": [bar_steps[1]] * 4,
            "psR": [bar_steps[1], bar_steps[1]],
        }
        out_stores = []

        for half in range(2):
            hoff = half * HALF
            trh = []
            for dc in range(NDC):
                t = DMT("dS", xnqTh[:, dc, :],
                        scr_q[hoff:hoff + HALF, dc * 128:(dc + 1) * 128],
                        deps=q_hi + U["xnqTh"])
                trh.append(t)

            rt_steps = []
            for dc in range(NDC):
                tl = DMT("dS", lowin[:, dc % 2, :],
                         scr_qlo[hoff:hoff + HALF, dc * 128:(dc + 1) * 128],
                         deps=q_lo + [U["lowin"][dc % 2]])

                def rtfn(e, dc=dc):
                    for sub in range(2):
                        ps = [psE, psF][sub]
                        e.matmul(ps[0:16, :], centhi[:, dc, :],
                                 xnqTh[:, dc, sub * 512:(sub + 1) * 512],
                                 start=(dc == 0), stop=False)
                        e.matmul(ps[0:16, :], centlo[:, dc, :],
                                 xnqTh[:, dc, sub * 512:(sub + 1) * 512],
                                 start=False, stop=False)
                        ins = e.matmul(ps[0:16, :], centhi[:, dc, :],
                                       lowin[:, dc % 2, sub * 512:(sub + 1) * 512],
                                       start=False, stop=(dc == NDC - 1))
                    return ins
                st = S.op("T", rtfn, deps=[trh[dc], tl, clo, ld_centh] + U["psR"])
                U["lowin"][dc % 2] = st
                rt_steps.append(st)
            rt_evs = []
            for sub in range(2):
                ps = [psE, psF][sub]
                ev = S.op("S", lambda e, sub=sub, ps=ps: e.activation(
                    sct[:, sub * 512:(sub + 1) * 512], ps[0:16, :], AF.Sigmoid),
                    deps=[rt_steps[-1]] + U["sct"])
                rt_evs.append(ev)
            badd = S.op("V", lambda e: e.tensor_scalar_add(sct[:, :], sct[:, :], cbias_s[:, 0:1]),
                        deps=rt_evs + [ld_cb])
            U["psY"][2] = rt_evs[0]
            U["psY"][3] = rt_evs[1]

            gat_steps = []
            gT_steps = []
            c2 = None
            for tt in range(8):
                c1 = S.op("V", lambda e, tt=tt: e.tensor_copy(schi[:, :], sct[:, tt * 128:(tt + 1) * 128]),
                          deps=[badd, U["schi"], c2])
                c2 = S.op("V", lambda e, tt=tt: e.tensor_sub(
                    sclo[:, :], sct[:, tt * 128:(tt + 1) * 128], schi[:, :]),
                    deps=[c1, U["sclo"]])
                def mk1(pap, tst):
                    return S.op("V", lambda e, pap=pap: e.tensor_copy(schiT[:, :], pap),
                                deps=[tst, U["schiT"]])
                t1, e1 = TRP(schi[:, :], 128, 16, ident[0:16, 0:16], mk1, [c1])

                def mk2(pap, tst):
                    return S.op("V", lambda e, pap=pap: e.tensor_add(sc32[:, :], schiT[:, :], pap),
                                deps=[tst, e1, U["sc32"]])
                t2, e2 = TRP(sclo[:, :], 128, 16, ident[0:16, 0:16], mk2, [c2])
                U["schiT"] = e2
                U["schi"] = t1
                U["sclo"] = t2
                prev = S.op("V", lambda e: e.tensor_scalar(rnk[:, :], sc32[:, :], sc32[:, 0:1],
                                                           None, ALU.is_lt),
                            deps=[e2, U["rnk"]])
                for j in range(1, NEXP):
                    prev = S.op("V", lambda e, j=j: e.scalar_tensor_tensor(
                        rnk[:, :], sc32[:, :], sc32[:, j:j + 1], rnk[:, :], ALU.is_lt, ALU.add),
                        deps=[prev])
                mk1 = S.op("V", lambda e: e.tensor_scalar(mskb[:, :], rnk[:, :], 3.5, None, ALU.is_lt),
                           deps=[prev, U["mskb"]])
                gr = S.op("V", lambda e: e.tensor_mul(graw[:, :], sc32[:, :], mskb[:, :]),
                          deps=[mk1, U["graw"]])
                U["sc32"] = gr
                U["rnk"] = mk1
                ds = S.op("V", lambda e: e.reduce_sum(den[:, 0:1], graw[:, :], axis=AX.X),
                          deps=[gr, U["den"]])
                dm = S.op("V", lambda e: e.tensor_scalar_max(den[:, 1:2], den[:, 0:1], 1e-8),
                          deps=[ds])
                dr = S.op("V", lambda e: e.reciprocal(den[:, 0:1], den[:, 1:2]), deps=[dm])
                U["den"] = dr
                ga = S.op("V", lambda e, tt=tt: e.tensor_scalar(
                    gating[:, tt, :], graw[:, :], den[:, 0:1], None, ALU.mult),
                    deps=[dr, U["gat"][tt]])
                U["mskb"] = ga
                U["graw"] = ga
                gb = S.op("V", lambda e, tt=tt: e.tensor_copy(gbf[:, :], gating[:, tt, :]),
                          deps=[ga, U["gbf"]])
                def mk3(pap, tst, tt=tt):
                    return S.op("V", lambda e, pap=pap: e.tensor_copy(gT[:, tt, :], pap),
                                deps=[tst, U["gT"][tt]])
                t3, e3 = TRP(gbf[:, :], 16, 128, ident[:, :], mk3, [gb])
                U["gbf"] = t3
                gat_steps.append(ga)
                gT_steps.append(e3)
                U["gat"][tt] = ga
                U["gT"][tt] = e3

            U["sct"] = [c2]
            if dbg:
                dgat = DMA("dY", dbg_gat[:, half * 8:(half + 1) * 8, :], gating[:, :, :],
                           deps=gat_steps)
                for tt in range(8):
                    U["gat"][tt] = dgat

            def expert_pass(jconv, b1s, b1col, mode, gcol):
                w1_src = sw1[jconv - 16] if jconv >= 16 else rw1[jconv]
                w2_src = sw2[jconv - 16] if jconv >= 16 else rw2[jconv]
                h_evs = []
                for fs in range(NFC):
                    wb = w1b[fs % 2]
                    if half == 0:
                        w1f = w1_src[:, fs * 128:(fs + 1) * 128].rearrange(
                            "(dc p) f -> p dc f", p=128)
                        dma1 = DMA("dG", wb[:, 0:8, :], w1f[:, 0:8, :],
                                   deps=U["w1b"][fs % 2])
                        dma2 = DMA("dG", wb[:, 8:16, :], w1f[:, 8:16, :],
                                   deps=U["w1b"][fs % 2])
                        scrw = DMA("dS",
                                   scr_w1[jconv, fs].rearrange("p (dc f) -> p dc f", f=128),
                                   wb[:, :, :], deps=[dma1, dma2])
                        scrw1_done[(jconv, fs)] = scrw
                    else:
                        w1sl = scr_w1[jconv, fs].rearrange("p (dc f) -> p dc f", f=128)
                        q = "dG" if fs % 2 == 0 else "dY"
                        dma1 = DMA(q, wb[:, :, :], w1sl,
                                   deps=U["w1b"][fs % 2] + [scrw1_done[(jconv, fs)]])
                        dma2 = None
                        scrw = None
                    musers = [] if scrw is None else [scrw]
                    for sub in range(2):
                        g = (fs * 2 + sub) % 2
                        ps = [psA, psB][g]

                        def hfn(e, wb=wb, sub=sub, ps=ps):
                            for dc in range(NDC):
                                ins = e.matmul(ps[:, :], wb[:, dc, :],
                                               xnqTh[:, dc, sub * 512:(sub + 1) * 512],
                                               start=(dc == 0), stop=(dc == NDC - 1))
                            return ins
                        mmst = S.op("T", hfn,
                                    deps=[dma1, dma2, U["psH"][g]] + trh + U["hT"])
                        ev = S.op("S", lambda e, fs=fs, sub=sub, ps=ps, b1s=b1s, b1col=b1col: e.activation(
                            hT[:, fs, sub * 512:(sub + 1) * 512], ps[:, :], AF.Relu,
                            bias=b1s[:, fs, b1col:b1col + 1]),
                            deps=[mmst, ld_sb1, ld_rb1])
                        U["psH"][g] = ev
                        h_evs.append(ev)
                        musers.append(mmst)
                    U["w1b"][fs % 2] = musers
                y_last_mm = None
                for qd in range(4):
                    qb = qd % 2
                    if half == 0:
                        w2f = w2_src[:, qd * 512:(qd + 1) * 512].rearrange(
                            "(fc p) d -> p fc d", p=128)
                        wd1 = DMA("dG", w2b[:, 0:6, qb, :], w2f[:, 0:6, :],
                                  deps=U["w2b"][qb])
                        wd2 = DMA("dG", w2b[:, 6:11, qb, :], w2f[:, 6:11, :],
                                  deps=U["w2b"][qb])
                        sc1 = DMA("dS",
                                  scr_w2[jconv, 2 * qd].rearrange("p (fc x) -> p fc x", x=256),
                                  w2b[:, :, qb, 0:256], deps=[wd1, wd2])
                        sc2 = DMA("dY",
                                  scr_w2[jconv, 2 * qd + 1].rearrange("p (fc x) -> p fc x", x=256),
                                  w2b[:, :, qb, 256:512], deps=[wd1, wd2])
                        scrw2_done[(jconv, 2 * qd)] = sc1
                        scrw2_done[(jconv, 2 * qd + 1)] = sc2
                        yusers = [sc1, sc2]
                    else:
                        wd1 = DMA("dG", w2b[:, :, qb, 0:256],
                                  scr_w2[jconv, 2 * qd].rearrange("p (fc x) -> p fc x", x=256),
                                  deps=U["w2b"][qb] + [scrw2_done[(jconv, 2 * qd)]])
                        wd2 = DMA("dY", w2b[:, :, qb, 256:512],
                                  scr_w2[jconv, 2 * qd + 1].rearrange("p (fc x) -> p fc x", x=256),
                                  deps=U["w2b"][qb] + [scrw2_done[(jconv, 2 * qd + 1)]])
                        yusers = []
                    for tt in range(8):
                        bk = (qd * 8 + tt) % 4
                        ps = psY[bk]
                        extras = (mode == "init")
                        if extras:
                            rl1 = DMA("dY", xnqrl[tt % 2][:, :],
                                      scr_q[hoff + tt * 128:hoff + (tt + 1) * 128, :],
                                      deps=[U["xnqrl"][tt % 2]] + q_hi)
                            rl2 = DMA("dY", xlorl[:, :],
                                      scr_qlo[hoff + tt * 128:hoff + (tt + 1) * 128, :],
                                      deps=[U["xlorl"]] + q_lo)

                        def yfn(e, qd=qd, qb=qb, tt=tt, ps=ps, extras=extras):
                            for fs in range(NFC):
                                ins = e.matmul(ps[:, :], hT[:, fs, tt * 128:(tt + 1) * 128],
                                               w2b[:, fs, qb, :],
                                               start=(fs == 0), stop=(fs == NFC - 1 and not extras))
                            if extras:
                                e.matmul(ps[:, :], gT[:, tt, :], b2ext[:, qd * 512:(qd + 1) * 512],
                                         start=False, stop=False)
                                e.matmul(ps[:, :], ones2[:, :], sb2n[:, qd * 512:(qd + 1) * 512],
                                         start=False, stop=False)
                                e.matmul(ps[:, :], ident[:, :], xnqrl[tt % 2][:, qd * 512:(qd + 1) * 512],
                                         start=False, stop=False)
                                ins = e.matmul(ps[:, :], ident[:, :], xlorl[:, qd * 512:(qd + 1) * 512],
                                               start=False, stop=True)
                            return ins
                        deps = h_evs + [wd1, wd2, U["psY"][bk]]
                        if extras:
                            deps += [gT_steps[tt], ld_rb2, ld_sb2, ones2_set, rl1, rl2, ld_ident]
                        mmst = S.op("T", yfn, deps=deps)
                        if extras:
                            U["xnqrl"][tt % 2] = mmst
                            U["xlorl"] = mmst
                        accdep = U["acc"].get((tt, qd))
                        if mode == "init":
                            ev = S.op("V", lambda e, tt=tt, qd=qd, ps=ps: e.tensor_copy(
                                acc[:, tt, qd * 512:(qd + 1) * 512], ps[:, :]),
                                deps=[mmst, accdep])
                        elif mode == "add":
                            ev = S.op("V", lambda e, tt=tt, qd=qd, ps=ps: e.scalar_tensor_tensor(
                                acc[:, tt, qd * 512:(qd + 1) * 512], ps[:, :], 1.0,
                                acc[:, tt, qd * 512:(qd + 1) * 512], ALU.mult, ALU.add),
                                deps=[mmst, accdep])
                        else:
                            ev = S.op("V", lambda e, tt=tt, qd=qd, ps=ps, gcol=gcol: e.scalar_tensor_tensor(
                                acc[:, tt, qd * 512:(qd + 1) * 512], ps[:, :],
                                gating[:, tt, gcol:gcol + 1],
                                acc[:, tt, qd * 512:(qd + 1) * 512], ALU.mult, ALU.add),
                                deps=[mmst, accdep, gat_steps[tt]])
                        U["acc"][(tt, qd)] = ev
                        U["psY"][bk] = ev
                        yusers.append(mmst)
                        y_last_mm = mmst
                    U["w2b"][qb] = yusers
                U["hT"] = [y_last_mm]
                return y_last_mm

            init_last = expert_pass(16, sb1s, 0, "init", None)
            for tt in range(8):
                U["gT"][tt] = init_last
            expert_pass(17, sb1s, 1, "add", None)
            for j in range(NEXP):
                expert_pass(j, rb1s, j, "gated", j)
            U["psR"] = [U["psY"][2], U["psY"][3]]

            if dbg:
                dacc = DMA("dY", dbg_acc[:, half * 8:(half + 1) * 8, :], acc[:, :, :],
                           deps=[U["acc"][(tt, qd)] for tt in range(8) for qd in range(4)])
                dbg_acc_dep = [dacc]
            else:
                dbg_acc_dep = []

            for tt in range(8):
                acc_done = [U["acc"][(tt, qd)] for qd in range(4)] + dbg_acc_dep
                sil = S.op("S", lambda e, tt=tt: e.activation(gsil[:, :], acc[:, tt, :], AF.Silu),
                           deps=acc_done + [U["gsil"]])
                srload = DMA("dY", srl[:, :], s_scr[hoff + tt * 128:hoff + (tt + 1) * 128, :],
                             deps=[s_store_list[half * 8 + tt], U["srl"]])
                xqload = DMA("dY", xqrl[:, :], xq[hoff + tt * 128:hoff + (tt + 1) * 128, :],
                             deps=[U["xqrl"]])
                mu = S.op("V", lambda e: e.tensor_mul(fin[:, :], srl[:, :], gsil[:, :]),
                          deps=[sil, srload, U["fin"]])
                ad = S.op("V", lambda e: e.tensor_add(xqrl[:, :], fin[:, :], xqrl[:, :]),
                          deps=[mu, xqload])
                ost = DMA("dY", out[hoff + tt * 128:hoff + (tt + 1) * 128, :], xqrl[:, :],
                          deps=[ad])
                out_stores.append(ost)
                U["gsil"] = mu
                U["fin"] = ad
                U["srl"] = mu
                U["xqrl"] = ost
                for qd in range(4):
                    U["acc"][(tt, qd)] = sil

            U["xnqTh"] = [U["hT"][0], rt_steps[-1]]

        S.op("Y", lambda e: e.nop(), deps=out_stores)

        S.emit(nc, sems)
        P5.close()
    return nc


_PROG_CACHE = {}


def _get_prog(dbg=0, serialize=False):
    key = (dbg, serialize)
    if key not in _PROG_CACHE:
        _PROG_CACHE[key] = build_program(dbg=dbg, serialize=serialize)
    return _PROG_CACHE[key]


def make_in_maps(inputs, dbg=0):
    f32 = np.float32
    g = lambda k: np.ascontiguousarray(np.asarray(inputs[k], dtype=f32))
    x = g("L0_attn_x")
    WQ = g("L0_attn_WQ"); bQ = g("L0_attn_bQ")
    WK = g("L0_attn_WK"); bK = g("L0_attn_bK")
    WV = g("L0_attn_WV"); bV = g("L0_attn_bV")
    PJ = g("L0_attn_proj")
    sw1 = g("L0_ffn_shared_w1"); sb1 = g("L0_ffn_shared_b1")
    sw2 = g("L0_ffn_shared_w2"); sb2 = g("L0_ffn_shared_b2")
    rw1 = g("L0_ffn_routed_w1"); rb1 = g("L0_ffn_routed_b1")
    rw2 = g("L0_ffn_routed_w2"); rb2 = g("L0_ffn_routed_b2")
    cent = g("L0_ffn_centroids"); cb = g("L0_ffn_bias")

    common = {
        "ident": np.eye(128, dtype=BF),
        "wq": WQ,
        "bq_r": np.ascontiguousarray(bQ.reshape(16, 128).T),
        "wk": WK,
        "bk_r": np.ascontiguousarray(bK.reshape(4, 128).T),
        "wv": WV,
        "bv": bV.reshape(1, DK),
        "proj": PJ,
        "sw1": sw1,
        "sb1_r": np.ascontiguousarray(sb1.reshape(2, NFC, 128).transpose(2, 1, 0)),
        "sw2": sw2,
        "sb2": sb2,
        "rw1": rw1,
        "rb1_r": np.ascontiguousarray(rb1.reshape(NEXP, NFC, 128).transpose(2, 1, 0)),
        "rw2": rw2,
        "rb2": rb2,
        "cent_r": np.ascontiguousarray(cent.T),
        "cbias": cb.reshape(NEXP, 1),
    }
    masks = [build_masks(0), build_masks(1)]
    in_maps = []
    for core in range(8):
        b, p = core // 2, core % 2
        J = own_tile_ids(p)
        xq_host = np.ascontiguousarray(
            np.concatenate([x[b, 128 * j:128 * (j + 1), :] for j in J], axis=0))
        m = dict(common)
        m["xkv"] = x[b]
        m["xq"] = xq_host
        m["masks"] = masks[p]
        in_maps.append(m)
    return in_maps


def assemble(results):
    y = np.empty((4, TKV, D), dtype=np.float32)
    for core in range(8):
        b, p = core // 2, core % 2
        J = own_tile_ids(p)
        o = results[core]["out"]
        for i, j in enumerate(J):
            y[b, 128 * j:128 * (j + 1), :] = o[128 * i:128 * (i + 1), :]
    return y


def kernel(**inputs):
    nc = _get_prog(dbg=0)
    in_maps = make_in_maps(inputs)
    res = run_bass_kernel_spmd(nc, in_maps, core_ids=list(range(8)))
    return assemble(res.results)


# version 11
# speedup vs baseline: 1.8465x; 1.0453x over previous
import numpy as np
import ml_dtypes
from contextlib import ExitStack

import concourse.bass as bass
import concourse.mybir as mybir
from concourse.bass_utils import run_bass_kernel_spmd

F32 = mybir.dt.float32
BF16 = mybir.dt.bfloat16
AF = mybir.ActivationFunctionType
ALU = mybir.AluOpType
AX = mybir.AxisListType

BF = ml_dtypes.bfloat16
NEG = np.float32(-1e30)

D = 2048
DK = 512
TQ = 2048
TKV = 4096
DFF = 1408
NFC = 11
NDC = 16
NEXP = 16
HALF = 1024
RSQ128 = float(1.0 / np.sqrt(128.0))


def own_tile_ids(parity):
    lo = [2 * i + parity for i in range(8)]
    hi = [31 - 2 * i - parity for i in range(8)]
    return sorted(lo + hi)


def build_masks(parity):
    J = own_tile_ids(parity)
    m = np.zeros((128, 16, 256), dtype=np.float32)
    for i in range(16):
        q = 128 * J[i] + np.arange(128)[:, None]
        k = 256 * i + np.arange(256)[None, :]
        m[:, i, :] = np.where(k <= q, 0.0, NEG)
    return m.astype(BF)


class Step:
    __slots__ = ("idx", "ch", "fn", "deps", "amt", "val", "key")

    def __init__(self, idx, ch, fn, deps, amt, key):
        self.idx, self.ch, self.fn, self.deps, self.amt = idx, ch, fn, deps, amt
        self.val = None
        self.key = key


class Sched:
    ENGS = ("T", "V", "S", "G", "Y")
    CHANNELS = ("T", "V", "S", "G", "Y", "dY", "dS", "dG", "dGc")
    QUEUE_OF = {"dY": "Y", "dS": "S", "dG": "G", "dGc": "G"}
    NOBAR = ("dGc",)
    NROT = 16

    def __init__(self, serialize=False):
        self.steps = []
        self.last = {}
        self.rot = {c: 0 for c in self.CHANNELS}
        self.serialize = serialize

    def _add(self, ch, fn, deps, amt):
        deps = [d for d in deps if d is not None]
        if self.serialize and self.steps:
            deps = deps + [self.steps[-1]]
        if amt == 16:
            key = (ch, self.rot[ch] % self.NROT)
            self.rot[ch] += 1
        else:
            key = (ch, 0)
        st = Step(len(self.steps), ch, fn, deps, amt, key)
        self.steps.append(st)
        self.last[key] = st
        return st

    def op(self, eng, fn, deps=()):
        return self._add(eng, fn, list(deps), 1)

    def dma(self, queue, fn, deps=()):
        return self._add(queue, fn, list(deps), 16)

    def snapshot(self):
        return [s for k, s in self.last.items() if k[0] not in self.NOBAR]

    def emit(self, nc, sems):
        referenced = set()
        for s in self.steps:
            for d in s.deps:
                referenced.add(d.idx)
        counts = {}
        for s in self.steps:
            if s.amt == 16 or s.idx in referenced:
                counts[s.key] = counts.get(s.key, 0) + s.amt
                s.val = counts[s.key]
        owner = {c: self.QUEUE_OF.get(c, c) for c in self.CHANNELS}

        with nc.Block() as blk:
            def emit_for(name, eng):
                water = {}
                for s in self.steps:
                    if owner[s.ch] != name:
                        continue
                    for d in s.deps:
                        if s.ch == "T" and d.ch == "T":
                            continue
                        if water.get(d.key, 0) < d.val:
                            eng.wait_ge(sems[d.key], d.val)
                            water[d.key] = d.val
                    if s.amt == 16 and s.val > 16 and water.get(s.key, 0) < s.val - 16:
                        eng.wait_ge(sems[s.key], s.val - 16)
                        water[s.key] = s.val - 16
                    instr = s.fn(eng)
                    if s.val is not None:
                        instr.then_inc(sems[s.key], s.amt)

            @blk.tensor
            def _(t):
                emit_for("T", t)

            @blk.vector
            def _(v):
                emit_for("V", v)

            @blk.scalar
            def _(s):
                emit_for("S", s)

            @blk.gpsimd
            def _(g):
                emit_for("G", g)

            @blk.sync
            def _(y):
                emit_for("Y", y)


def build_program(dbg=0, serialize=False):
    nc = bass.Bass("TRN2", target_bir_lowering=False)
    S = Sched(serialize=serialize)

    ap = lambda t: t.ap()
    xkv = ap(nc.dram_tensor("xkv", [TKV, D], F32, kind="ExternalInput"))
    xq = ap(nc.dram_tensor("xq", [TQ, D], F32, kind="ExternalInput"))
    masks_in = ap(nc.dram_tensor("masks", [128, 16, 256], BF16, kind="ExternalInput"))
    ident_in = ap(nc.dram_tensor("ident", [128, 128], BF16, kind="ExternalInput"))
    wq = ap(nc.dram_tensor("wq", [D, D], F32, kind="ExternalInput"))
    bq_r = ap(nc.dram_tensor("bq_r", [128, 16], F32, kind="ExternalInput"))
    wk = ap(nc.dram_tensor("wk", [DK, D], F32, kind="ExternalInput"))
    bk_r = ap(nc.dram_tensor("bk_r", [128, 4], F32, kind="ExternalInput"))
    wv = ap(nc.dram_tensor("wv", [DK, D], F32, kind="ExternalInput"))
    bv_in = ap(nc.dram_tensor("bv", [1, DK], F32, kind="ExternalInput"))
    proj = ap(nc.dram_tensor("proj", [D, D], F32, kind="ExternalInput"))
    sw1 = ap(nc.dram_tensor("sw1", [2, D, DFF], F32, kind="ExternalInput"))
    sb1_r = ap(nc.dram_tensor("sb1_r", [128, NFC, 2], F32, kind="ExternalInput"))
    sw2 = ap(nc.dram_tensor("sw2", [2, DFF, D], F32, kind="ExternalInput"))
    sb2 = ap(nc.dram_tensor("sb2", [2, D], F32, kind="ExternalInput"))
    rw1 = ap(nc.dram_tensor("rw1", [NEXP, D, DFF], F32, kind="ExternalInput"))
    rb1_r = ap(nc.dram_tensor("rb1_r", [128, NFC, NEXP], F32, kind="ExternalInput"))
    rw2 = ap(nc.dram_tensor("rw2", [NEXP, DFF, D], F32, kind="ExternalInput"))
    rb2 = ap(nc.dram_tensor("rb2", [NEXP, D], F32, kind="ExternalInput"))
    cent_r = ap(nc.dram_tensor("cent_r", [D, NEXP], F32, kind="ExternalInput"))
    cbias = ap(nc.dram_tensor("cbias", [NEXP, 1], F32, kind="ExternalInput"))

    dscr = "ExternalOutput" if dbg else "Internal"
    scr_kv = ap(nc.dram_tensor("scr_kv", [TKV, D], BF16, kind=dscr))
    scr_q = ap(nc.dram_tensor("scr_q", [TQ, D], BF16, kind=dscr))
    scr_qlo = ap(nc.dram_tensor("scr_qlo", [TQ, D], BF16, kind="Internal"))
    s_scr = ap(nc.dram_tensor("s_scr", [TQ, D], BF16, kind=dscr))
    scr_w1 = ap(nc.dram_tensor("scr_w1", [18, NFC, 128, NDC * 128], BF16, kind="Internal"))
    scr_w2 = ap(nc.dram_tensor("scr_w2", [18, 8, 128, NFC * 256], BF16, kind="Internal"))
    out = ap(nc.dram_tensor("out", [TQ, D], F32, kind="ExternalOutput"))
    if dbg:
        dbg_kt = ap(nc.dram_tensor("dbg_kt", [128, 4, TKV], BF16, kind="ExternalOutput"))
        dbg_v = ap(nc.dram_tensor("dbg_v", [128, 32, DK], BF16, kind="ExternalOutput"))
        dbg_qs = ap(nc.dram_tensor("dbg_qs", [128, 4, TQ], BF16, kind="ExternalOutput"))
        dbg_gat = ap(nc.dram_tensor("dbg_gat", [128, 16, NEXP], F32, kind="ExternalOutput"))
        dbg_acc = ap(nc.dram_tensor("dbg_acc", [128, 16, D], F32, kind="ExternalOutput"))

    def DMA(q, dst, src, deps=()):
        return S.dma(q, lambda e, a=dst, b=src: e.dma_start(a, b), deps)

    def DMT(q, dst, src, deps=()):
        return S.dma(q, lambda e, a=dst, b=src: e.dma_start_transpose(a, b), deps)

    with ExitStack() as CTX:
        sems = {}
        for c in Sched.CHANNELS:
            nrot = Sched.NROT if c.startswith("d") else 1
            for i in range(nrot):
                sems[(c, i)] = CTX.enter_context(nc.semaphore("ch_%s_%d" % (c, i)))
        A = lambda name, shape, dt=F32: CTX.enter_context(nc.sbuf_tensor(name, shape, dt))

        ident = A("ident_sb", [128, 128], BF16)
        bqs = A("bqs_sb", [128, 4])
        bks = A("bks_sb", [128, 4])
        bvrow = A("bvrow_sb", [1, DK], BF16)
        ones1 = A("ones1_sb", [1, 128], BF16)
        sb1s = A("sb1_sb", [128, NFC, 2])
        rb1s = A("rb1_sb", [128, NFC, NEXP])
        b2ext = A("b2ext_sb", [16, D], BF16)
        sb2n = A("sb2n_sb", [2, D], BF16)
        ones2 = A("ones2_sb", [2, 128], BF16)
        centhi = A("centhi_sb", [128, NDC, NEXP], BF16)
        centlo = A("centlo_sb", [128, NDC, NEXP], BF16)
        cbias_s = A("cbias_sb", [16, 1])

        ATT = ExitStack()
        AA = lambda name, shape, dt=F32: ATT.enter_context(nc.sbuf_tensor(name, shape, dt))
        Pfold = AA("pfold_sb", [128, 4, D], BF16)
        WKT = AA("wkt_sb", [128, NDC, DK], BF16)
        WVT = AA("wvt_sb", [128, NDC, DK], BF16)
        QsumT = AA("qsumt_sb", [128, 4, TQ], BF16)

        psA = CTX.enter_context(nc.psum_tensor("psA", [128, 512], F32))
        psB = CTX.enter_context(nc.psum_tensor("psB", [128, 512], F32))
        psC = CTX.enter_context(nc.psum_tensor("psC", [128, 512], F32))
        psD = CTX.enter_context(nc.psum_tensor("psD", [128, 512], F32))
        psE = CTX.enter_context(nc.psum_tensor("psE", [128, 512], F32))
        psF = CTX.enter_context(nc.psum_tensor("psF", [128, 512], F32))
        ptr1 = CTX.enter_context(nc.psum_tensor("ptr1", [128, 1024], BF16))
        ptr2 = CTX.enter_context(nc.psum_tensor("ptr2", [128, 1024], BF16))
        psY = [psC, psD, psE, psF]

        TP = {"k": 0, "ev": [None, None]}

        def TRP(src_ap, outP, outF, id_ap, evac_maker, deps):
            k = TP["k"]
            TP["k"] += 1
            bank = [ptr1, ptr2][k % 2]
            pap = bank[0:outP, 0:outF]
            tst = S.op("T", lambda e, pap=pap, s=src_ap, ia=id_ap: e.transpose(pap, s, ia),
                       deps=list(deps) + [TP["ev"][k % 2]])
            ev = evac_maker(pap, tst)
            TP["ev"][k % 2] = ev
            return tst, ev

        P1wq = ExitStack()
        WQT = P1wq.enter_context(nc.sbuf_tensor("wqt_sb", [128, NDC, DK], BF16))
        P1a = ExitStack()
        A1 = lambda name, shape, dt=F32: P1a.enter_context(nc.sbuf_tensor(name, shape, dt))
        wstg = A1("wstg", [128, 2, D])
        wstg2 = A1("wstg2", [128, 2, D])
        wbfh = A1("wbfh", [128, 2, D], BF16)
        bqr_s = A1("bqr_s", [128, 16])
        centf = A1("centf", [128, NDC, NEXP])
        wbfh2 = A1("wbfh2", [128, 2, D], BF16)

        ld_ident = DMA("dY", ident[:, :], ident_in[:, :])
        ld_bqr = DMA("dY", bqr_s[:, :], bq_r[:, :])
        ld_bk = DMA("dY", bks[:, :], bk_r[:, :])
        ld_sb1 = DMA("dY", sb1s[:, :, :], sb1_r[:, :, :])
        ld_rb1 = DMA("dY", rb1s[:, :, :], rb1_r[:, :, :])
        ld_cb = DMA("dY", cbias_s[:, :], cbias[:, :])
        ld_centf = DMA("dY", centf[:, :, :], cent_r.rearrange("(dc p) j -> p dc j", p=128))
        ld_bv = DMA("dG", bvrow[:, :], bv_in[:, :])
        ld_rb2 = DMA("dG", b2ext[:, :], rb2[:, :])
        ld_sb2 = DMA("dG", sb2n[:, :], sb2[:, :])
        ld_centh = DMA("dG", centhi[:, :, :], cent_r.rearrange("(dc p) j -> p dc j", p=128))

        s1 = S.op("V", lambda e: e.tensor_add(bqs[:, :], bqr_s[:, 0:4], bqr_s[:, 4:8]), deps=[ld_bqr])
        s2 = S.op("V", lambda e: e.tensor_add(bqs[:, :], bqs[:, :], bqr_s[:, 8:12]), deps=[s1])
        bqs_done = S.op("V", lambda e: e.tensor_add(bqs[:, :], bqs[:, :], bqr_s[:, 12:16]), deps=[s2])
        clo = S.op("V", lambda e: e.tensor_sub(centlo[:, :, :], centf[:, :, :], centhi[:, :, :]),
                   deps=[ld_centf, ld_centh])
        ones_set = S.op("V", lambda e: e.memset(ones1[:, :], 1.0))
        ones2_set = S.op("V", lambda e: e.memset(ones2[:, :], 1.0))

        def fold_weight(src_dram, out_cvt_fn, prev_users):
            outs = []
            prevA = prev_users
            for ah in range(2):
                l0 = DMA("dG", wstg[:, :, :],
                         src_dram[256 * ah:256 * ah + 256, :].rearrange("(a p) d -> p a d", p=128),
                         deps=prevA)
                fold = l0
                addstep = None
                for r in range(1, 4):
                    lb = DMA("dG", wstg2[:, :, :],
                             src_dram[512 * r + 256 * ah:512 * r + 256 * ah + 256, :]
                             .rearrange("(a p) d -> p a d", p=128),
                             deps=[addstep] + prevA)
                    addstep = S.op("V", lambda e: e.tensor_add(wstg[:, :, :], wstg[:, :, :],
                                                              wstg2[:, :, :]),
                                   deps=[l0, lb, addstep])
                outs.append(out_cvt_fn(ah, addstep))
                prevA = [outs[-1][0]]
            return outs

        def emit_tr(srcbuf, a_local, ah, dst, dstcol_base, dep):
            evs = []
            for dc in range(NDC):
                def mk(pap, tst, d=dc, ds=dst, cb=dstcol_base):
                    return S.op("V", lambda e: e.tensor_copy(ds[:, d, cb:cb + 128], pap),
                                deps=[tst])
                tst, ev = TRP(srcbuf[:, a_local, dc * 128:(dc + 1) * 128], 128, 128,
                              ident[:, :], mk, [dep, ld_ident])
                evs.append(ev)
            return evs

        def wq_cvt(ah, fold_step):
            cvt = S.op("V", lambda e: e.tensor_copy(wbfh[:, :, :], wstg[:, :, :]), deps=[fold_step])
            evs = []
            for al in range(2):
                evs += emit_tr(wbfh, al, ah, WQT, (2 * ah + al) * 128, cvt)
            return [evs[-1], cvt] + evs

        wq_outs = fold_weight(wq, wq_cvt, [])
        wqt_done = wq_outs[-1][0]
        wq_users = [s for o in wq_outs for s in o]

        def pf_cvt(ah, fold_step):
            cvt = S.op("V", lambda e, ah=ah: e.tensor_scalar(
                Pfold[:, 2 * ah:2 * ah + 2, :], wstg[:, :, :], RSQ128, None, ALU.mult),
                deps=[fold_step])
            return [cvt]

        pf_outs = fold_weight(proj, pf_cvt, wq_users)
        cvt_pf = pf_outs[-1][0]


        kv_tr_last = {}
        prev_half_load = None
        for (wsrc, dst) in ((wk, WKT), (wv, WVT)):
            for ah in range(2):
                lh = DMA("dG", wbfh2[:, :, :],
                         wsrc[256 * ah:256 * ah + 256, :].rearrange("(a p) d -> p a d", p=128),
                         deps=[prev_half_load])
                evs = []
                for al in range(2):
                    evs += emit_tr(wbfh2, al, ah, dst, (2 * ah + al) * 128, lh)
                prev_half_load = evs[-1]
                kv_tr_last[id(dst)] = evs[-1]
        wkt_done = kv_tr_last[id(WKT)]
        wvt_done = kv_tr_last[id(WVT)]

        scrw1_done = {}
        scrw2_done = {}

        bar_snap = S.snapshot()
        bar_steps = [S.op(e, lambda eng: eng.nop(), deps=bar_snap) for e in ("T", "V", "S", "Y")]

        P1a.close()

        PHq = ExitStack()
        A2 = lambda name, shape, dt=F32: PHq.enter_context(nc.sbuf_tensor(name, shape, dt))
        xt = [A2("xt0", [128, D]), A2("xt1", [128, D])]
        stats = [A2("st0", [128, 8]), A2("st1", [128, 8])]
        hib = [A2("hib0", [128, D], BF16), A2("hib1", [128, D], BF16)]
        lob = [A2("lob0", [128, D], BF16), A2("lob1", [128, D], BF16)]
        xfb = [A2("xfb0", [128, D]), A2("xfb1", [128, D])]
        xnWq = A2("xnWq", [128, NDC, 2, 512], BF16)

        def ln_tile(bufs, src_ap, row0, b, want_lo, prevu, dst_scr, prev_lo):
            xt, stats, hib, lob, xfb = bufs
            xb = xt[b]
            hb = hib[b]
            xfb_b = xfb[b] if xfb is not None else None
            lob_b = lob[b] if lob is not None else None
            dummy = xfb_b if xfb_b is not None else hb
            ld = DMA("dY", xb[:, :], src_ap[row0:row0 + 128, :], deps=prevu)
            st = stats[b]
            r1 = S.op("V", lambda e, xb=xb: e.reduce_sum(st[:, 0:1], xb[:, :], axis=AX.X), deps=[ld])
            sq = S.op("S", lambda e, xb=xb, dummy=dummy: e.activation(dummy[:, :], xb[:, :], AF.Square,
                                                                     accum_out=st[:, 4:5]),
                      deps=[ld] + prevu + prev_lo)
            m = S.op("V", lambda e: e.tensor_scalar(st[:, 1:2], st[:, 0:1], float(1.0 / D), None, ALU.mult),
                     deps=[r1])
            v2 = S.op("V", lambda e: e.tensor_scalar(st[:, 2:3], st[:, 4:5], float(1.0 / D), 1e-5,
                                                     ALU.mult, ALU.add), deps=[sq])
            mm = S.op("V", lambda e: e.tensor_mul(st[:, 3:4], st[:, 1:2], st[:, 1:2]), deps=[m])
            vv = S.op("V", lambda e: e.tensor_sub(st[:, 2:3], st[:, 2:3], st[:, 3:4]), deps=[v2, mm])
            lnv = S.op("S", lambda e: e.activation(st[:, 7:8], st[:, 2:3], AF.Ln), deps=[vv])
            rstd = S.op("S", lambda e: e.activation(st[:, 5:6], st[:, 7:8], AF.Exp, scale=-0.5), deps=[lnv])
            nmr = S.op("V", lambda e: e.tensor_scalar(st[:, 6:7], st[:, 1:2], st[:, 5:6], -1.0,
                                                      ALU.mult, ALU.mult), deps=[rstd])
            hi = S.op("S", lambda e, xb=xb, hb=hb: e.activation(hb[:, :], xb[:, :], AF.Identity,
                                                                bias=st[:, 6:7], scale=st[:, 5:6]),
                      deps=[nmr] + prevu)
            lo_store = None
            lo_users = []
            xf = lo = None
            if want_lo:
                xf = S.op("S", lambda e, xb=xb, xfb_b=xfb_b: e.activation(
                    xfb_b[:, :], xb[:, :], AF.Identity,
                    bias=st[:, 6:7], scale=st[:, 5:6]),
                    deps=[nmr, sq] + prev_lo)
                lo = S.op("V", lambda e, hb=hb, xfb_b=xfb_b, lob_b=lob_b: e.tensor_sub(
                    lob_b[:, :], xfb_b[:, :], hb[:, :]),
                    deps=[xf, hi] + prev_lo)
                lo_store = DMA("dY", scr_qlo[row0:row0 + 128, :], lob_b[:, :], deps=[lo])
                lo_users = [xf, lo, lo_store]
            hi_store = DMA("dY", dst_scr[row0:row0 + 128, :], hb[:, :], deps=[hi])
            last = [hi_store, sq, hi, r1]
            if want_lo:
                last += [xf, lo]
            return hi_store, lo_store, last, lo_users

        q_hi = []
        q_lo = []
        prev_lo = [[], []]
        prevu = [list(bar_steps), list(bar_steps)]
        qwin_users = [[], []]
        pq_ev = [None, None]
        qev_all = []
        for tc in range(4):
            for kk in range(4):
                k = tc * 4 + kk
                hs, ls, lu, lou = ln_tile((xt, stats, hib, lob, xfb), xq, 128 * k, k % 2, True,
                                          prevu[k % 2], scr_q, prev_lo[k % 2])
                q_hi.append(hs)
                q_lo.append(ls)
                prevu[k % 2] = lu
                prev_lo[k % 2] = lou
            w = tc % 2
            trs = []
            for dc in range(NDC):
                t = DMT("dS", xnWq[:, dc, w, :],
                        scr_q[tc * 512:(tc + 1) * 512, dc * 128:(dc + 1) * 128],
                        deps=q_hi[4 * tc:4 * tc + 4] + qwin_users[w] + bar_steps)
                trs.append(t)
            users = []
            for a in range(4):
                g = a % 2
                ps = [psA, psB][g]

                def qfn(e, a=a, w=w, ps=ps):
                    for dc in range(NDC):
                        ins = e.matmul(ps[:, :], WQT[:, dc, a * 128:(a + 1) * 128],
                                       xnWq[:, dc, w, :], start=(dc == 0), stop=(dc == NDC - 1))
                    return ins
                mmst = S.op("T", qfn, deps=trs + [wqt_done, pq_ev[g]])
                ev = S.op("S", lambda e, a=a, tc=tc, ps=ps: e.activation(
                    QsumT[:, a, tc * 512:(tc + 1) * 512], ps[:, :], AF.Identity,
                    bias=bqs[:, a:a + 1]), deps=[mmst, bqs_done])
                pq_ev[g] = ev
                qev_all.append(ev)
                users.append(mmst)
            qwin_users[w] = users

        bar_snap = S.snapshot()
        bar_steps = [S.op(e, lambda eng: eng.nop(), deps=bar_snap) for e in ("T", "V", "S", "Y")]
        PHq.close()
        P1wq.close()


        ATT2 = ExitStack()
        AA2 = lambda name, shape, dt=F32: ATT2.enter_context(nc.sbuf_tensor(name, shape, dt))
        KT = AA2("kt_sb", [128, 4, TKV], BF16)
        Vsb = AA2("v_sb", [128, 32, DK], BF16)
        PHkv = ExitStack()
        A2 = lambda name, shape, dt=F32: PHkv.enter_context(nc.sbuf_tensor(name, shape, dt))
        xt = [A2("kxt0", [128, D]), A2("kxt1", [128, D])]
        stats = [A2("kst0", [128, 8]), A2("kst1", [128, 8])]
        hib = [A2("khib0", [128, D], BF16), A2("khib1", [128, D], BF16)]
        xnWk = A2("xnWk", [128, NDC, 2, 512], BF16)

        kv_hi = []
        prevu = [list(bar_steps), list(bar_steps)]
        prev_lo = []
        for k in range(32):
            hs, _, lu, _ = ln_tile((xt, stats, hib, None, None), xkv, 128 * k, k % 2, False,
                                   prevu[k % 2], scr_kv, [])
            kv_hi.append(hs)
            prevu[k % 2] = lu

        win_users = [bar_steps, bar_steps]
        pk_ev = [qev_all[-1], qev_all[-1]]
        pv_ev = [None, None]
        k_ev_all = []
        v_ev_all = []
        for tc in range(8):
            w = tc % 2
            trs = []
            for dc in range(NDC):
                t = DMT("dS", xnWk[:, dc, w, :],
                        scr_kv[tc * 512:(tc + 1) * 512, dc * 128:(dc + 1) * 128],
                        deps=kv_hi[4 * tc:4 * tc + 4] + list(win_users[w]))
                trs.append(t)
            users = []
            for a in range(4):
                g = a % 2
                ps = [psA, psB][g]

                def kfn(e, a=a, w=w, ps=ps):
                    for dc in range(NDC):
                        ins = e.matmul(ps[:, :], WKT[:, dc, a * 128:(a + 1) * 128],
                                       xnWk[:, dc, w, :], start=(dc == 0), stop=(dc == NDC - 1))
                    return ins
                mmst = S.op("T", kfn, deps=trs + [wkt_done, pk_ev[g]])
                ev = S.op("S", lambda e, a=a, tc=tc, ps=ps: e.activation(
                    KT[:, a, tc * 512:(tc + 1) * 512], ps[:, :], AF.Identity,
                    bias=bks[:, a:a + 1]), deps=[mmst, ld_bk])
                pk_ev[g] = ev
                k_ev_all.append(ev)
                users.append(mmst)
            for sub in range(4):
                g = sub % 2
                ps = [psC, psD][g]
                kb = tc * 4 + sub

                def vfn(e, sub=sub, w=w, ps=ps):
                    for dc in range(NDC):
                        e.matmul(ps[:, :], xnWk[:, dc, w, sub * 128:(sub + 1) * 128],
                                 WVT[:, dc, :], start=(dc == 0), stop=False)
                    return e.matmul(ps[:, :], ones1[:, :], bvrow[:, :], start=False, stop=True)
                mmst = S.op("T", vfn, deps=trs + [wvt_done, ld_bv, ones_set, pv_ev[g]])
                ev = S.op("V", lambda e, kb=kb, ps=ps: e.tensor_copy(Vsb[:, kb, :], ps[:, :]),
                          deps=[mmst])
                pv_ev[g] = ev
                v_ev_all.append(ev)
                users.append(mmst)
            win_users[w] = users

        if dbg:
            d1 = DMA("dY", dbg_kt[:, :, :], KT[:, :, :], deps=k_ev_all)
            d2 = DMA("dY", dbg_v[:, :, :], Vsb[:, :, :], deps=v_ev_all)
            d3 = DMA("dY", dbg_qs[:, :, :], QsumT[:, :, :], deps=qev_all)
            dbg_proj = [d1, d2, d3]
        else:
            dbg_proj = []

        bar_snap = S.snapshot()
        bar_steps = [S.op(e, lambda eng: eng.nop(), deps=bar_snap) for e in ("T", "V", "S", "Y")]
        PHkv.close()


        P4 = ExitStack()
        A4 = lambda name, shape, dt=F32: P4.enter_context(nc.sbuf_tensor(name, shape, dt))
        masks = A4("masks_sb", [128, 16, 256], BF16)
        srow = A4("srow", [128, TKV])
        Pbf = A4("pbf", [128, TKV], BF16)
        PTb = A4("ptb", [128, 32, 128], BF16)
        Zbf = A4("zbf", [128, 512], BF16)
        ZTb = A4("ztb", [128, 4, 128], BF16)
        stile = [A4("stile0", [128, D], BF16), A4("stile1", [128, D], BF16)]
        smax = A4("smax", [128, 2])
        ssum = A4("ssum", [128, 2])

        ld_masks = DMA("dY", masks[:, :, :], masks_in[:, :, :], deps=bar_steps)

        psS_ev = [None, None]
        s_stores = [None, None]
        prev_srow_reader = None
        prev_pbf_reader = None
        prev_rcp = None
        prev_av = None
        prev_zread = None
        psZ_ev = None
        psSP_ev = [None, None]
        s_store_list = []

        for i in range(16):
            nkeys = 256 * (i + 1)
            nkb = nkeys // 128
            blocks = []
            off = 0
            while off < nkeys:
                ln_ = min(512, nkeys - off)
                blocks.append((off, ln_))
                off += ln_
            ev_blocks = []
            for bi, (off, ln_) in enumerate(blocks):
                g = bi % 2
                ps = [psA, psB][g]

                def scfn(e, i=i, off=off, ln_=ln_, ps=ps):
                    for a in range(4):
                        ins = e.matmul(ps[:, 0:ln_], QsumT[:, a, i * 128:(i + 1) * 128],
                                       KT[:, a, off:off + ln_], start=(a == 0), stop=(a == 3))
                    return ins
                tc_max = (off + ln_ - 1) // 512
                mmst = S.op("T", scfn,
                            deps=[psS_ev[g]] + qev_all[:4 * (i // 4 + 1)]
                            + k_ev_all[:4 * (tc_max + 1)] + dbg_proj)
                ev = S.op("V", lambda e, off=off, ln_=ln_, ps=ps: e.tensor_copy(
                    srow[:, off:off + ln_], ps[:, 0:ln_]),
                    deps=[mmst, prev_srow_reader])
                psS_ev[g] = ev
                ev_blocks.append(ev)
            mk = S.op("V", lambda e, i=i, nk=nkeys: e.tensor_add(
                srow[:, nk - 256:nk], srow[:, nk - 256:nk], masks[:, i, :]),
                deps=[ev_blocks[-1], ld_masks])
            nmx = S.op("V", lambda e, nk=nkeys: e.reduce_max(
                smax[:, 0:1], srow[:, 0:nk], axis=AX.X, negate=True),
                deps=[mk, prev_pbf_reader] + ev_blocks)
            exps = S.op("S", lambda e, nk=nkeys: e.activation(
                Pbf[:, 0:nk], srow[:, 0:nk], AF.Exp, bias=smax[:, 0:1],
                accum_out=ssum[:, 0:1]),
                deps=[nmx, prev_pbf_reader, prev_rcp])
            rcp = S.op("V", lambda e: e.reciprocal(ssum[:, 1:2], ssum[:, 0:1]),
                       deps=[exps, prev_zread])
            prev_rcp = rcp
            pt_evs = []
            last_pt_t = None
            for kb in range(nkb):
                def mk(pap, tst, kb=kb):
                    return S.op("V", lambda e: e.tensor_copy(PTb[:, kb, :], pap),
                                deps=[tst, prev_av])
                tst, evt = TRP(Pbf[:, kb * 128:(kb + 1) * 128], 128, 128, ident[:, :],
                               mk, [exps, prev_av])
                pt_evs.append(evt)
                last_pt_t = tst

            def avfn(e, nkb=nkb):
                for kb in range(nkb):
                    ins = e.matmul(psC[:, :], PTb[:, kb, :], Vsb[:, kb, :],
                                   start=(kb == 0), stop=(kb == nkb - 1))
                return ins
            av = S.op("T", avfn,
                      deps=pt_evs + v_ev_all[:4 * ((nkeys - 1) // 512 + 1)] + [psZ_ev])
            prev_av = av
            prev_pbf_reader = last_pt_t
            prev_srow_reader = exps
            zev = S.op("V", lambda e: e.tensor_scalar(
                Zbf[:, :], psC[:, :], ssum[:, 1:2], None, ALU.mult),
                deps=[av, rcp, prev_zread])
            psZ_ev = zev
            zt_evs = []
            zt_ts = []
            for c in range(4):
                def mk(pap, tst, c=c):
                    return S.op("V", lambda e: e.tensor_copy(ZTb[:, c, :], pap),
                                deps=[tst, prev_zread])
                tst, evt = TRP(Zbf[:, c * 128:(c + 1) * 128], 128, 128, ident[:, :],
                               mk, [zev, prev_zread])
                zt_evs.append(evt)
                zt_ts.append(tst)
            prev_zread = zt_ts[-1]
            sb = i % 2
            sp_evs = []
            for dg in range(4):
                g = dg % 2
                ps = [psE, psF][g]

                def spfn(e, dg=dg, ps=ps):
                    for c in range(4):
                        ins = e.matmul(ps[:, :], ZTb[:, c, :], Pfold[:, c, dg * 512:(dg + 1) * 512],
                                       start=(c == 0), stop=(c == 3))
                    return ins
                mmst = S.op("T", spfn, deps=zt_evs + [cvt_pf, psSP_ev[g]])
                ev = S.op("S", lambda e, dg=dg, ps=ps, sb=sb: e.activation(
                    stile[sb][:, dg * 512:(dg + 1) * 512], ps[:, :], AF.Copy),
                    deps=[mmst, s_stores[sb]])
                psSP_ev[g] = ev
                sp_evs.append(ev)
            sst = DMA("dY", s_scr[i * 128:(i + 1) * 128, :], stile[sb][:, :], deps=sp_evs)
            s_stores[sb] = sst
            s_store_list.append(sst)

        bar_snap = S.snapshot()
        bar_steps = [S.op(e, lambda eng: eng.nop(), deps=bar_snap) for e in ("T", "V", "S", "Y")]
        P4.close()
        ATT2.close()
        ATT.close()

        P5 = ExitStack()
        A5 = lambda name, shape, dt=F32: P5.enter_context(nc.sbuf_tensor(name, shape, dt))
        xnqTh = A5("xnqTh", [128, NDC, HALF], BF16)
        lowin = A5("lowin", [128, 2, HALF], BF16)
        acc = A5("acc", [128, 8, D])
        hT = A5("hT", [128, NFC, HALF], BF16)
        w2b = A5("w2b", [128, NFC, 2, 512], BF16)
        w1b = [A5("w1b0", [128, NDC, 128], BF16), A5("w1b1", [128, NDC, 128], BF16)]
        sct = A5("sct", [16, HALF])
        schi = A5("schi", [16, 128], BF16)
        sclo = A5("sclo", [16, 128], BF16)
        schiT = A5("schiT", [128, 16])
        sc32 = A5("sc32", [128, 16])
        rnk = A5("rnk", [128, 16])
        mskb = A5("mskb", [128, 16])
        graw = A5("graw", [128, 16])
        den = A5("den", [128, 2])
        gating = A5("gating", [128, 8, NEXP])
        gbf = A5("gbf", [128, NEXP], BF16)
        gT = A5("gt", [16, 8, 128], BF16)
        xnqrl = [A5("xnqrl0", [128, D], BF16), A5("xnqrl1", [128, D], BF16)]
        xlorl = A5("xlorl", [128, D], BF16)
        xqrl = A5("xqrl", [128, D])
        srl = A5("srl", [128, D], BF16)
        gsil = A5("gsil", [128, D], BF16)
        fin = A5("fin", [128, D])

        b0 = bar_steps[0]
        U = {
            "xnqTh": list(bar_steps), "lowin": [b0, b0], "sct": [b0],
            "schi": b0, "sclo": b0, "schiT": b0, "sc32": b0, "rnk": b0,
            "mskb": b0, "graw": b0, "den": b0, "gbf": b0,
            "gat": [b0] * 8, "gT": [b0] * 8,
            "acc": {}, "hT": [b0], "w2b": [[b0], [b0]], "w1b": [[b0], [b0]],
            "xnqrl": [b0, b0], "xlorl": b0, "xqrl": b0,
            "srl": b0, "gsil": b0, "fin": b0,
            "psH": [bar_steps[1], bar_steps[1]], "psY": [bar_steps[1]] * 4,
            "psR": [bar_steps[1], bar_steps[1]],
        }
        out_stores = []

        for half in range(2):
            hoff = half * HALF
            trh = []
            for dc in range(NDC):
                t = DMT("dS", xnqTh[:, dc, :],
                        scr_q[hoff:hoff + HALF, dc * 128:(dc + 1) * 128],
                        deps=q_hi + U["xnqTh"])
                trh.append(t)

            rt_steps = []
            for dc in range(NDC):
                tl = DMT("dS", lowin[:, dc % 2, :],
                         scr_qlo[hoff:hoff + HALF, dc * 128:(dc + 1) * 128],
                         deps=q_lo + [U["lowin"][dc % 2]])

                def rtfn(e, dc=dc):
                    for sub in range(2):
                        ps = [psE, psF][sub]
                        e.matmul(ps[0:16, :], centhi[:, dc, :],
                                 xnqTh[:, dc, sub * 512:(sub + 1) * 512],
                                 start=(dc == 0), stop=False)
                        e.matmul(ps[0:16, :], centlo[:, dc, :],
                                 xnqTh[:, dc, sub * 512:(sub + 1) * 512],
                                 start=False, stop=False)
                        ins = e.matmul(ps[0:16, :], centhi[:, dc, :],
                                       lowin[:, dc % 2, sub * 512:(sub + 1) * 512],
                                       start=False, stop=(dc == NDC - 1))
                    return ins
                st = S.op("T", rtfn, deps=[trh[dc], tl, clo, ld_centh] + U["psR"])
                U["lowin"][dc % 2] = st
                rt_steps.append(st)
            rt_evs = []
            for sub in range(2):
                ps = [psE, psF][sub]
                ev = S.op("S", lambda e, sub=sub, ps=ps: e.activation(
                    sct[:, sub * 512:(sub + 1) * 512], ps[0:16, :], AF.Sigmoid),
                    deps=[rt_steps[-1]] + U["sct"])
                rt_evs.append(ev)
            badd = S.op("V", lambda e: e.tensor_scalar_add(sct[:, :], sct[:, :], cbias_s[:, 0:1]),
                        deps=rt_evs + [ld_cb])
            U["psY"][2] = rt_evs[0]
            U["psY"][3] = rt_evs[1]

            gat_steps = []
            gT_steps = []
            c2 = None
            for tt in range(8):
                c1 = S.op("V", lambda e, tt=tt: e.tensor_copy(schi[:, :], sct[:, tt * 128:(tt + 1) * 128]),
                          deps=[badd, U["schi"], c2])
                c2 = S.op("V", lambda e, tt=tt: e.tensor_sub(
                    sclo[:, :], sct[:, tt * 128:(tt + 1) * 128], schi[:, :]),
                    deps=[c1, U["sclo"]])
                def mk1(pap, tst):
                    return S.op("V", lambda e, pap=pap: e.tensor_copy(schiT[:, :], pap),
                                deps=[tst, U["schiT"]])
                t1, e1 = TRP(schi[:, :], 128, 16, ident[0:16, 0:16], mk1, [c1])

                def mk2(pap, tst):
                    return S.op("V", lambda e, pap=pap: e.tensor_add(sc32[:, :], schiT[:, :], pap),
                                deps=[tst, e1, U["sc32"]])
                t2, e2 = TRP(sclo[:, :], 128, 16, ident[0:16, 0:16], mk2, [c2])
                U["schiT"] = e2
                U["schi"] = t1
                U["sclo"] = t2
                prev = S.op("V", lambda e: e.tensor_scalar(rnk[:, :], sc32[:, :], sc32[:, 0:1],
                                                           None, ALU.is_lt),
                            deps=[e2, U["rnk"]])
                for j in range(1, NEXP):
                    prev = S.op("V", lambda e, j=j: e.scalar_tensor_tensor(
                        rnk[:, :], sc32[:, :], sc32[:, j:j + 1], rnk[:, :], ALU.is_lt, ALU.add),
                        deps=[prev])
                mk1 = S.op("V", lambda e: e.tensor_scalar(mskb[:, :], rnk[:, :], 3.5, None, ALU.is_lt),
                           deps=[prev, U["mskb"]])
                gr = S.op("V", lambda e: e.tensor_mul(graw[:, :], sc32[:, :], mskb[:, :]),
                          deps=[mk1, U["graw"]])
                U["sc32"] = gr
                U["rnk"] = mk1
                ds = S.op("V", lambda e: e.reduce_sum(den[:, 0:1], graw[:, :], axis=AX.X),
                          deps=[gr, U["den"]])
                dm = S.op("V", lambda e: e.tensor_scalar_max(den[:, 1:2], den[:, 0:1], 1e-8),
                          deps=[ds])
                dr = S.op("V", lambda e: e.reciprocal(den[:, 0:1], den[:, 1:2]), deps=[dm])
                U["den"] = dr
                ga = S.op("V", lambda e, tt=tt: e.tensor_scalar(
                    gating[:, tt, :], graw[:, :], den[:, 0:1], None, ALU.mult),
                    deps=[dr, U["gat"][tt]])
                U["mskb"] = ga
                U["graw"] = ga
                gb = S.op("V", lambda e, tt=tt: e.tensor_copy(gbf[:, :], gating[:, tt, :]),
                          deps=[ga, U["gbf"]])
                def mk3(pap, tst, tt=tt):
                    return S.op("V", lambda e, pap=pap: e.tensor_copy(gT[:, tt, :], pap),
                                deps=[tst, U["gT"][tt]])
                t3, e3 = TRP(gbf[:, :], 16, 128, ident[:, :], mk3, [gb])
                U["gbf"] = t3
                gat_steps.append(ga)
                gT_steps.append(e3)
                U["gat"][tt] = ga
                U["gT"][tt] = e3

            U["sct"] = [c2]
            if dbg:
                dgat = DMA("dY", dbg_gat[:, half * 8:(half + 1) * 8, :], gating[:, :, :],
                           deps=gat_steps)
                for tt in range(8):
                    U["gat"][tt] = dgat

            def expert_pass(jconv, b1s, b1col, mode, gcol):
                w1_src = sw1[jconv - 16] if jconv >= 16 else rw1[jconv]
                w2_src = sw2[jconv - 16] if jconv >= 16 else rw2[jconv]
                h_evs = []
                for fs in range(NFC):
                    wb = w1b[fs % 2]
                    if half == 0:
                        w1f = w1_src[:, fs * 128:(fs + 1) * 128].rearrange(
                            "(dc p) f -> p dc f", p=128)
                        dma1 = DMA("dG", wb[:, 0:8, :], w1f[:, 0:8, :],
                                   deps=U["w1b"][fs % 2])
                        dma2 = DMA("dG", wb[:, 8:16, :], w1f[:, 8:16, :],
                                   deps=U["w1b"][fs % 2])
                        scrw = DMA("dS",
                                   scr_w1[jconv, fs].rearrange("p (dc f) -> p dc f", f=128),
                                   wb[:, :, :], deps=[dma1, dma2])
                        scrw1_done[(jconv, fs)] = scrw
                    else:
                        w1sl = scr_w1[jconv, fs].rearrange("p (dc f) -> p dc f", f=128)
                        q = "dG" if fs % 2 == 0 else "dY"
                        dma1 = DMA(q, wb[:, :, :], w1sl,
                                   deps=U["w1b"][fs % 2] + [scrw1_done[(jconv, fs)]])
                        dma2 = None
                        scrw = None
                    musers = [] if scrw is None else [scrw]
                    for sub in range(2):
                        g = (fs * 2 + sub) % 2
                        ps = [psA, psB][g]

                        def hfn(e, wb=wb, sub=sub, ps=ps):
                            for dc in range(NDC):
                                ins = e.matmul(ps[:, :], wb[:, dc, :],
                                               xnqTh[:, dc, sub * 512:(sub + 1) * 512],
                                               start=(dc == 0), stop=(dc == NDC - 1))
                            return ins
                        mmst = S.op("T", hfn,
                                    deps=[dma1, dma2, U["psH"][g]] + trh + U["hT"])
                        ev = S.op("S", lambda e, fs=fs, sub=sub, ps=ps, b1s=b1s, b1col=b1col: e.activation(
                            hT[:, fs, sub * 512:(sub + 1) * 512], ps[:, :], AF.Relu,
                            bias=b1s[:, fs, b1col:b1col + 1]),
                            deps=[mmst, ld_sb1, ld_rb1])
                        U["psH"][g] = ev
                        h_evs.append(ev)
                        musers.append(mmst)
                    U["w1b"][fs % 2] = musers
                y_last_mm = None
                for qd in range(4):
                    qb = qd % 2
                    if half == 0:
                        w2f = w2_src[:, qd * 512:(qd + 1) * 512].rearrange(
                            "(fc p) d -> p fc d", p=128)
                        wd1 = DMA("dG", w2b[:, 0:6, qb, :], w2f[:, 0:6, :],
                                  deps=U["w2b"][qb])
                        wd2 = DMA("dG", w2b[:, 6:11, qb, :], w2f[:, 6:11, :],
                                  deps=U["w2b"][qb])
                        sc1 = DMA("dS",
                                  scr_w2[jconv, 2 * qd].rearrange("p (fc x) -> p fc x", x=256),
                                  w2b[:, :, qb, 0:256], deps=[wd1, wd2])
                        sc2 = DMA("dY",
                                  scr_w2[jconv, 2 * qd + 1].rearrange("p (fc x) -> p fc x", x=256),
                                  w2b[:, :, qb, 256:512], deps=[wd1, wd2])
                        scrw2_done[(jconv, 2 * qd)] = sc1
                        scrw2_done[(jconv, 2 * qd + 1)] = sc2
                        yusers = [sc1, sc2]
                    else:
                        wd1 = DMA("dG", w2b[:, :, qb, 0:256],
                                  scr_w2[jconv, 2 * qd].rearrange("p (fc x) -> p fc x", x=256),
                                  deps=U["w2b"][qb] + [scrw2_done[(jconv, 2 * qd)]])
                        wd2 = DMA("dY", w2b[:, :, qb, 256:512],
                                  scr_w2[jconv, 2 * qd + 1].rearrange("p (fc x) -> p fc x", x=256),
                                  deps=U["w2b"][qb] + [scrw2_done[(jconv, 2 * qd + 1)]])
                        yusers = []
                    for tt in range(8):
                        bk = (qd * 8 + tt) % 4
                        ps = psY[bk]
                        extras = (mode == "init")
                        if extras:
                            rl1 = DMA("dY", xnqrl[tt % 2][:, :],
                                      scr_q[hoff + tt * 128:hoff + (tt + 1) * 128, :],
                                      deps=[U["xnqrl"][tt % 2]] + q_hi)
                            rl2 = DMA("dY", xlorl[:, :],
                                      scr_qlo[hoff + tt * 128:hoff + (tt + 1) * 128, :],
                                      deps=[U["xlorl"]] + q_lo)

                        def yfn(e, qd=qd, qb=qb, tt=tt, ps=ps, extras=extras):
                            for fs in range(NFC):
                                ins = e.matmul(ps[:, :], hT[:, fs, tt * 128:(tt + 1) * 128],
                                               w2b[:, fs, qb, :],
                                               start=(fs == 0), stop=(fs == NFC - 1 and not extras))
                            if extras:
                                e.matmul(ps[:, :], gT[:, tt, :], b2ext[:, qd * 512:(qd + 1) * 512],
                                         start=False, stop=False)
                                e.matmul(ps[:, :], ones2[:, :], sb2n[:, qd * 512:(qd + 1) * 512],
                                         start=False, stop=False)
                                e.matmul(ps[:, :], ident[:, :], xnqrl[tt % 2][:, qd * 512:(qd + 1) * 512],
                                         start=False, stop=False)
                                ins = e.matmul(ps[:, :], ident[:, :], xlorl[:, qd * 512:(qd + 1) * 512],
                                               start=False, stop=True)
                            return ins
                        deps = h_evs + [wd1, wd2, U["psY"][bk]]
                        if extras:
                            deps += [gT_steps[tt], ld_rb2, ld_sb2, ones2_set, rl1, rl2, ld_ident]
                        mmst = S.op("T", yfn, deps=deps)
                        if extras:
                            U["xnqrl"][tt % 2] = mmst
                            U["xlorl"] = mmst
                        accdep = U["acc"].get((tt, qd))
                        if mode == "init":
                            ev = S.op("V", lambda e, tt=tt, qd=qd, ps=ps: e.tensor_copy(
                                acc[:, tt, qd * 512:(qd + 1) * 512], ps[:, :]),
                                deps=[mmst, accdep])
                        elif mode == "add":
                            ev = S.op("V", lambda e, tt=tt, qd=qd, ps=ps: e.scalar_tensor_tensor(
                                acc[:, tt, qd * 512:(qd + 1) * 512], ps[:, :], 1.0,
                                acc[:, tt, qd * 512:(qd + 1) * 512], ALU.mult, ALU.add),
                                deps=[mmst, accdep])
                        else:
                            ev = S.op("V", lambda e, tt=tt, qd=qd, ps=ps, gcol=gcol: e.scalar_tensor_tensor(
                                acc[:, tt, qd * 512:(qd + 1) * 512], ps[:, :],
                                gating[:, tt, gcol:gcol + 1],
                                acc[:, tt, qd * 512:(qd + 1) * 512], ALU.mult, ALU.add),
                                deps=[mmst, accdep, gat_steps[tt]])
                        U["acc"][(tt, qd)] = ev
                        U["psY"][bk] = ev
                        yusers.append(mmst)
                        y_last_mm = mmst
                    U["w2b"][qb] = yusers
                U["hT"] = [y_last_mm]
                return y_last_mm

            init_last = expert_pass(16, sb1s, 0, "init", None)
            for tt in range(8):
                U["gT"][tt] = init_last
            expert_pass(17, sb1s, 1, "add", None)
            for j in range(NEXP):
                expert_pass(j, rb1s, j, "gated", j)
            U["psR"] = [U["psY"][2], U["psY"][3]]

            if dbg:
                dacc = DMA("dY", dbg_acc[:, half * 8:(half + 1) * 8, :], acc[:, :, :],
                           deps=[U["acc"][(tt, qd)] for tt in range(8) for qd in range(4)])
                dbg_acc_dep = [dacc]
            else:
                dbg_acc_dep = []

            for tt in range(8):
                acc_done = [U["acc"][(tt, qd)] for qd in range(4)] + dbg_acc_dep
                sil = S.op("S", lambda e, tt=tt: e.activation(gsil[:, :], acc[:, tt, :], AF.Silu),
                           deps=acc_done + [U["gsil"]])
                srload = DMA("dY", srl[:, :], s_scr[hoff + tt * 128:hoff + (tt + 1) * 128, :],
                             deps=[s_store_list[half * 8 + tt], U["srl"]])
                xqload = DMA("dY", xqrl[:, :], xq[hoff + tt * 128:hoff + (tt + 1) * 128, :],
                             deps=[U["xqrl"]])
                mu = S.op("V", lambda e: e.tensor_mul(fin[:, :], srl[:, :], gsil[:, :]),
                          deps=[sil, srload, U["fin"]])
                ad = S.op("V", lambda e: e.tensor_add(xqrl[:, :], fin[:, :], xqrl[:, :]),
                          deps=[mu, xqload])
                ost = DMA("dY", out[hoff + tt * 128:hoff + (tt + 1) * 128, :], xqrl[:, :],
                          deps=[ad])
                out_stores.append(ost)
                U["gsil"] = mu
                U["fin"] = ad
                U["srl"] = mu
                U["xqrl"] = ost
                for qd in range(4):
                    U["acc"][(tt, qd)] = sil

            U["xnqTh"] = [U["hT"][0], rt_steps[-1]]

        S.op("Y", lambda e: e.nop(), deps=out_stores)

        S.emit(nc, sems)
        P5.close()
    return nc


_PROG_CACHE = {}


def _get_prog(dbg=0, serialize=False):
    key = (dbg, serialize)
    if key not in _PROG_CACHE:
        _PROG_CACHE[key] = build_program(dbg=dbg, serialize=serialize)
    return _PROG_CACHE[key]


def make_in_maps(inputs, dbg=0):
    f32 = np.float32
    g = lambda k: np.ascontiguousarray(np.asarray(inputs[k], dtype=f32))
    x = g("L0_attn_x")
    WQ = g("L0_attn_WQ"); bQ = g("L0_attn_bQ")
    WK = g("L0_attn_WK"); bK = g("L0_attn_bK")
    WV = g("L0_attn_WV"); bV = g("L0_attn_bV")
    PJ = g("L0_attn_proj")
    sw1 = g("L0_ffn_shared_w1"); sb1 = g("L0_ffn_shared_b1")
    sw2 = g("L0_ffn_shared_w2"); sb2 = g("L0_ffn_shared_b2")
    rw1 = g("L0_ffn_routed_w1"); rb1 = g("L0_ffn_routed_b1")
    rw2 = g("L0_ffn_routed_w2"); rb2 = g("L0_ffn_routed_b2")
    cent = g("L0_ffn_centroids"); cb = g("L0_ffn_bias")

    common = {
        "ident": np.eye(128, dtype=BF),
        "wq": WQ,
        "bq_r": np.ascontiguousarray(bQ.reshape(16, 128).T),
        "wk": WK,
        "bk_r": np.ascontiguousarray(bK.reshape(4, 128).T),
        "wv": WV,
        "bv": bV.reshape(1, DK),
        "proj": PJ,
        "sw1": sw1,
        "sb1_r": np.ascontiguousarray(sb1.reshape(2, NFC, 128).transpose(2, 1, 0)),
        "sw2": sw2,
        "sb2": sb2,
        "rw1": rw1,
        "rb1_r": np.ascontiguousarray(rb1.reshape(NEXP, NFC, 128).transpose(2, 1, 0)),
        "rw2": rw2,
        "rb2": rb2,
        "cent_r": np.ascontiguousarray(cent.T),
        "cbias": cb.reshape(NEXP, 1),
    }
    masks = [build_masks(0), build_masks(1)]
    in_maps = []
    for core in range(8):
        b, p = core // 2, core % 2
        J = own_tile_ids(p)
        xq_host = np.ascontiguousarray(
            np.concatenate([x[b, 128 * j:128 * (j + 1), :] for j in J], axis=0))
        m = dict(common)
        m["xkv"] = x[b]
        m["xq"] = xq_host
        m["masks"] = masks[p]
        in_maps.append(m)
    return in_maps


def assemble(results):
    y = np.empty((4, TKV, D), dtype=np.float32)
    for core in range(8):
        b, p = core // 2, core % 2
        J = own_tile_ids(p)
        o = results[core]["out"]
        for i, j in enumerate(J):
            y[b, 128 * j:128 * (j + 1), :] = o[128 * i:128 * (i + 1), :]
    return y


def kernel(**inputs):
    nc = _get_prog(dbg=0)
    in_maps = make_in_maps(inputs)
    res = run_bass_kernel_spmd(nc, in_maps, core_ids=list(range(8)))
    return assemble(res.results)


# version 12
# speedup vs baseline: 1.9043x; 1.0313x over previous
import numpy as np
import ml_dtypes
from contextlib import ExitStack

import concourse.bass as bass
import concourse.mybir as mybir
from concourse.bass_utils import run_bass_kernel_spmd

F32 = mybir.dt.float32
BF16 = mybir.dt.bfloat16
AF = mybir.ActivationFunctionType
ALU = mybir.AluOpType
AX = mybir.AxisListType

BF = ml_dtypes.bfloat16
NEG = np.float32(-1e30)

D = 2048
DK = 512
TQ = 2048
TKV = 4096
DFF = 1408
NFC = 11
NDC = 16
NEXP = 16
HALF = 1024
RSQ128 = float(1.0 / np.sqrt(128.0))


def own_tile_ids(parity):
    lo = [2 * i + parity for i in range(8)]
    hi = [31 - 2 * i - parity for i in range(8)]
    return sorted(lo + hi)


def build_masks(parity):
    J = own_tile_ids(parity)
    m = np.zeros((128, 16, 256), dtype=np.float32)
    for i in range(16):
        q = 128 * J[i] + np.arange(128)[:, None]
        k = 256 * i + np.arange(256)[None, :]
        m[:, i, :] = np.where(k <= q, 0.0, NEG)
    return m.astype(BF)


class Step:
    __slots__ = ("idx", "ch", "fn", "deps", "amt", "val", "key")

    def __init__(self, idx, ch, fn, deps, amt, key):
        self.idx, self.ch, self.fn, self.deps, self.amt = idx, ch, fn, deps, amt
        self.val = None
        self.key = key


class Sched:
    ENGS = ("T", "V", "S", "G", "Y")
    CHANNELS = ("T", "V", "S", "G", "Y", "dY", "dS", "dG", "dGc")
    QUEUE_OF = {"dY": "Y", "dS": "S", "dG": "G", "dGc": "G"}
    NOBAR = ("dGc",)
    NROT = 16

    def __init__(self, serialize=False):
        self.steps = []
        self.last = {}
        self.rot = {c: 0 for c in self.CHANNELS}
        self.serialize = serialize

    def _add(self, ch, fn, deps, amt):
        deps = [d for d in deps if d is not None]
        if self.serialize and self.steps:
            deps = deps + [self.steps[-1]]
        if amt == 16:
            key = (ch, self.rot[ch] % self.NROT)
            self.rot[ch] += 1
        else:
            key = (ch, 0)
        st = Step(len(self.steps), ch, fn, deps, amt, key)
        self.steps.append(st)
        self.last[key] = st
        return st

    def op(self, eng, fn, deps=()):
        return self._add(eng, fn, list(deps), 1)

    def dma(self, queue, fn, deps=()):
        return self._add(queue, fn, list(deps), 16)

    def snapshot(self):
        return [s for k, s in self.last.items() if k[0] not in self.NOBAR]

    def emit(self, nc, sems):
        referenced = set()
        for s in self.steps:
            for d in s.deps:
                referenced.add(d.idx)
        counts = {}
        for s in self.steps:
            if s.amt == 16 or s.idx in referenced:
                counts[s.key] = counts.get(s.key, 0) + s.amt
                s.val = counts[s.key]
        owner = {c: self.QUEUE_OF.get(c, c) for c in self.CHANNELS}

        with nc.Block() as blk:
            def emit_for(name, eng):
                water = {}
                for s in self.steps:
                    if owner[s.ch] != name:
                        continue
                    for d in s.deps:
                        if s.ch == "T" and d.ch == "T":
                            continue
                        if water.get(d.key, 0) < d.val:
                            eng.wait_ge(sems[d.key], d.val)
                            water[d.key] = d.val
                    if s.amt == 16 and s.val > 16 and water.get(s.key, 0) < s.val - 16:
                        eng.wait_ge(sems[s.key], s.val - 16)
                        water[s.key] = s.val - 16
                    instr = s.fn(eng)
                    if s.val is not None:
                        instr.then_inc(sems[s.key], s.amt)

            @blk.tensor
            def _(t):
                emit_for("T", t)

            @blk.vector
            def _(v):
                emit_for("V", v)

            @blk.scalar
            def _(s):
                emit_for("S", s)

            @blk.gpsimd
            def _(g):
                emit_for("G", g)

            @blk.sync
            def _(y):
                emit_for("Y", y)


def build_program(dbg=0, serialize=False):
    nc = bass.Bass("TRN2", target_bir_lowering=False)
    S = Sched(serialize=serialize)

    ap = lambda t: t.ap()
    xkv = ap(nc.dram_tensor("xkv", [TKV, D], F32, kind="ExternalInput"))
    xq = ap(nc.dram_tensor("xq", [TQ, D], F32, kind="ExternalInput"))
    masks_in = ap(nc.dram_tensor("masks", [128, 16, 256], BF16, kind="ExternalInput"))
    ident_in = ap(nc.dram_tensor("ident", [128, 128], BF16, kind="ExternalInput"))
    wq = ap(nc.dram_tensor("wq", [D, D], F32, kind="ExternalInput"))
    bq_r = ap(nc.dram_tensor("bq_r", [128, 16], F32, kind="ExternalInput"))
    wk = ap(nc.dram_tensor("wk", [DK, D], F32, kind="ExternalInput"))
    bk_r = ap(nc.dram_tensor("bk_r", [128, 4], F32, kind="ExternalInput"))
    wv = ap(nc.dram_tensor("wv", [DK, D], F32, kind="ExternalInput"))
    bv_in = ap(nc.dram_tensor("bv", [1, DK], F32, kind="ExternalInput"))
    proj = ap(nc.dram_tensor("proj", [D, D], F32, kind="ExternalInput"))
    sw1 = ap(nc.dram_tensor("sw1", [2, D, DFF], F32, kind="ExternalInput"))
    sb1_r = ap(nc.dram_tensor("sb1_r", [128, NFC, 2], F32, kind="ExternalInput"))
    sw2 = ap(nc.dram_tensor("sw2", [2, DFF, D], F32, kind="ExternalInput"))
    sb2 = ap(nc.dram_tensor("sb2", [2, D], F32, kind="ExternalInput"))
    rw1 = ap(nc.dram_tensor("rw1", [NEXP, D, DFF], F32, kind="ExternalInput"))
    rb1_r = ap(nc.dram_tensor("rb1_r", [128, NFC, NEXP], F32, kind="ExternalInput"))
    rw2 = ap(nc.dram_tensor("rw2", [NEXP, DFF, D], F32, kind="ExternalInput"))
    rb2 = ap(nc.dram_tensor("rb2", [NEXP, D], F32, kind="ExternalInput"))
    cent_r = ap(nc.dram_tensor("cent_r", [D, NEXP], F32, kind="ExternalInput"))
    cbias = ap(nc.dram_tensor("cbias", [NEXP, 1], F32, kind="ExternalInput"))

    dscr = "ExternalOutput" if dbg else "Internal"
    scr_kv = ap(nc.dram_tensor("scr_kv", [TKV, D], BF16, kind=dscr))
    scr_q = ap(nc.dram_tensor("scr_q", [TQ, D], BF16, kind=dscr))
    scr_qlo = ap(nc.dram_tensor("scr_qlo", [TQ, D], BF16, kind="Internal"))
    s_scr = ap(nc.dram_tensor("s_scr", [TQ, D], BF16, kind=dscr))
    scr_w1 = ap(nc.dram_tensor("scr_w1", [18, NFC, 128, NDC * 128], BF16, kind="Internal"))
    scr_w2 = ap(nc.dram_tensor("scr_w2", [18, 8, 128, NFC * 256], BF16, kind="Internal"))
    out = ap(nc.dram_tensor("out", [TQ, D], F32, kind="ExternalOutput"))
    if dbg:
        dbg_kt = ap(nc.dram_tensor("dbg_kt", [128, 4, TKV], BF16, kind="ExternalOutput"))
        dbg_v = ap(nc.dram_tensor("dbg_v", [128, 32, DK], BF16, kind="ExternalOutput"))
        dbg_qs = ap(nc.dram_tensor("dbg_qs", [128, 4, TQ], BF16, kind="ExternalOutput"))
        dbg_gat = ap(nc.dram_tensor("dbg_gat", [128, 16, NEXP], F32, kind="ExternalOutput"))
        dbg_acc = ap(nc.dram_tensor("dbg_acc", [128, 16, D], F32, kind="ExternalOutput"))

    def DMA(q, dst, src, deps=()):
        return S.dma(q, lambda e, a=dst, b=src: e.dma_start(a, b), deps)

    def DMT(q, dst, src, deps=()):
        return S.dma(q, lambda e, a=dst, b=src: e.dma_start_transpose(a, b), deps)

    with ExitStack() as CTX:
        sems = {}
        for c in Sched.CHANNELS:
            nrot = Sched.NROT if c.startswith("d") else 1
            for i in range(nrot):
                sems[(c, i)] = CTX.enter_context(nc.semaphore("ch_%s_%d" % (c, i)))
        A = lambda name, shape, dt=F32: CTX.enter_context(nc.sbuf_tensor(name, shape, dt))

        ident = A("ident_sb", [128, 128], BF16)
        bqs = A("bqs_sb", [128, 4])
        bks = A("bks_sb", [128, 4])
        bvrow = A("bvrow_sb", [1, DK], BF16)
        ones1 = A("ones1_sb", [1, 128], BF16)
        sb1s = A("sb1_sb", [128, NFC, 2])
        rb1s = A("rb1_sb", [128, NFC, NEXP])
        b2ext = A("b2ext_sb", [16, D], BF16)
        sb2n = A("sb2n_sb", [2, D], BF16)
        ones2 = A("ones2_sb", [2, 128], BF16)
        centhi = A("centhi_sb", [128, NDC, NEXP], BF16)
        centlo = A("centlo_sb", [128, NDC, NEXP], BF16)
        cbias_s = A("cbias_sb", [16, 1])

        ATT = ExitStack()
        AA = lambda name, shape, dt=F32: ATT.enter_context(nc.sbuf_tensor(name, shape, dt))
        Pfold = AA("pfold_sb", [128, 4, D], BF16)
        WKT = AA("wkt_sb", [128, NDC, DK], BF16)
        WVT = AA("wvt_sb", [128, NDC, DK], BF16)
        QsumT = AA("qsumt_sb", [128, 4, TQ], BF16)

        psA = CTX.enter_context(nc.psum_tensor("psA", [128, 512], F32))
        psB = CTX.enter_context(nc.psum_tensor("psB", [128, 512], F32))
        psC = CTX.enter_context(nc.psum_tensor("psC", [128, 512], F32))
        psD = CTX.enter_context(nc.psum_tensor("psD", [128, 512], F32))
        psE = CTX.enter_context(nc.psum_tensor("psE", [128, 512], F32))
        psF = CTX.enter_context(nc.psum_tensor("psF", [128, 512], F32))
        ptr1 = CTX.enter_context(nc.psum_tensor("ptr1", [128, 1024], BF16))
        ptr2 = CTX.enter_context(nc.psum_tensor("ptr2", [128, 1024], BF16))
        psY = [psC, psD, psE, psF]

        TP = {"k": 0, "ev": [None, None]}

        def TRP(src_ap, outP, outF, id_ap, evac_maker, deps):
            k = TP["k"]
            TP["k"] += 1
            bank = [ptr1, ptr2][k % 2]
            pap = bank[0:outP, 0:outF]
            tst = S.op("T", lambda e, pap=pap, s=src_ap, ia=id_ap: e.transpose(pap, s, ia),
                       deps=list(deps) + [TP["ev"][k % 2]])
            ev = evac_maker(pap, tst)
            TP["ev"][k % 2] = ev
            return tst, ev

        P1wq = ExitStack()
        WQT = P1wq.enter_context(nc.sbuf_tensor("wqt_sb", [128, NDC, DK], BF16))
        P1a = ExitStack()
        A1 = lambda name, shape, dt=F32: P1a.enter_context(nc.sbuf_tensor(name, shape, dt))
        wstg = A1("wstg", [128, 2, D])
        wstg2 = A1("wstg2", [128, 2, D])
        wbfh = A1("wbfh", [128, 2, D], BF16)
        bqr_s = A1("bqr_s", [128, 16])
        centf = A1("centf", [128, NDC, NEXP])
        wbfh2 = A1("wbfh2", [128, 2, D], BF16)

        ld_ident = DMA("dY", ident[:, :], ident_in[:, :])
        ld_bqr = DMA("dY", bqr_s[:, :], bq_r[:, :])
        ld_bk = DMA("dY", bks[:, :], bk_r[:, :])
        ld_sb1 = DMA("dY", sb1s[:, :, :], sb1_r[:, :, :])
        ld_rb1 = DMA("dY", rb1s[:, :, :], rb1_r[:, :, :])
        ld_cb = DMA("dY", cbias_s[:, :], cbias[:, :])
        ld_centf = DMA("dY", centf[:, :, :], cent_r.rearrange("(dc p) j -> p dc j", p=128))
        ld_bv = DMA("dG", bvrow[:, :], bv_in[:, :])
        ld_rb2 = DMA("dG", b2ext[:, :], rb2[:, :])
        ld_sb2 = DMA("dG", sb2n[:, :], sb2[:, :])
        ld_centh = DMA("dG", centhi[:, :, :], cent_r.rearrange("(dc p) j -> p dc j", p=128))

        s1 = S.op("V", lambda e: e.tensor_add(bqs[:, :], bqr_s[:, 0:4], bqr_s[:, 4:8]), deps=[ld_bqr])
        s2 = S.op("V", lambda e: e.tensor_add(bqs[:, :], bqs[:, :], bqr_s[:, 8:12]), deps=[s1])
        bqs_done = S.op("V", lambda e: e.tensor_add(bqs[:, :], bqs[:, :], bqr_s[:, 12:16]), deps=[s2])
        clo = S.op("V", lambda e: e.tensor_sub(centlo[:, :, :], centf[:, :, :], centhi[:, :, :]),
                   deps=[ld_centf, ld_centh])
        ones_set = S.op("V", lambda e: e.memset(ones1[:, :], 1.0))
        ones2_set = S.op("V", lambda e: e.memset(ones2[:, :], 1.0))

        def fold_weight(src_dram, out_cvt_fn, prev_users):
            outs = []
            prevA = prev_users
            for ah in range(2):
                l0 = DMA("dG", wstg[:, :, :],
                         src_dram[256 * ah:256 * ah + 256, :].rearrange("(a p) d -> p a d", p=128),
                         deps=prevA)
                fold = l0
                addstep = None
                for r in range(1, 4):
                    lb = DMA("dG", wstg2[:, :, :],
                             src_dram[512 * r + 256 * ah:512 * r + 256 * ah + 256, :]
                             .rearrange("(a p) d -> p a d", p=128),
                             deps=[addstep] + prevA)
                    addstep = S.op("V", lambda e: e.tensor_add(wstg[:, :, :], wstg[:, :, :],
                                                              wstg2[:, :, :]),
                                   deps=[l0, lb, addstep])
                outs.append(out_cvt_fn(ah, addstep))
                prevA = [outs[-1][0]]
            return outs

        def emit_tr(srcbuf, a_local, ah, dst, dstcol_base, dep):
            evs = []
            for dc in range(NDC):
                def mk(pap, tst, d=dc, ds=dst, cb=dstcol_base):
                    return S.op("V", lambda e: e.tensor_copy(ds[:, d, cb:cb + 128], pap),
                                deps=[tst])
                tst, ev = TRP(srcbuf[:, a_local, dc * 128:(dc + 1) * 128], 128, 128,
                              ident[:, :], mk, [dep, ld_ident])
                evs.append(ev)
            return evs

        def wq_cvt(ah, fold_step):
            cvt = S.op("V", lambda e: e.tensor_copy(wbfh[:, :, :], wstg[:, :, :]), deps=[fold_step])
            evs = []
            for al in range(2):
                evs += emit_tr(wbfh, al, ah, WQT, (2 * ah + al) * 128, cvt)
            return [evs[-1], cvt] + evs

        wq_outs = fold_weight(wq, wq_cvt, [])
        wqt_done = wq_outs[-1][0]
        wq_users = [s for o in wq_outs for s in o]

        def pf_cvt(ah, fold_step):
            cvt = S.op("V", lambda e, ah=ah: e.tensor_scalar(
                Pfold[:, 2 * ah:2 * ah + 2, :], wstg[:, :, :], RSQ128, None, ALU.mult),
                deps=[fold_step])
            return [cvt]

        pf_outs = fold_weight(proj, pf_cvt, wq_users)
        cvt_pf = pf_outs[-1][0]


        kv_tr_last = {}
        prev_half_load = None
        for (wsrc, dst) in ((wk, WKT), (wv, WVT)):
            for ah in range(2):
                lh = DMA("dG", wbfh2[:, :, :],
                         wsrc[256 * ah:256 * ah + 256, :].rearrange("(a p) d -> p a d", p=128),
                         deps=[prev_half_load])
                evs = []
                for al in range(2):
                    evs += emit_tr(wbfh2, al, ah, dst, (2 * ah + al) * 128, lh)
                prev_half_load = evs[-1]
                kv_tr_last[id(dst)] = evs[-1]
        wkt_done = kv_tr_last[id(WKT)]
        wvt_done = kv_tr_last[id(WVT)]

        scrw1_done = {}
        scrw2_done = {}

        bar_snap = S.snapshot()
        bar_steps = [S.op(e, lambda eng: eng.nop(), deps=bar_snap) for e in ("T", "V", "S", "Y")]

        P1a.close()

        PHq = ExitStack()
        A2 = lambda name, shape, dt=F32: PHq.enter_context(nc.sbuf_tensor(name, shape, dt))
        xt = [A2("xt0", [128, D]), A2("xt1", [128, D])]
        stats = [A2("st0", [128, 8]), A2("st1", [128, 8])]
        hib = [A2("hib0", [128, D], BF16), A2("hib1", [128, D], BF16)]
        lob = [A2("lob0", [128, D], BF16), A2("lob1", [128, D], BF16)]
        xfb = [A2("xfb0", [128, D]), A2("xfb1", [128, D])]
        xnWq = A2("xnWq", [128, NDC, 2, 512], BF16)

        def ln_tile(bufs, src_ap, row0, b, want_lo, prevu, dst_scr, prev_lo):
            xt, stats, hib, lob, xfb = bufs
            xb = xt[b]
            hb = hib[b]
            xfb_b = xfb[b] if xfb is not None else None
            lob_b = lob[b] if lob is not None else None
            dummy = xfb_b if xfb_b is not None else hb
            ld = DMA("dY", xb[:, :], src_ap[row0:row0 + 128, :], deps=prevu)
            st = stats[b]
            r1 = S.op("V", lambda e, xb=xb: e.reduce_sum(st[:, 0:1], xb[:, :], axis=AX.X), deps=[ld])
            sq = S.op("S", lambda e, xb=xb, dummy=dummy: e.activation(dummy[:, :], xb[:, :], AF.Square,
                                                                     accum_out=st[:, 4:5]),
                      deps=[ld] + prevu + prev_lo)
            m = S.op("V", lambda e: e.tensor_scalar(st[:, 1:2], st[:, 0:1], float(1.0 / D), None, ALU.mult),
                     deps=[r1])
            v2 = S.op("V", lambda e: e.tensor_scalar(st[:, 2:3], st[:, 4:5], float(1.0 / D), 1e-5,
                                                     ALU.mult, ALU.add), deps=[sq])
            mm = S.op("V", lambda e: e.tensor_mul(st[:, 3:4], st[:, 1:2], st[:, 1:2]), deps=[m])
            vv = S.op("V", lambda e: e.tensor_sub(st[:, 2:3], st[:, 2:3], st[:, 3:4]), deps=[v2, mm])
            lnv = S.op("S", lambda e: e.activation(st[:, 7:8], st[:, 2:3], AF.Ln), deps=[vv])
            rstd = S.op("S", lambda e: e.activation(st[:, 5:6], st[:, 7:8], AF.Exp, scale=-0.5), deps=[lnv])
            nmr = S.op("V", lambda e: e.tensor_scalar(st[:, 6:7], st[:, 1:2], st[:, 5:6], -1.0,
                                                      ALU.mult, ALU.mult), deps=[rstd])
            hi = S.op("S", lambda e, xb=xb, hb=hb: e.activation(hb[:, :], xb[:, :], AF.Identity,
                                                                bias=st[:, 6:7], scale=st[:, 5:6]),
                      deps=[nmr] + prevu)
            lo_store = None
            lo_users = []
            xf = lo = None
            if want_lo:
                xf = S.op("S", lambda e, xb=xb, xfb_b=xfb_b: e.activation(
                    xfb_b[:, :], xb[:, :], AF.Identity,
                    bias=st[:, 6:7], scale=st[:, 5:6]),
                    deps=[nmr, sq] + prev_lo)
                lo = S.op("V", lambda e, hb=hb, xfb_b=xfb_b, lob_b=lob_b: e.tensor_sub(
                    lob_b[:, :], xfb_b[:, :], hb[:, :]),
                    deps=[xf, hi] + prev_lo)
                lo_store = DMA("dY", scr_qlo[row0:row0 + 128, :], lob_b[:, :], deps=[lo])
                lo_users = [xf, lo, lo_store]
            hi_store = DMA("dY", dst_scr[row0:row0 + 128, :], hb[:, :], deps=[hi])
            last = [hi_store, sq, hi, r1]
            if want_lo:
                last += [xf, lo]
            return hi_store, lo_store, last, lo_users, hi

        q_hi = []
        q_lo = []
        prev_lo = [[], []]
        prevu = [list(bar_steps), list(bar_steps)]
        qwin_users = [[], []]
        pq_ev = [None, None]
        qev_all = []
        for tc in range(4):
            w = tc % 2
            evs_this = []
            for kk in range(4):
                k = tc * 4 + kk
                hs, ls, lu, lou, hi_st = ln_tile((xt, stats, hib, lob, xfb), xq, 128 * k, k % 2,
                                                 True, prevu[k % 2], scr_q, prev_lo[k % 2])
                q_hi.append(hs)
                q_lo.append(ls)
                hb = hib[k % 2]
                tsts = []
                for dc in range(NDC):
                    def mkev(pap, tst, dc=dc, w=w, kk=kk):
                        return S.op("V", lambda e, pap=pap, dc=dc, w=w, kk=kk: e.tensor_copy(
                            xnWq[:, dc, w, kk * 128:(kk + 1) * 128], pap),
                            deps=[tst] + qwin_users[w])
                    tst, ev = TRP(hb[:, dc * 128:(dc + 1) * 128], 128, 128, ident[:, :],
                                  mkev, [hi_st, ld_ident])
                    tsts.append(tst)
                    evs_this.append(ev)
                prevu[k % 2] = lu + [tsts[-1]]
                prev_lo[k % 2] = lou
            users = []
            for a in range(4):
                g = a % 2
                ps = [psA, psB][g]

                def qfn(e, a=a, w=w, ps=ps):
                    for dc in range(NDC):
                        ins = e.matmul(ps[:, :], WQT[:, dc, a * 128:(a + 1) * 128],
                                       xnWq[:, dc, w, :], start=(dc == 0), stop=(dc == NDC - 1))
                    return ins
                mmst = S.op("T", qfn, deps=evs_this + [wqt_done, pq_ev[g]])
                ev = S.op("S", lambda e, a=a, tc=tc, ps=ps: e.activation(
                    QsumT[:, a, tc * 512:(tc + 1) * 512], ps[:, :], AF.Identity,
                    bias=bqs[:, a:a + 1]), deps=[mmst, bqs_done])
                pq_ev[g] = ev
                qev_all.append(ev)
                users.append(mmst)
            qwin_users[w] = users

        bar_snap = S.snapshot()
        bar_steps = [S.op(e, lambda eng: eng.nop(), deps=bar_snap) for e in ("T", "V", "S", "Y")]
        PHq.close()
        P1wq.close()


        ATT2 = ExitStack()
        AA2 = lambda name, shape, dt=F32: ATT2.enter_context(nc.sbuf_tensor(name, shape, dt))
        KT = AA2("kt_sb", [128, 4, TKV], BF16)
        Vsb = AA2("v_sb", [128, 32, DK], BF16)
        PHkv = ExitStack()
        A2 = lambda name, shape, dt=F32: PHkv.enter_context(nc.sbuf_tensor(name, shape, dt))
        xt = [A2("kxt0", [128, D]), A2("kxt1", [128, D])]
        stats = [A2("kst0", [128, 8]), A2("kst1", [128, 8])]
        hib = [A2("khib0", [128, D], BF16), A2("khib1", [128, D], BF16)]
        xnWk = A2("xnWk", [128, NDC, 2, 512], BF16)

        kv_hi = []
        prevu = [list(bar_steps), list(bar_steps)]
        win_users = [bar_steps, bar_steps]
        pk_ev = [qev_all[-1], qev_all[-1]]
        pv_ev = [None, None]
        k_ev_all = []
        v_ev_all = []
        for tc in range(8):
            w = tc % 2
            evs_this = []
            for kk in range(4):
                k = tc * 4 + kk
                hs, _, lu, _, hi_st = ln_tile((xt, stats, hib, None, None), xkv, 128 * k,
                                              k % 2, False, prevu[k % 2], scr_kv, [])
                kv_hi.append(hs)
                hb = hib[k % 2]
                tsts = []
                for dc in range(NDC):
                    def mkev(pap, tst, dc=dc, w=w, kk=kk):
                        return S.op("V", lambda e, pap=pap, dc=dc, w=w, kk=kk: e.tensor_copy(
                            xnWk[:, dc, w, kk * 128:(kk + 1) * 128], pap),
                            deps=[tst] + list(win_users[w]))
                    tst, ev = TRP(hb[:, dc * 128:(dc + 1) * 128], 128, 128, ident[:, :],
                                  mkev, [hi_st, ld_ident])
                    tsts.append(tst)
                    evs_this.append(ev)
                prevu[k % 2] = lu + [tsts[-1]]
            users = []
            for a in range(4):
                g = a % 2
                ps = [psA, psB][g]

                def kfn(e, a=a, w=w, ps=ps):
                    for dc in range(NDC):
                        ins = e.matmul(ps[:, :], WKT[:, dc, a * 128:(a + 1) * 128],
                                       xnWk[:, dc, w, :], start=(dc == 0), stop=(dc == NDC - 1))
                    return ins
                mmst = S.op("T", kfn, deps=evs_this + [wkt_done, pk_ev[g]])
                ev = S.op("S", lambda e, a=a, tc=tc, ps=ps: e.activation(
                    KT[:, a, tc * 512:(tc + 1) * 512], ps[:, :], AF.Identity,
                    bias=bks[:, a:a + 1]), deps=[mmst, ld_bk])
                pk_ev[g] = ev
                k_ev_all.append(ev)
                users.append(mmst)
            for sub in range(4):
                g = sub % 2
                ps = [psC, psD][g]
                kb = tc * 4 + sub

                def vfn(e, sub=sub, w=w, ps=ps):
                    for dc in range(NDC):
                        e.matmul(ps[:, :], xnWk[:, dc, w, sub * 128:(sub + 1) * 128],
                                 WVT[:, dc, :], start=(dc == 0), stop=False)
                    return e.matmul(ps[:, :], ones1[:, :], bvrow[:, :], start=False, stop=True)
                mmst = S.op("T", vfn, deps=evs_this + [wvt_done, ld_bv, ones_set, pv_ev[g]])
                ev = S.op("V", lambda e, kb=kb, ps=ps: e.tensor_copy(Vsb[:, kb, :], ps[:, :]),
                          deps=[mmst])
                pv_ev[g] = ev
                v_ev_all.append(ev)
                users.append(mmst)
            win_users[w] = users

        if dbg:
            d1 = DMA("dY", dbg_kt[:, :, :], KT[:, :, :], deps=k_ev_all)
            d2 = DMA("dY", dbg_v[:, :, :], Vsb[:, :, :], deps=v_ev_all)
            d3 = DMA("dY", dbg_qs[:, :, :], QsumT[:, :, :], deps=qev_all)
            dbg_proj = [d1, d2, d3]
        else:
            dbg_proj = []

        bar_snap = S.snapshot()
        bar_steps = [S.op(e, lambda eng: eng.nop(), deps=bar_snap) for e in ("T", "V", "S", "Y")]
        PHkv.close()


        P4 = ExitStack()
        A4 = lambda name, shape, dt=F32: P4.enter_context(nc.sbuf_tensor(name, shape, dt))
        masks = A4("masks_sb", [128, 16, 256], BF16)
        srow = A4("srow", [128, TKV])
        Pbf = A4("pbf", [128, TKV], BF16)
        PTb = A4("ptb", [128, 32, 128], BF16)
        Zbf = A4("zbf", [128, 512], BF16)
        ZTb = A4("ztb", [128, 4, 128], BF16)
        stile = [A4("stile0", [128, D], BF16), A4("stile1", [128, D], BF16)]
        smax = A4("smax", [128, 2])
        ssum = A4("ssum", [128, 2])

        ld_masks = DMA("dY", masks[:, :, :], masks_in[:, :, :], deps=bar_steps)

        psS_ev = [None, None]
        s_stores = [None, None]
        prev_srow_reader = None
        prev_pbf_reader = None
        prev_rcp = None
        prev_av = None
        prev_zread = None
        psZ_ev = None
        psSP_ev = [None, None]
        s_store_list = []

        for i in range(16):
            nkeys = 256 * (i + 1)
            nkb = nkeys // 128
            blocks = []
            off = 0
            while off < nkeys:
                ln_ = min(512, nkeys - off)
                blocks.append((off, ln_))
                off += ln_
            ev_blocks = []
            for bi, (off, ln_) in enumerate(blocks):
                g = bi % 2
                ps = [psA, psB][g]

                def scfn(e, i=i, off=off, ln_=ln_, ps=ps):
                    for a in range(4):
                        ins = e.matmul(ps[:, 0:ln_], QsumT[:, a, i * 128:(i + 1) * 128],
                                       KT[:, a, off:off + ln_], start=(a == 0), stop=(a == 3))
                    return ins
                tc_max = (off + ln_ - 1) // 512
                mmst = S.op("T", scfn,
                            deps=[psS_ev[g]] + qev_all[:4 * (i // 4 + 1)]
                            + k_ev_all[:4 * (tc_max + 1)] + dbg_proj)
                ev = S.op("V", lambda e, off=off, ln_=ln_, ps=ps: e.tensor_copy(
                    srow[:, off:off + ln_], ps[:, 0:ln_]),
                    deps=[mmst, prev_srow_reader])
                psS_ev[g] = ev
                ev_blocks.append(ev)
            mk = S.op("V", lambda e, i=i, nk=nkeys: e.tensor_add(
                srow[:, nk - 256:nk], srow[:, nk - 256:nk], masks[:, i, :]),
                deps=[ev_blocks[-1], ld_masks])
            nmx = S.op("V", lambda e, nk=nkeys: e.reduce_max(
                smax[:, 0:1], srow[:, 0:nk], axis=AX.X, negate=True),
                deps=[mk, prev_pbf_reader] + ev_blocks)
            exps = S.op("S", lambda e, nk=nkeys: e.activation(
                Pbf[:, 0:nk], srow[:, 0:nk], AF.Exp, bias=smax[:, 0:1],
                accum_out=ssum[:, 0:1]),
                deps=[nmx, prev_pbf_reader, prev_rcp])
            rcp = S.op("V", lambda e: e.reciprocal(ssum[:, 1:2], ssum[:, 0:1]),
                       deps=[exps, prev_zread])
            prev_rcp = rcp
            pt_evs = []
            last_pt_t = None
            for kb in range(nkb):
                def mk(pap, tst, kb=kb):
                    return S.op("V", lambda e: e.tensor_copy(PTb[:, kb, :], pap),
                                deps=[tst, prev_av])
                tst, evt = TRP(Pbf[:, kb * 128:(kb + 1) * 128], 128, 128, ident[:, :],
                               mk, [exps, prev_av])
                pt_evs.append(evt)
                last_pt_t = tst

            def avfn(e, nkb=nkb):
                for kb in range(nkb):
                    ins = e.matmul(psC[:, :], PTb[:, kb, :], Vsb[:, kb, :],
                                   start=(kb == 0), stop=(kb == nkb - 1))
                return ins
            av = S.op("T", avfn,
                      deps=pt_evs + v_ev_all[:4 * ((nkeys - 1) // 512 + 1)] + [psZ_ev])
            prev_av = av
            prev_pbf_reader = last_pt_t
            prev_srow_reader = exps
            zev = S.op("V", lambda e: e.tensor_scalar(
                Zbf[:, :], psC[:, :], ssum[:, 1:2], None, ALU.mult),
                deps=[av, rcp, prev_zread])
            psZ_ev = zev
            zt_evs = []
            zt_ts = []
            for c in range(4):
                def mk(pap, tst, c=c):
                    return S.op("V", lambda e: e.tensor_copy(ZTb[:, c, :], pap),
                                deps=[tst, prev_zread])
                tst, evt = TRP(Zbf[:, c * 128:(c + 1) * 128], 128, 128, ident[:, :],
                               mk, [zev, prev_zread])
                zt_evs.append(evt)
                zt_ts.append(tst)
            prev_zread = zt_ts[-1]
            sb = i % 2
            sp_evs = []
            for dg in range(4):
                g = dg % 2
                ps = [psE, psF][g]

                def spfn(e, dg=dg, ps=ps):
                    for c in range(4):
                        ins = e.matmul(ps[:, :], ZTb[:, c, :], Pfold[:, c, dg * 512:(dg + 1) * 512],
                                       start=(c == 0), stop=(c == 3))
                    return ins
                mmst = S.op("T", spfn, deps=zt_evs + [cvt_pf, psSP_ev[g]])
                ev = S.op("S", lambda e, dg=dg, ps=ps, sb=sb: e.activation(
                    stile[sb][:, dg * 512:(dg + 1) * 512], ps[:, :], AF.Copy),
                    deps=[mmst, s_stores[sb]])
                psSP_ev[g] = ev
                sp_evs.append(ev)
            sst = DMA("dY", s_scr[i * 128:(i + 1) * 128, :], stile[sb][:, :], deps=sp_evs)
            s_stores[sb] = sst
            s_store_list.append(sst)

        bar_snap = S.snapshot()
        bar_steps = [S.op(e, lambda eng: eng.nop(), deps=bar_snap) for e in ("T", "V", "S", "Y")]
        P4.close()
        ATT2.close()
        ATT.close()

        P5 = ExitStack()
        A5 = lambda name, shape, dt=F32: P5.enter_context(nc.sbuf_tensor(name, shape, dt))
        xnqTh = A5("xnqTh", [128, NDC, HALF], BF16)
        lowin = A5("lowin", [128, 2, HALF], BF16)
        acc = A5("acc", [128, 8, D])
        hT = A5("hT", [128, NFC, HALF], BF16)
        w2b = A5("w2b", [128, NFC, 2, 512], BF16)
        w1b = [A5("w1b0", [128, NDC, 128], BF16), A5("w1b1", [128, NDC, 128], BF16)]
        sct = A5("sct", [16, HALF])
        schi = A5("schi", [16, 128], BF16)
        sclo = A5("sclo", [16, 128], BF16)
        schiT = A5("schiT", [128, 16])
        sc32 = A5("sc32", [128, 16])
        rnk = A5("rnk", [128, 16])
        mskb = A5("mskb", [128, 16])
        graw = A5("graw", [128, 16])
        den = A5("den", [128, 2])
        gating = A5("gating", [128, 8, NEXP])
        gbf = A5("gbf", [128, NEXP], BF16)
        gT = A5("gt", [16, 8, 128], BF16)
        xnqrl = [A5("xnqrl0", [128, D], BF16), A5("xnqrl1", [128, D], BF16)]
        xlorl = A5("xlorl", [128, D], BF16)
        xqrl = A5("xqrl", [128, D])
        srl = A5("srl", [128, D], BF16)
        gsil = A5("gsil", [128, D], BF16)
        fin = A5("fin", [128, D])

        b0 = bar_steps[0]
        U = {
            "xnqTh": list(bar_steps), "lowin": [b0, b0], "sct": [b0],
            "schi": b0, "sclo": b0, "schiT": b0, "sc32": b0, "rnk": b0,
            "mskb": b0, "graw": b0, "den": b0, "gbf": b0,
            "gat": [b0] * 8, "gT": [b0] * 8,
            "acc": {}, "hT": [b0], "w2b": [[b0], [b0]], "w1b": [[b0], [b0]],
            "xnqrl": [b0, b0], "xlorl": b0, "xqrl": b0,
            "srl": b0, "gsil": b0, "fin": b0,
            "psH": [bar_steps[1], bar_steps[1]], "psY": [bar_steps[1]] * 4,
            "psR": [bar_steps[1], bar_steps[1]],
        }
        out_stores = []

        for half in range(2):
            hoff = half * HALF
            trh = []
            for dc in range(NDC):
                t = DMT("dS", xnqTh[:, dc, :],
                        scr_q[hoff:hoff + HALF, dc * 128:(dc + 1) * 128],
                        deps=q_hi + U["xnqTh"])
                trh.append(t)

            rt_steps = []
            for dc in range(NDC):
                tl = DMT("dS", lowin[:, dc % 2, :],
                         scr_qlo[hoff:hoff + HALF, dc * 128:(dc + 1) * 128],
                         deps=q_lo + [U["lowin"][dc % 2]])

                def rtfn(e, dc=dc):
                    for sub in range(2):
                        ps = [psE, psF][sub]
                        e.matmul(ps[0:16, :], centhi[:, dc, :],
                                 xnqTh[:, dc, sub * 512:(sub + 1) * 512],
                                 start=(dc == 0), stop=False)
                        e.matmul(ps[0:16, :], centlo[:, dc, :],
                                 xnqTh[:, dc, sub * 512:(sub + 1) * 512],
                                 start=False, stop=False)
                        ins = e.matmul(ps[0:16, :], centhi[:, dc, :],
                                       lowin[:, dc % 2, sub * 512:(sub + 1) * 512],
                                       start=False, stop=(dc == NDC - 1))
                    return ins
                st = S.op("T", rtfn, deps=[trh[dc], tl, clo, ld_centh] + U["psR"])
                U["lowin"][dc % 2] = st
                rt_steps.append(st)
            rt_evs = []
            for sub in range(2):
                ps = [psE, psF][sub]
                ev = S.op("S", lambda e, sub=sub, ps=ps: e.activation(
                    sct[:, sub * 512:(sub + 1) * 512], ps[0:16, :], AF.Sigmoid),
                    deps=[rt_steps[-1]] + U["sct"])
                rt_evs.append(ev)
            badd = S.op("V", lambda e: e.tensor_scalar_add(sct[:, :], sct[:, :], cbias_s[:, 0:1]),
                        deps=rt_evs + [ld_cb])
            U["psY"][2] = rt_evs[0]
            U["psY"][3] = rt_evs[1]

            gat_steps = []
            gT_steps = []
            c2 = None
            for tt in range(8):
                c1 = S.op("V", lambda e, tt=tt: e.tensor_copy(schi[:, :], sct[:, tt * 128:(tt + 1) * 128]),
                          deps=[badd, U["schi"], c2])
                c2 = S.op("V", lambda e, tt=tt: e.tensor_sub(
                    sclo[:, :], sct[:, tt * 128:(tt + 1) * 128], schi[:, :]),
                    deps=[c1, U["sclo"]])
                def mk1(pap, tst):
                    return S.op("V", lambda e, pap=pap: e.tensor_copy(schiT[:, :], pap),
                                deps=[tst, U["schiT"]])
                t1, e1 = TRP(schi[:, :], 128, 16, ident[0:16, 0:16], mk1, [c1])

                def mk2(pap, tst):
                    return S.op("V", lambda e, pap=pap: e.tensor_add(sc32[:, :], schiT[:, :], pap),
                                deps=[tst, e1, U["sc32"]])
                t2, e2 = TRP(sclo[:, :], 128, 16, ident[0:16, 0:16], mk2, [c2])
                U["schiT"] = e2
                U["schi"] = t1
                U["sclo"] = t2
                prev = S.op("V", lambda e: e.tensor_scalar(rnk[:, :], sc32[:, :], sc32[:, 0:1],
                                                           None, ALU.is_lt),
                            deps=[e2, U["rnk"]])
                for j in range(1, NEXP):
                    prev = S.op("V", lambda e, j=j: e.scalar_tensor_tensor(
                        rnk[:, :], sc32[:, :], sc32[:, j:j + 1], rnk[:, :], ALU.is_lt, ALU.add),
                        deps=[prev])
                mk1 = S.op("V", lambda e: e.tensor_scalar(mskb[:, :], rnk[:, :], 3.5, None, ALU.is_lt),
                           deps=[prev, U["mskb"]])
                gr = S.op("V", lambda e: e.tensor_mul(graw[:, :], sc32[:, :], mskb[:, :]),
                          deps=[mk1, U["graw"]])
                U["sc32"] = gr
                U["rnk"] = mk1
                ds = S.op("V", lambda e: e.reduce_sum(den[:, 0:1], graw[:, :], axis=AX.X),
                          deps=[gr, U["den"]])
                dm = S.op("V", lambda e: e.tensor_scalar_max(den[:, 1:2], den[:, 0:1], 1e-8),
                          deps=[ds])
                dr = S.op("V", lambda e: e.reciprocal(den[:, 0:1], den[:, 1:2]), deps=[dm])
                U["den"] = dr
                ga = S.op("V", lambda e, tt=tt: e.tensor_scalar(
                    gating[:, tt, :], graw[:, :], den[:, 0:1], None, ALU.mult),
                    deps=[dr, U["gat"][tt]])
                U["mskb"] = ga
                U["graw"] = ga
                gb = S.op("V", lambda e, tt=tt: e.tensor_copy(gbf[:, :], gating[:, tt, :]),
                          deps=[ga, U["gbf"]])
                def mk3(pap, tst, tt=tt):
                    return S.op("V", lambda e, pap=pap: e.tensor_copy(gT[:, tt, :], pap),
                                deps=[tst, U["gT"][tt]])
                t3, e3 = TRP(gbf[:, :], 16, 128, ident[:, :], mk3, [gb])
                U["gbf"] = t3
                gat_steps.append(ga)
                gT_steps.append(e3)
                U["gat"][tt] = ga
                U["gT"][tt] = e3

            U["sct"] = [c2]
            if dbg:
                dgat = DMA("dY", dbg_gat[:, half * 8:(half + 1) * 8, :], gating[:, :, :],
                           deps=gat_steps)
                for tt in range(8):
                    U["gat"][tt] = dgat

            def expert_pass(jconv, b1s, b1col, mode, gcol):
                w1_src = sw1[jconv - 16] if jconv >= 16 else rw1[jconv]
                w2_src = sw2[jconv - 16] if jconv >= 16 else rw2[jconv]
                h_evs = []
                for fs in range(NFC):
                    wb = w1b[fs % 2]
                    if half == 0:
                        w1f = w1_src[:, fs * 128:(fs + 1) * 128].rearrange(
                            "(dc p) f -> p dc f", p=128)
                        dma1 = DMA("dG", wb[:, 0:8, :], w1f[:, 0:8, :],
                                   deps=U["w1b"][fs % 2])
                        dma2 = DMA("dG", wb[:, 8:16, :], w1f[:, 8:16, :],
                                   deps=U["w1b"][fs % 2])
                        scrw = DMA("dS",
                                   scr_w1[jconv, fs].rearrange("p (dc f) -> p dc f", f=128),
                                   wb[:, :, :], deps=[dma1, dma2])
                        scrw1_done[(jconv, fs)] = scrw
                    else:
                        w1sl = scr_w1[jconv, fs].rearrange("p (dc f) -> p dc f", f=128)
                        q = "dG" if fs % 2 == 0 else "dY"
                        dma1 = DMA(q, wb[:, :, :], w1sl,
                                   deps=U["w1b"][fs % 2] + [scrw1_done[(jconv, fs)]])
                        dma2 = None
                        scrw = None
                    musers = [] if scrw is None else [scrw]
                    for sub in range(2):
                        g = (fs * 2 + sub) % 2
                        ps = [psA, psB][g]

                        def hfn(e, wb=wb, sub=sub, ps=ps):
                            for dc in range(NDC):
                                ins = e.matmul(ps[:, :], wb[:, dc, :],
                                               xnqTh[:, dc, sub * 512:(sub + 1) * 512],
                                               start=(dc == 0), stop=(dc == NDC - 1))
                            return ins
                        mmst = S.op("T", hfn,
                                    deps=[dma1, dma2, U["psH"][g]] + trh + U["hT"])
                        ev = S.op("S", lambda e, fs=fs, sub=sub, ps=ps, b1s=b1s, b1col=b1col: e.activation(
                            hT[:, fs, sub * 512:(sub + 1) * 512], ps[:, :], AF.Relu,
                            bias=b1s[:, fs, b1col:b1col + 1]),
                            deps=[mmst, ld_sb1, ld_rb1])
                        U["psH"][g] = ev
                        h_evs.append(ev)
                        musers.append(mmst)
                    U["w1b"][fs % 2] = musers
                y_last_mm = None
                for qd in range(4):
                    qb = qd % 2
                    if half == 0:
                        w2f = w2_src[:, qd * 512:(qd + 1) * 512].rearrange(
                            "(fc p) d -> p fc d", p=128)
                        wd1 = DMA("dG", w2b[:, 0:6, qb, :], w2f[:, 0:6, :],
                                  deps=U["w2b"][qb])
                        wd2 = DMA("dG", w2b[:, 6:11, qb, :], w2f[:, 6:11, :],
                                  deps=U["w2b"][qb])
                        sc1 = DMA("dS",
                                  scr_w2[jconv, 2 * qd].rearrange("p (fc x) -> p fc x", x=256),
                                  w2b[:, :, qb, 0:256], deps=[wd1, wd2])
                        sc2 = DMA("dY",
                                  scr_w2[jconv, 2 * qd + 1].rearrange("p (fc x) -> p fc x", x=256),
                                  w2b[:, :, qb, 256:512], deps=[wd1, wd2])
                        scrw2_done[(jconv, 2 * qd)] = sc1
                        scrw2_done[(jconv, 2 * qd + 1)] = sc2
                        yusers = [sc1, sc2]
                    else:
                        wd1 = DMA("dG", w2b[:, :, qb, 0:256],
                                  scr_w2[jconv, 2 * qd].rearrange("p (fc x) -> p fc x", x=256),
                                  deps=U["w2b"][qb] + [scrw2_done[(jconv, 2 * qd)]])
                        wd2 = DMA("dY", w2b[:, :, qb, 256:512],
                                  scr_w2[jconv, 2 * qd + 1].rearrange("p (fc x) -> p fc x", x=256),
                                  deps=U["w2b"][qb] + [scrw2_done[(jconv, 2 * qd + 1)]])
                        yusers = []
                    for tt in range(8):
                        bk = (qd * 8 + tt) % 4
                        ps = psY[bk]
                        extras = (mode == "init")
                        if extras:
                            rl1 = DMA("dY", xnqrl[tt % 2][:, :],
                                      scr_q[hoff + tt * 128:hoff + (tt + 1) * 128, :],
                                      deps=[U["xnqrl"][tt % 2]] + q_hi)
                            rl2 = DMA("dY", xlorl[:, :],
                                      scr_qlo[hoff + tt * 128:hoff + (tt + 1) * 128, :],
                                      deps=[U["xlorl"]] + q_lo)

                        def yfn(e, qd=qd, qb=qb, tt=tt, ps=ps, extras=extras):
                            for fs in range(NFC):
                                ins = e.matmul(ps[:, :], hT[:, fs, tt * 128:(tt + 1) * 128],
                                               w2b[:, fs, qb, :],
                                               start=(fs == 0), stop=(fs == NFC - 1 and not extras))
                            if extras:
                                e.matmul(ps[:, :], gT[:, tt, :], b2ext[:, qd * 512:(qd + 1) * 512],
                                         start=False, stop=False)
                                e.matmul(ps[:, :], ones2[:, :], sb2n[:, qd * 512:(qd + 1) * 512],
                                         start=False, stop=False)
                                e.matmul(ps[:, :], ident[:, :], xnqrl[tt % 2][:, qd * 512:(qd + 1) * 512],
                                         start=False, stop=False)
                                ins = e.matmul(ps[:, :], ident[:, :], xlorl[:, qd * 512:(qd + 1) * 512],
                                               start=False, stop=True)
                            return ins
                        deps = h_evs + [wd1, wd2, U["psY"][bk]]
                        if extras:
                            deps += [gT_steps[tt], ld_rb2, ld_sb2, ones2_set, rl1, rl2, ld_ident]
                        mmst = S.op("T", yfn, deps=deps)
                        if extras:
                            U["xnqrl"][tt % 2] = mmst
                            U["xlorl"] = mmst
                        accdep = U["acc"].get((tt, qd))
                        if mode == "init":
                            ev = S.op("V", lambda e, tt=tt, qd=qd, ps=ps: e.tensor_copy(
                                acc[:, tt, qd * 512:(qd + 1) * 512], ps[:, :]),
                                deps=[mmst, accdep])
                        elif mode == "add":
                            ev = S.op("V", lambda e, tt=tt, qd=qd, ps=ps: e.scalar_tensor_tensor(
                                acc[:, tt, qd * 512:(qd + 1) * 512], ps[:, :], 1.0,
                                acc[:, tt, qd * 512:(qd + 1) * 512], ALU.mult, ALU.add),
                                deps=[mmst, accdep])
                        else:
                            ev = S.op("V", lambda e, tt=tt, qd=qd, ps=ps, gcol=gcol: e.scalar_tensor_tensor(
                                acc[:, tt, qd * 512:(qd + 1) * 512], ps[:, :],
                                gating[:, tt, gcol:gcol + 1],
                                acc[:, tt, qd * 512:(qd + 1) * 512], ALU.mult, ALU.add),
                                deps=[mmst, accdep, gat_steps[tt]])
                        U["acc"][(tt, qd)] = ev
                        U["psY"][bk] = ev
                        yusers.append(mmst)
                        y_last_mm = mmst
                    U["w2b"][qb] = yusers
                U["hT"] = [y_last_mm]
                return y_last_mm

            init_last = expert_pass(16, sb1s, 0, "init", None)
            for tt in range(8):
                U["gT"][tt] = init_last
            expert_pass(17, sb1s, 1, "add", None)
            for j in range(NEXP):
                expert_pass(j, rb1s, j, "gated", j)
            U["psR"] = [U["psY"][2], U["psY"][3]]

            if dbg:
                dacc = DMA("dY", dbg_acc[:, half * 8:(half + 1) * 8, :], acc[:, :, :],
                           deps=[U["acc"][(tt, qd)] for tt in range(8) for qd in range(4)])
                dbg_acc_dep = [dacc]
            else:
                dbg_acc_dep = []

            for tt in range(8):
                acc_done = [U["acc"][(tt, qd)] for qd in range(4)] + dbg_acc_dep
                sil = S.op("S", lambda e, tt=tt: e.activation(gsil[:, :], acc[:, tt, :], AF.Silu),
                           deps=acc_done + [U["gsil"]])
                srload = DMA("dY", srl[:, :], s_scr[hoff + tt * 128:hoff + (tt + 1) * 128, :],
                             deps=[s_store_list[half * 8 + tt], U["srl"]])
                xqload = DMA("dY", xqrl[:, :], xq[hoff + tt * 128:hoff + (tt + 1) * 128, :],
                             deps=[U["xqrl"]])
                mu = S.op("V", lambda e: e.tensor_mul(fin[:, :], srl[:, :], gsil[:, :]),
                          deps=[sil, srload, U["fin"]])
                ad = S.op("V", lambda e: e.tensor_add(xqrl[:, :], fin[:, :], xqrl[:, :]),
                          deps=[mu, xqload])
                ost = DMA("dY", out[hoff + tt * 128:hoff + (tt + 1) * 128, :], xqrl[:, :],
                          deps=[ad])
                out_stores.append(ost)
                U["gsil"] = mu
                U["fin"] = ad
                U["srl"] = mu
                U["xqrl"] = ost
                for qd in range(4):
                    U["acc"][(tt, qd)] = sil

            U["xnqTh"] = [U["hT"][0], rt_steps[-1]]

        S.op("Y", lambda e: e.nop(), deps=out_stores)

        S.emit(nc, sems)
        P5.close()
    return nc


_PROG_CACHE = {}


def _get_prog(dbg=0, serialize=False):
    key = (dbg, serialize)
    if key not in _PROG_CACHE:
        _PROG_CACHE[key] = build_program(dbg=dbg, serialize=serialize)
    return _PROG_CACHE[key]


def make_in_maps(inputs, dbg=0):
    f32 = np.float32
    g = lambda k: np.ascontiguousarray(np.asarray(inputs[k], dtype=f32))
    x = g("L0_attn_x")
    WQ = g("L0_attn_WQ"); bQ = g("L0_attn_bQ")
    WK = g("L0_attn_WK"); bK = g("L0_attn_bK")
    WV = g("L0_attn_WV"); bV = g("L0_attn_bV")
    PJ = g("L0_attn_proj")
    sw1 = g("L0_ffn_shared_w1"); sb1 = g("L0_ffn_shared_b1")
    sw2 = g("L0_ffn_shared_w2"); sb2 = g("L0_ffn_shared_b2")
    rw1 = g("L0_ffn_routed_w1"); rb1 = g("L0_ffn_routed_b1")
    rw2 = g("L0_ffn_routed_w2"); rb2 = g("L0_ffn_routed_b2")
    cent = g("L0_ffn_centroids"); cb = g("L0_ffn_bias")

    common = {
        "ident": np.eye(128, dtype=BF),
        "wq": WQ,
        "bq_r": np.ascontiguousarray(bQ.reshape(16, 128).T),
        "wk": WK,
        "bk_r": np.ascontiguousarray(bK.reshape(4, 128).T),
        "wv": WV,
        "bv": bV.reshape(1, DK),
        "proj": PJ,
        "sw1": sw1,
        "sb1_r": np.ascontiguousarray(sb1.reshape(2, NFC, 128).transpose(2, 1, 0)),
        "sw2": sw2,
        "sb2": sb2,
        "rw1": rw1,
        "rb1_r": np.ascontiguousarray(rb1.reshape(NEXP, NFC, 128).transpose(2, 1, 0)),
        "rw2": rw2,
        "rb2": rb2,
        "cent_r": np.ascontiguousarray(cent.T),
        "cbias": cb.reshape(NEXP, 1),
    }
    masks = [build_masks(0), build_masks(1)]
    in_maps = []
    for core in range(8):
        b, p = core // 2, core % 2
        J = own_tile_ids(p)
        xq_host = np.ascontiguousarray(
            np.concatenate([x[b, 128 * j:128 * (j + 1), :] for j in J], axis=0))
        m = dict(common)
        m["xkv"] = x[b]
        m["xq"] = xq_host
        m["masks"] = masks[p]
        in_maps.append(m)
    return in_maps


def assemble(results):
    y = np.empty((4, TKV, D), dtype=np.float32)
    for core in range(8):
        b, p = core // 2, core % 2
        J = own_tile_ids(p)
        o = results[core]["out"]
        for i, j in enumerate(J):
            y[b, 128 * j:128 * (j + 1), :] = o[128 * i:128 * (i + 1), :]
    return y


def kernel(**inputs):
    nc = _get_prog(dbg=0)
    in_maps = make_in_maps(inputs)
    res = run_bass_kernel_spmd(nc, in_maps, core_ids=list(range(8)))
    return assemble(res.results)


# version 13
# speedup vs baseline: 1.9168x; 1.0066x over previous
import numpy as np
import ml_dtypes
from contextlib import ExitStack

import concourse.bass as bass
import concourse.mybir as mybir
from concourse.bass_utils import run_bass_kernel_spmd

F32 = mybir.dt.float32
BF16 = mybir.dt.bfloat16
AF = mybir.ActivationFunctionType
ALU = mybir.AluOpType
AX = mybir.AxisListType

BF = ml_dtypes.bfloat16
NEG = np.float32(-1e30)

D = 2048
DK = 512
TQ = 2048
TKV = 4096
DFF = 1408
NFC = 11
NDC = 16
NEXP = 16
HALF = 1024
RSQ128 = float(1.0 / np.sqrt(128.0))


def own_tile_ids(parity):
    lo = [2 * i + parity for i in range(8)]
    hi = [31 - 2 * i - parity for i in range(8)]
    return sorted(lo + hi)


def build_masks(parity):
    J = own_tile_ids(parity)
    m = np.zeros((128, 16, 256), dtype=np.float32)
    for i in range(16):
        q = 128 * J[i] + np.arange(128)[:, None]
        k = 256 * i + np.arange(256)[None, :]
        m[:, i, :] = np.where(k <= q, 0.0, NEG)
    return m.astype(BF)


class Step:
    __slots__ = ("idx", "ch", "fn", "deps", "amt", "val", "key")

    def __init__(self, idx, ch, fn, deps, amt, key):
        self.idx, self.ch, self.fn, self.deps, self.amt = idx, ch, fn, deps, amt
        self.val = None
        self.key = key


class Sched:
    ENGS = ("T", "V", "S", "G", "Y")
    CHANNELS = ("T", "V", "S", "G", "Y", "dY", "dS", "dG", "dGc")
    QUEUE_OF = {"dY": "Y", "dS": "S", "dG": "G", "dGc": "G"}
    NOBAR = ("dGc",)
    NROT = 16

    def __init__(self, serialize=False):
        self.steps = []
        self.last = {}
        self.rot = {c: 0 for c in self.CHANNELS}
        self.serialize = serialize

    def _add(self, ch, fn, deps, amt):
        deps = [d for d in deps if d is not None]
        if self.serialize and self.steps:
            deps = deps + [self.steps[-1]]
        if amt == 16:
            key = (ch, self.rot[ch] % self.NROT)
            self.rot[ch] += 1
        else:
            key = (ch, 0)
        st = Step(len(self.steps), ch, fn, deps, amt, key)
        self.steps.append(st)
        self.last[key] = st
        return st

    def op(self, eng, fn, deps=()):
        return self._add(eng, fn, list(deps), 1)

    def dma(self, queue, fn, deps=()):
        return self._add(queue, fn, list(deps), 16)

    def snapshot(self):
        return [s for k, s in self.last.items() if k[0] not in self.NOBAR]

    def emit(self, nc, sems):
        referenced = set()
        for s in self.steps:
            for d in s.deps:
                referenced.add(d.idx)
        counts = {}
        for s in self.steps:
            if s.amt == 16 or s.idx in referenced:
                counts[s.key] = counts.get(s.key, 0) + s.amt
                s.val = counts[s.key]
        owner = {c: self.QUEUE_OF.get(c, c) for c in self.CHANNELS}

        with nc.Block() as blk:
            def emit_for(name, eng):
                water = {}
                for s in self.steps:
                    if owner[s.ch] != name:
                        continue
                    for d in s.deps:
                        if s.ch == "T" and d.ch == "T":
                            continue
                        if water.get(d.key, 0) < d.val:
                            eng.wait_ge(sems[d.key], d.val)
                            water[d.key] = d.val
                    if s.amt == 16 and s.val > 16 and water.get(s.key, 0) < s.val - 16:
                        eng.wait_ge(sems[s.key], s.val - 16)
                        water[s.key] = s.val - 16
                    instr = s.fn(eng)
                    if s.val is not None:
                        instr.then_inc(sems[s.key], s.amt)

            @blk.tensor
            def _(t):
                emit_for("T", t)

            @blk.vector
            def _(v):
                emit_for("V", v)

            @blk.scalar
            def _(s):
                emit_for("S", s)

            @blk.gpsimd
            def _(g):
                emit_for("G", g)

            @blk.sync
            def _(y):
                emit_for("Y", y)


def build_program(dbg=0, serialize=False):
    nc = bass.Bass("TRN2", target_bir_lowering=False)
    S = Sched(serialize=serialize)

    ap = lambda t: t.ap()
    xkv = ap(nc.dram_tensor("xkv", [TKV, D], F32, kind="ExternalInput"))
    xq = ap(nc.dram_tensor("xq", [TQ, D], F32, kind="ExternalInput"))
    masks_in = ap(nc.dram_tensor("masks", [128, 16, 256], BF16, kind="ExternalInput"))
    ident_in = ap(nc.dram_tensor("ident", [128, 128], BF16, kind="ExternalInput"))
    wq = ap(nc.dram_tensor("wq", [D, D], F32, kind="ExternalInput"))
    bq_r = ap(nc.dram_tensor("bq_r", [128, 16], F32, kind="ExternalInput"))
    wk = ap(nc.dram_tensor("wk", [DK, D], F32, kind="ExternalInput"))
    bk_r = ap(nc.dram_tensor("bk_r", [128, 4], F32, kind="ExternalInput"))
    wv = ap(nc.dram_tensor("wv", [DK, D], F32, kind="ExternalInput"))
    bv_in = ap(nc.dram_tensor("bv", [1, DK], F32, kind="ExternalInput"))
    proj = ap(nc.dram_tensor("proj", [D, D], F32, kind="ExternalInput"))
    sw1 = ap(nc.dram_tensor("sw1", [2, D, DFF], F32, kind="ExternalInput"))
    sb1_r = ap(nc.dram_tensor("sb1_r", [128, NFC, 2], F32, kind="ExternalInput"))
    sw2 = ap(nc.dram_tensor("sw2", [2, DFF, D], F32, kind="ExternalInput"))
    sb2 = ap(nc.dram_tensor("sb2", [2, D], F32, kind="ExternalInput"))
    rw1 = ap(nc.dram_tensor("rw1", [NEXP, D, DFF], F32, kind="ExternalInput"))
    rb1_r = ap(nc.dram_tensor("rb1_r", [128, NFC, NEXP], F32, kind="ExternalInput"))
    rw2 = ap(nc.dram_tensor("rw2", [NEXP, DFF, D], F32, kind="ExternalInput"))
    rb2 = ap(nc.dram_tensor("rb2", [NEXP, D], F32, kind="ExternalInput"))
    cent_r = ap(nc.dram_tensor("cent_r", [D, NEXP], F32, kind="ExternalInput"))
    cbias = ap(nc.dram_tensor("cbias", [NEXP, 1], F32, kind="ExternalInput"))

    dscr = "ExternalOutput" if dbg else "Internal"
    scr_kv = ap(nc.dram_tensor("scr_kv", [TKV, D], BF16, kind=dscr))
    scr_q = ap(nc.dram_tensor("scr_q", [TQ, D], BF16, kind=dscr))
    scr_qlo = ap(nc.dram_tensor("scr_qlo", [TQ, D], BF16, kind="Internal"))
    s_scr = ap(nc.dram_tensor("s_scr", [TQ, D], BF16, kind=dscr))
    scr_w1 = ap(nc.dram_tensor("scr_w1", [18, NFC, 128, NDC * 128], BF16, kind="Internal"))
    scr_w2 = ap(nc.dram_tensor("scr_w2", [18, 8, 128, NFC * 256], BF16, kind="Internal"))
    out = ap(nc.dram_tensor("out", [TQ, D], F32, kind="ExternalOutput"))
    if dbg:
        dbg_kt = ap(nc.dram_tensor("dbg_kt", [128, 4, TKV], BF16, kind="ExternalOutput"))
        dbg_v = ap(nc.dram_tensor("dbg_v", [128, 32, DK], BF16, kind="ExternalOutput"))
        dbg_qs = ap(nc.dram_tensor("dbg_qs", [128, 4, TQ], BF16, kind="ExternalOutput"))
        dbg_gat = ap(nc.dram_tensor("dbg_gat", [128, 16, NEXP], F32, kind="ExternalOutput"))
        dbg_acc = ap(nc.dram_tensor("dbg_acc", [128, 16, D], F32, kind="ExternalOutput"))

    def DMA(q, dst, src, deps=()):
        return S.dma(q, lambda e, a=dst, b=src: e.dma_start(a, b), deps)

    def DMT(q, dst, src, deps=()):
        return S.dma(q, lambda e, a=dst, b=src: e.dma_start_transpose(a, b), deps)

    with ExitStack() as CTX:
        sems = {}
        for c in Sched.CHANNELS:
            nrot = Sched.NROT if c.startswith("d") else 1
            for i in range(nrot):
                sems[(c, i)] = CTX.enter_context(nc.semaphore("ch_%s_%d" % (c, i)))
        A = lambda name, shape, dt=F32: CTX.enter_context(nc.sbuf_tensor(name, shape, dt))

        ident = A("ident_sb", [128, 128], BF16)
        bqs = A("bqs_sb", [128, 4])
        bks = A("bks_sb", [128, 4])
        bvrow = A("bvrow_sb", [1, DK], BF16)
        ones1 = A("ones1_sb", [1, 128], BF16)
        sb1s = A("sb1_sb", [128, NFC, 2])
        rb1s = A("rb1_sb", [128, NFC, NEXP])
        b2ext = A("b2ext_sb", [16, D], BF16)
        sb2n = A("sb2n_sb", [2, D], BF16)
        ones2 = A("ones2_sb", [2, 128], BF16)
        centhi = A("centhi_sb", [128, NDC, NEXP], BF16)
        centlo = A("centlo_sb", [128, NDC, NEXP], BF16)
        cbias_s = A("cbias_sb", [16, 1])

        ATT = ExitStack()
        AA = lambda name, shape, dt=F32: ATT.enter_context(nc.sbuf_tensor(name, shape, dt))
        Pfold = AA("pfold_sb", [128, 4, D], BF16)
        WKT = AA("wkt_sb", [128, NDC, DK], BF16)
        WVT = AA("wvt_sb", [128, NDC, DK], BF16)
        QsumT = AA("qsumt_sb", [128, 4, TQ], BF16)

        psA = CTX.enter_context(nc.psum_tensor("psA", [128, 512], F32))
        psB = CTX.enter_context(nc.psum_tensor("psB", [128, 512], F32))
        psC = CTX.enter_context(nc.psum_tensor("psC", [128, 512], F32))
        psD = CTX.enter_context(nc.psum_tensor("psD", [128, 512], F32))
        psE = CTX.enter_context(nc.psum_tensor("psE", [128, 512], F32))
        psF = CTX.enter_context(nc.psum_tensor("psF", [128, 512], F32))
        ptr1 = CTX.enter_context(nc.psum_tensor("ptr1", [128, 1024], BF16))
        ptr2 = CTX.enter_context(nc.psum_tensor("ptr2", [128, 1024], BF16))
        psY = [psC, psD, psE, psF]

        TP = {"k": 0, "ev": [None, None]}

        def TRP(src_ap, outP, outF, id_ap, evac_maker, deps):
            k = TP["k"]
            TP["k"] += 1
            bank = [ptr1, ptr2][k % 2]
            pap = bank[0:outP, 0:outF]
            tst = S.op("T", lambda e, pap=pap, s=src_ap, ia=id_ap: e.transpose(pap, s, ia),
                       deps=list(deps) + [TP["ev"][k % 2]])
            ev = evac_maker(pap, tst)
            TP["ev"][k % 2] = ev
            return tst, ev

        P1wq = ExitStack()
        WQT = P1wq.enter_context(nc.sbuf_tensor("wqt_sb", [128, NDC, DK], BF16))
        P1a = ExitStack()
        A1 = lambda name, shape, dt=F32: P1a.enter_context(nc.sbuf_tensor(name, shape, dt))
        wstg = A1("wstg", [128, 2, D])
        wstg2 = A1("wstg2", [128, 2, D])
        wbfh = A1("wbfh", [128, 2, D], BF16)
        bqr_s = A1("bqr_s", [128, 16])
        centf = A1("centf", [128, NDC, NEXP])
        wbfh2 = A1("wbfh2", [128, 2, D], BF16)

        ld_ident = DMA("dY", ident[:, :], ident_in[:, :])
        ld_bqr = DMA("dY", bqr_s[:, :], bq_r[:, :])
        ld_bk = DMA("dY", bks[:, :], bk_r[:, :])
        ld_sb1 = DMA("dY", sb1s[:, :, :], sb1_r[:, :, :])
        ld_rb1 = DMA("dY", rb1s[:, :, :], rb1_r[:, :, :])
        ld_cb = DMA("dY", cbias_s[:, :], cbias[:, :])
        ld_centf = DMA("dY", centf[:, :, :], cent_r.rearrange("(dc p) j -> p dc j", p=128))
        ld_bv = DMA("dG", bvrow[:, :], bv_in[:, :])
        ld_rb2 = DMA("dG", b2ext[:, :], rb2[:, :])
        ld_sb2 = DMA("dG", sb2n[:, :], sb2[:, :])
        ld_centh = DMA("dG", centhi[:, :, :], cent_r.rearrange("(dc p) j -> p dc j", p=128))

        s1 = S.op("V", lambda e: e.tensor_add(bqs[:, :], bqr_s[:, 0:4], bqr_s[:, 4:8]), deps=[ld_bqr])
        s2 = S.op("V", lambda e: e.tensor_add(bqs[:, :], bqs[:, :], bqr_s[:, 8:12]), deps=[s1])
        bqs_done = S.op("V", lambda e: e.tensor_add(bqs[:, :], bqs[:, :], bqr_s[:, 12:16]), deps=[s2])
        clo = S.op("V", lambda e: e.tensor_sub(centlo[:, :, :], centf[:, :, :], centhi[:, :, :]),
                   deps=[ld_centf, ld_centh])
        ones_set = S.op("V", lambda e: e.memset(ones1[:, :], 1.0))
        ones2_set = S.op("V", lambda e: e.memset(ones2[:, :], 1.0))

        def fold_weight(src_dram, out_cvt_fn, prev_users):
            outs = []
            prevA = prev_users
            for ah in range(2):
                l0 = DMA("dG", wstg[:, :, :],
                         src_dram[256 * ah:256 * ah + 256, :].rearrange("(a p) d -> p a d", p=128),
                         deps=prevA)
                fold = l0
                addstep = None
                for r in range(1, 4):
                    lb = DMA("dG", wstg2[:, :, :],
                             src_dram[512 * r + 256 * ah:512 * r + 256 * ah + 256, :]
                             .rearrange("(a p) d -> p a d", p=128),
                             deps=[addstep] + prevA)
                    addstep = S.op("V", lambda e: e.tensor_add(wstg[:, :, :], wstg[:, :, :],
                                                              wstg2[:, :, :]),
                                   deps=[l0, lb, addstep])
                outs.append(out_cvt_fn(ah, addstep))
                prevA = [outs[-1][0]]
            return outs

        def emit_tr(srcbuf, a_local, ah, dst, dstcol_base, dep):
            evs = []
            for dc in range(NDC):
                def mk(pap, tst, d=dc, ds=dst, cb=dstcol_base):
                    return S.op("V", lambda e: e.tensor_copy(ds[:, d, cb:cb + 128], pap),
                                deps=[tst])
                tst, ev = TRP(srcbuf[:, a_local, dc * 128:(dc + 1) * 128], 128, 128,
                              ident[:, :], mk, [dep, ld_ident])
                evs.append(ev)
            return evs

        def wq_cvt(ah, fold_step):
            cvt = S.op("V", lambda e: e.tensor_copy(wbfh[:, :, :], wstg[:, :, :]), deps=[fold_step])
            evs = []
            for al in range(2):
                evs += emit_tr(wbfh, al, ah, WQT, (2 * ah + al) * 128, cvt)
            return [evs[-1], cvt] + evs

        wq_outs = fold_weight(wq, wq_cvt, [])
        wqt_done = wq_outs[-1][0]
        wq_users = [s for o in wq_outs for s in o]

        def pf_cvt(ah, fold_step):
            cvt = S.op("V", lambda e, ah=ah: e.tensor_scalar(
                Pfold[:, 2 * ah:2 * ah + 2, :], wstg[:, :, :], RSQ128, None, ALU.mult),
                deps=[fold_step])
            return [cvt]

        pf_outs = fold_weight(proj, pf_cvt, wq_users)
        cvt_pf = pf_outs[-1][0]


        kv_tr_last = {}
        prev_half_load = None
        for (wsrc, dst) in ((wk, WKT), (wv, WVT)):
            for ah in range(2):
                lh = DMA("dG", wbfh2[:, :, :],
                         wsrc[256 * ah:256 * ah + 256, :].rearrange("(a p) d -> p a d", p=128),
                         deps=[prev_half_load])
                evs = []
                for al in range(2):
                    evs += emit_tr(wbfh2, al, ah, dst, (2 * ah + al) * 128, lh)
                prev_half_load = evs[-1]
                kv_tr_last[id(dst)] = evs[-1]
        wkt_done = kv_tr_last[id(WKT)]
        wvt_done = kv_tr_last[id(WVT)]

        scrw1_done = {}
        scrw2_done = {}


        PHq = ExitStack()
        A2 = lambda name, shape, dt=F32: PHq.enter_context(nc.sbuf_tensor(name, shape, dt))
        xt = [A2("xt0", [128, D]), A2("xt1", [128, D])]
        stats = [A2("st0", [128, 8]), A2("st1", [128, 8])]
        hib = [A2("hib0", [128, D], BF16), A2("hib1", [128, D], BF16)]
        lob = [A2("lob0", [128, D], BF16), A2("lob1", [128, D], BF16)]
        xfb = [A2("xfb0", [128, D]), A2("xfb1", [128, D])]
        xnWq = A2("xnWq", [128, NDC, 1, 512], BF16)

        def ln_tile(bufs, src_ap, row0, b, want_lo, prevu, dst_scr, prev_lo):
            xt, stats, hib, lob, xfb = bufs
            xb = xt[b]
            hb = hib[b]
            xfb_b = xfb[b] if xfb is not None else None
            lob_b = lob[b] if lob is not None else None
            dummy = xfb_b if xfb_b is not None else hb
            ld = DMA("dY", xb[:, :], src_ap[row0:row0 + 128, :], deps=prevu)
            st = stats[b]
            r1 = S.op("V", lambda e, xb=xb: e.reduce_sum(st[:, 0:1], xb[:, :], axis=AX.X), deps=[ld])
            sq = S.op("S", lambda e, xb=xb, dummy=dummy: e.activation(dummy[:, :], xb[:, :], AF.Square,
                                                                     accum_out=st[:, 4:5]),
                      deps=[ld] + prevu + prev_lo)
            m = S.op("V", lambda e: e.tensor_scalar(st[:, 1:2], st[:, 0:1], float(1.0 / D), None, ALU.mult),
                     deps=[r1])
            v2 = S.op("V", lambda e: e.tensor_scalar(st[:, 2:3], st[:, 4:5], float(1.0 / D), 1e-5,
                                                     ALU.mult, ALU.add), deps=[sq])
            mm = S.op("V", lambda e: e.tensor_mul(st[:, 3:4], st[:, 1:2], st[:, 1:2]), deps=[m])
            vv = S.op("V", lambda e: e.tensor_sub(st[:, 2:3], st[:, 2:3], st[:, 3:4]), deps=[v2, mm])
            lnv = S.op("S", lambda e: e.activation(st[:, 7:8], st[:, 2:3], AF.Ln), deps=[vv])
            rstd = S.op("S", lambda e: e.activation(st[:, 5:6], st[:, 7:8], AF.Exp, scale=-0.5), deps=[lnv])
            nmr = S.op("V", lambda e: e.tensor_scalar(st[:, 6:7], st[:, 1:2], st[:, 5:6], -1.0,
                                                      ALU.mult, ALU.mult), deps=[rstd])
            hi = S.op("S", lambda e, xb=xb, hb=hb: e.activation(hb[:, :], xb[:, :], AF.Identity,
                                                                bias=st[:, 6:7], scale=st[:, 5:6]),
                      deps=[nmr] + prevu)
            lo_store = None
            lo_users = []
            xf = lo = None
            if want_lo:
                xf = S.op("S", lambda e, xb=xb, xfb_b=xfb_b: e.activation(
                    xfb_b[:, :], xb[:, :], AF.Identity,
                    bias=st[:, 6:7], scale=st[:, 5:6]),
                    deps=[nmr, sq] + prev_lo)
                lo = S.op("V", lambda e, hb=hb, xfb_b=xfb_b, lob_b=lob_b: e.tensor_sub(
                    lob_b[:, :], xfb_b[:, :], hb[:, :]),
                    deps=[xf, hi] + prev_lo)
                lo_store = DMA("dY", scr_qlo[row0:row0 + 128, :], lob_b[:, :], deps=[lo])
                lo_users = [xf, lo, lo_store]
            hi_store = DMA("dY", dst_scr[row0:row0 + 128, :], hb[:, :], deps=[hi])
            last = [hi_store, sq, hi, r1]
            if want_lo:
                last += [xf, lo]
            return hi_store, lo_store, last, lo_users, hi

        q_hi = []
        q_lo = []
        prev_lo = [[], []]
        prevu = [[], []]
        qwin_users = [[], []]
        pq_ev = [None, None]
        qev_all = []
        for tc in range(4):
            w = 0
            evs_this = []
            for kk in range(4):
                k = tc * 4 + kk
                hs, ls, lu, lou, hi_st = ln_tile((xt, stats, hib, lob, xfb), xq, 128 * k, k % 2,
                                                 True, prevu[k % 2], scr_q, prev_lo[k % 2])
                q_hi.append(hs)
                q_lo.append(ls)
                hb = hib[k % 2]
                tsts = []
                for dc in range(NDC):
                    def mkev(pap, tst, dc=dc, w=w, kk=kk):
                        return S.op("V", lambda e, pap=pap, dc=dc, w=w, kk=kk: e.tensor_copy(
                            xnWq[:, dc, w, kk * 128:(kk + 1) * 128], pap),
                            deps=[tst] + qwin_users[w])
                    tst, ev = TRP(hb[:, dc * 128:(dc + 1) * 128], 128, 128, ident[:, :],
                                  mkev, [hi_st, ld_ident])
                    tsts.append(tst)
                    evs_this.append(ev)
                prevu[k % 2] = lu + [tsts[-1]]
                prev_lo[k % 2] = lou
            users = []
            for a in range(4):
                g = a % 2
                ps = [psA, psB][g]

                def qfn(e, a=a, w=w, ps=ps):
                    for dc in range(NDC):
                        ins = e.matmul(ps[:, :], WQT[:, dc, a * 128:(a + 1) * 128],
                                       xnWq[:, dc, w, :], start=(dc == 0), stop=(dc == NDC - 1))
                    return ins
                mmst = S.op("T", qfn, deps=evs_this + [wqt_done, pq_ev[g]])
                ev = S.op("S", lambda e, a=a, tc=tc, ps=ps: e.activation(
                    QsumT[:, a, tc * 512:(tc + 1) * 512], ps[:, :], AF.Identity,
                    bias=bqs[:, a:a + 1]), deps=[mmst, bqs_done])
                pq_ev[g] = ev
                qev_all.append(ev)
                users.append(mmst)
            qwin_users[w] = users

        bar_snap = S.snapshot()
        bar_steps = [S.op(e, lambda eng: eng.nop(), deps=bar_snap) for e in ("T", "V", "S", "Y")]
        PHq.close()
        P1a.close()
        P1wq.close()


        ATT2 = ExitStack()
        AA2 = lambda name, shape, dt=F32: ATT2.enter_context(nc.sbuf_tensor(name, shape, dt))
        KT = AA2("kt_sb", [128, 4, TKV], BF16)
        Vsb = AA2("v_sb", [128, 32, DK], BF16)
        PHkv = ExitStack()
        A2 = lambda name, shape, dt=F32: PHkv.enter_context(nc.sbuf_tensor(name, shape, dt))
        xt = [A2("kxt0", [128, D]), A2("kxt1", [128, D])]
        stats = [A2("kst0", [128, 8]), A2("kst1", [128, 8])]
        hib = [A2("khib0", [128, D], BF16), A2("khib1", [128, D], BF16)]
        xnWk = A2("xnWk", [128, NDC, 1, 512], BF16)

        kv_hi = []
        prevu = [list(bar_steps), list(bar_steps)]
        win_users = [bar_steps, bar_steps]
        pk_ev = [qev_all[-1], qev_all[-1]]
        pv_ev = [None, None]
        k_ev_all = []
        v_ev_all = []
        for tc in range(8):
            w = 0
            evs_this = []
            for kk in range(4):
                k = tc * 4 + kk
                hs, _, lu, _, hi_st = ln_tile((xt, stats, hib, None, None), xkv, 128 * k,
                                              k % 2, False, prevu[k % 2], scr_kv, [])
                kv_hi.append(hs)
                hb = hib[k % 2]
                tsts = []
                for dc in range(NDC):
                    def mkev(pap, tst, dc=dc, w=w, kk=kk):
                        return S.op("V", lambda e, pap=pap, dc=dc, w=w, kk=kk: e.tensor_copy(
                            xnWk[:, dc, w, kk * 128:(kk + 1) * 128], pap),
                            deps=[tst] + list(win_users[w]))
                    tst, ev = TRP(hb[:, dc * 128:(dc + 1) * 128], 128, 128, ident[:, :],
                                  mkev, [hi_st, ld_ident])
                    tsts.append(tst)
                    evs_this.append(ev)
                prevu[k % 2] = lu + [tsts[-1]]
            users = []
            for a in range(4):
                g = a % 2
                ps = [psA, psB][g]

                def kfn(e, a=a, w=w, ps=ps):
                    for dc in range(NDC):
                        ins = e.matmul(ps[:, :], WKT[:, dc, a * 128:(a + 1) * 128],
                                       xnWk[:, dc, w, :], start=(dc == 0), stop=(dc == NDC - 1))
                    return ins
                mmst = S.op("T", kfn, deps=evs_this + [wkt_done, pk_ev[g]])
                ev = S.op("S", lambda e, a=a, tc=tc, ps=ps: e.activation(
                    KT[:, a, tc * 512:(tc + 1) * 512], ps[:, :], AF.Identity,
                    bias=bks[:, a:a + 1]), deps=[mmst, ld_bk])
                pk_ev[g] = ev
                k_ev_all.append(ev)
                users.append(mmst)
            for sub in range(4):
                g = sub % 2
                ps = [psC, psD][g]
                kb = tc * 4 + sub

                def vfn(e, sub=sub, w=w, ps=ps):
                    for dc in range(NDC):
                        e.matmul(ps[:, :], xnWk[:, dc, w, sub * 128:(sub + 1) * 128],
                                 WVT[:, dc, :], start=(dc == 0), stop=False)
                    return e.matmul(ps[:, :], ones1[:, :], bvrow[:, :], start=False, stop=True)
                mmst = S.op("T", vfn, deps=evs_this + [wvt_done, ld_bv, ones_set, pv_ev[g]])
                ev = S.op("V", lambda e, kb=kb, ps=ps: e.tensor_copy(Vsb[:, kb, :], ps[:, :]),
                          deps=[mmst])
                pv_ev[g] = ev
                v_ev_all.append(ev)
                users.append(mmst)
            win_users[w] = users

        if dbg:
            d1 = DMA("dY", dbg_kt[:, :, :], KT[:, :, :], deps=k_ev_all)
            d2 = DMA("dY", dbg_v[:, :, :], Vsb[:, :, :], deps=v_ev_all)
            d3 = DMA("dY", dbg_qs[:, :, :], QsumT[:, :, :], deps=qev_all)
            dbg_proj = [d1, d2, d3]
        else:
            dbg_proj = []

        bar_snap = S.snapshot()
        bar_steps = [S.op(e, lambda eng: eng.nop(), deps=bar_snap) for e in ("T", "V", "S", "Y")]
        PHkv.close()


        P4 = ExitStack()
        A4 = lambda name, shape, dt=F32: P4.enter_context(nc.sbuf_tensor(name, shape, dt))
        masks = A4("masks_sb", [128, 16, 256], BF16)
        srow = A4("srow", [128, TKV])
        Pbf = A4("pbf", [128, TKV], BF16)
        PTb = A4("ptb", [128, 32, 128], BF16)
        Zbf = A4("zbf", [128, 512], BF16)
        ZTb = A4("ztb", [128, 4, 128], BF16)
        stile = [A4("stile0", [128, D], BF16), A4("stile1", [128, D], BF16)]
        smax = A4("smax", [128, 2])
        ssum = A4("ssum", [128, 2])

        ld_masks = DMA("dY", masks[:, :, :], masks_in[:, :, :], deps=bar_steps)

        psS_ev = [None, None]
        s_stores = [None, None]
        prev_srow_reader = None
        prev_pbf_reader = None
        prev_rcp = None
        prev_av = None
        prev_zread = None
        psZ_ev = None
        psSP_ev = [None, None]
        s_store_list = []

        for i in range(16):
            nkeys = 256 * (i + 1)
            nkb = nkeys // 128
            blocks = []
            off = 0
            while off < nkeys:
                ln_ = min(512, nkeys - off)
                blocks.append((off, ln_))
                off += ln_
            ev_blocks = []
            for bi, (off, ln_) in enumerate(blocks):
                g = bi % 2
                ps = [psA, psB][g]

                def scfn(e, i=i, off=off, ln_=ln_, ps=ps):
                    for a in range(4):
                        ins = e.matmul(ps[:, 0:ln_], QsumT[:, a, i * 128:(i + 1) * 128],
                                       KT[:, a, off:off + ln_], start=(a == 0), stop=(a == 3))
                    return ins
                tc_max = (off + ln_ - 1) // 512
                mmst = S.op("T", scfn,
                            deps=[psS_ev[g]] + qev_all[:4 * (i // 4 + 1)]
                            + k_ev_all[:4 * (tc_max + 1)] + dbg_proj)
                ev = S.op("V", lambda e, off=off, ln_=ln_, ps=ps: e.tensor_copy(
                    srow[:, off:off + ln_], ps[:, 0:ln_]),
                    deps=[mmst, prev_srow_reader])
                psS_ev[g] = ev
                ev_blocks.append(ev)
            mk = S.op("V", lambda e, i=i, nk=nkeys: e.tensor_add(
                srow[:, nk - 256:nk], srow[:, nk - 256:nk], masks[:, i, :]),
                deps=[ev_blocks[-1], ld_masks])
            nmx = S.op("V", lambda e, nk=nkeys: e.reduce_max(
                smax[:, 0:1], srow[:, 0:nk], axis=AX.X, negate=True),
                deps=[mk, prev_pbf_reader] + ev_blocks)
            exps = S.op("S", lambda e, nk=nkeys: e.activation(
                Pbf[:, 0:nk], srow[:, 0:nk], AF.Exp, bias=smax[:, 0:1],
                accum_out=ssum[:, 0:1]),
                deps=[nmx, prev_pbf_reader, prev_rcp])
            rcp = S.op("V", lambda e: e.reciprocal(ssum[:, 1:2], ssum[:, 0:1]),
                       deps=[exps, prev_zread])
            prev_rcp = rcp
            pt_evs = []
            last_pt_t = None
            for kb in range(nkb):
                def mk(pap, tst, kb=kb):
                    return S.op("V", lambda e: e.tensor_copy(PTb[:, kb, :], pap),
                                deps=[tst, prev_av])
                tst, evt = TRP(Pbf[:, kb * 128:(kb + 1) * 128], 128, 128, ident[:, :],
                               mk, [exps, prev_av])
                pt_evs.append(evt)
                last_pt_t = tst

            def avfn(e, nkb=nkb):
                for kb in range(nkb):
                    ins = e.matmul(psC[:, :], PTb[:, kb, :], Vsb[:, kb, :],
                                   start=(kb == 0), stop=(kb == nkb - 1))
                return ins
            av = S.op("T", avfn,
                      deps=pt_evs + v_ev_all[:4 * ((nkeys - 1) // 512 + 1)] + [psZ_ev])
            prev_av = av
            prev_pbf_reader = last_pt_t
            prev_srow_reader = exps
            zev = S.op("V", lambda e: e.tensor_scalar(
                Zbf[:, :], psC[:, :], ssum[:, 1:2], None, ALU.mult),
                deps=[av, rcp, prev_zread])
            psZ_ev = zev
            zt_evs = []
            zt_ts = []
            for c in range(4):
                def mk(pap, tst, c=c):
                    return S.op("V", lambda e: e.tensor_copy(ZTb[:, c, :], pap),
                                deps=[tst, prev_zread])
                tst, evt = TRP(Zbf[:, c * 128:(c + 1) * 128], 128, 128, ident[:, :],
                               mk, [zev, prev_zread])
                zt_evs.append(evt)
                zt_ts.append(tst)
            prev_zread = zt_ts[-1]
            sb = i % 2
            sp_evs = []
            for dg in range(4):
                g = dg % 2
                ps = [psE, psF][g]

                def spfn(e, dg=dg, ps=ps):
                    for c in range(4):
                        ins = e.matmul(ps[:, :], ZTb[:, c, :], Pfold[:, c, dg * 512:(dg + 1) * 512],
                                       start=(c == 0), stop=(c == 3))
                    return ins
                mmst = S.op("T", spfn, deps=zt_evs + [cvt_pf, psSP_ev[g]])
                ev = S.op("S", lambda e, dg=dg, ps=ps, sb=sb: e.activation(
                    stile[sb][:, dg * 512:(dg + 1) * 512], ps[:, :], AF.Copy),
                    deps=[mmst, s_stores[sb]])
                psSP_ev[g] = ev
                sp_evs.append(ev)
            sst = DMA("dY", s_scr[i * 128:(i + 1) * 128, :], stile[sb][:, :], deps=sp_evs)
            s_stores[sb] = sst
            s_store_list.append(sst)

        bar_snap = S.snapshot()
        bar_steps = [S.op(e, lambda eng: eng.nop(), deps=bar_snap) for e in ("T", "V", "S", "Y")]
        P4.close()
        ATT2.close()
        ATT.close()

        P5 = ExitStack()
        A5 = lambda name, shape, dt=F32: P5.enter_context(nc.sbuf_tensor(name, shape, dt))
        xnqTh = A5("xnqTh", [128, NDC, HALF], BF16)
        lowin = A5("lowin", [128, 2, HALF], BF16)
        acc = A5("acc", [128, 8, D])
        hT = A5("hT", [128, NFC, HALF], BF16)
        w2b = A5("w2b", [128, NFC, 2, 512], BF16)
        w1b = [A5("w1b0", [128, NDC, 128], BF16), A5("w1b1", [128, NDC, 128], BF16)]
        sct = A5("sct", [16, HALF])
        schi = A5("schi", [16, 128], BF16)
        sclo = A5("sclo", [16, 128], BF16)
        schiT = A5("schiT", [128, 16])
        sc32 = A5("sc32", [128, 16])
        rnk = A5("rnk", [128, 16])
        mskb = A5("mskb", [128, 16])
        graw = A5("graw", [128, 16])
        den = A5("den", [128, 2])
        gating = A5("gating", [128, 8, NEXP])
        gbf = A5("gbf", [128, NEXP], BF16)
        gT = A5("gt", [16, 8, 128], BF16)
        xnqrl = [A5("xnqrl0", [128, D], BF16), A5("xnqrl1", [128, D], BF16)]
        xlorl = A5("xlorl", [128, D], BF16)
        xqrl = A5("xqrl", [128, D])
        srl = A5("srl", [128, D], BF16)
        gsil = A5("gsil", [128, D], BF16)
        fin = A5("fin", [128, D])

        b0 = bar_steps[0]
        U = {
            "xnqTh": list(bar_steps), "lowin": [b0, b0], "sct": [b0],
            "schi": b0, "sclo": b0, "schiT": b0, "sc32": b0, "rnk": b0,
            "mskb": b0, "graw": b0, "den": b0, "gbf": b0,
            "gat": [b0] * 8, "gT": [b0] * 8,
            "acc": {}, "hT": [b0], "w2b": [[b0], [b0]], "w1b": [[b0], [b0]],
            "xnqrl": [b0, b0], "xlorl": b0, "xqrl": b0,
            "srl": b0, "gsil": b0, "fin": b0,
            "psH": [bar_steps[1], bar_steps[1]], "psY": [bar_steps[1]] * 4,
            "psR": [bar_steps[1], bar_steps[1]],
        }
        out_stores = []

        for half in range(2):
            hoff = half * HALF
            trh = []
            for dc in range(NDC):
                t = DMT("dS", xnqTh[:, dc, :],
                        scr_q[hoff:hoff + HALF, dc * 128:(dc + 1) * 128],
                        deps=q_hi + U["xnqTh"])
                trh.append(t)

            rt_steps = []
            for dc in range(NDC):
                tl = DMT("dS", lowin[:, dc % 2, :],
                         scr_qlo[hoff:hoff + HALF, dc * 128:(dc + 1) * 128],
                         deps=q_lo + [U["lowin"][dc % 2]])

                def rtfn(e, dc=dc):
                    for sub in range(2):
                        ps = [psE, psF][sub]
                        e.matmul(ps[0:16, :], centhi[:, dc, :],
                                 xnqTh[:, dc, sub * 512:(sub + 1) * 512],
                                 start=(dc == 0), stop=False)
                        e.matmul(ps[0:16, :], centlo[:, dc, :],
                                 xnqTh[:, dc, sub * 512:(sub + 1) * 512],
                                 start=False, stop=False)
                        ins = e.matmul(ps[0:16, :], centhi[:, dc, :],
                                       lowin[:, dc % 2, sub * 512:(sub + 1) * 512],
                                       start=False, stop=(dc == NDC - 1))
                    return ins
                st = S.op("T", rtfn, deps=[trh[dc], tl, clo, ld_centh] + U["psR"])
                U["lowin"][dc % 2] = st
                rt_steps.append(st)
            rt_evs = []
            for sub in range(2):
                ps = [psE, psF][sub]
                ev = S.op("S", lambda e, sub=sub, ps=ps: e.activation(
                    sct[:, sub * 512:(sub + 1) * 512], ps[0:16, :], AF.Sigmoid),
                    deps=[rt_steps[-1]] + U["sct"])
                rt_evs.append(ev)
            badd = S.op("V", lambda e: e.tensor_scalar_add(sct[:, :], sct[:, :], cbias_s[:, 0:1]),
                        deps=rt_evs + [ld_cb])
            U["psY"][2] = rt_evs[0]
            U["psY"][3] = rt_evs[1]

            gat_steps = []
            gT_steps = []
            c2 = None
            for tt in range(8):
                c1 = S.op("V", lambda e, tt=tt: e.tensor_copy(schi[:, :], sct[:, tt * 128:(tt + 1) * 128]),
                          deps=[badd, U["schi"], c2])
                c2 = S.op("V", lambda e, tt=tt: e.tensor_sub(
                    sclo[:, :], sct[:, tt * 128:(tt + 1) * 128], schi[:, :]),
                    deps=[c1, U["sclo"]])
                def mk1(pap, tst):
                    return S.op("V", lambda e, pap=pap: e.tensor_copy(schiT[:, :], pap),
                                deps=[tst, U["schiT"]])
                t1, e1 = TRP(schi[:, :], 128, 16, ident[0:16, 0:16], mk1, [c1])

                def mk2(pap, tst):
                    return S.op("V", lambda e, pap=pap: e.tensor_add(sc32[:, :], schiT[:, :], pap),
                                deps=[tst, e1, U["sc32"]])
                t2, e2 = TRP(sclo[:, :], 128, 16, ident[0:16, 0:16], mk2, [c2])
                U["schiT"] = e2
                U["schi"] = t1
                U["sclo"] = t2
                prev = S.op("V", lambda e: e.tensor_scalar(rnk[:, :], sc32[:, :], sc32[:, 0:1],
                                                           None, ALU.is_lt),
                            deps=[e2, U["rnk"]])
                for j in range(1, NEXP):
                    prev = S.op("V", lambda e, j=j: e.scalar_tensor_tensor(
                        rnk[:, :], sc32[:, :], sc32[:, j:j + 1], rnk[:, :], ALU.is_lt, ALU.add),
                        deps=[prev])
                mk1 = S.op("V", lambda e: e.tensor_scalar(mskb[:, :], rnk[:, :], 3.5, None, ALU.is_lt),
                           deps=[prev, U["mskb"]])
                gr = S.op("V", lambda e: e.tensor_mul(graw[:, :], sc32[:, :], mskb[:, :]),
                          deps=[mk1, U["graw"]])
                U["sc32"] = gr
                U["rnk"] = mk1
                ds = S.op("V", lambda e: e.reduce_sum(den[:, 0:1], graw[:, :], axis=AX.X),
                          deps=[gr, U["den"]])
                dm = S.op("V", lambda e: e.tensor_scalar_max(den[:, 1:2], den[:, 0:1], 1e-8),
                          deps=[ds])
                dr = S.op("V", lambda e: e.reciprocal(den[:, 0:1], den[:, 1:2]), deps=[dm])
                U["den"] = dr
                ga = S.op("V", lambda e, tt=tt: e.tensor_scalar(
                    gating[:, tt, :], graw[:, :], den[:, 0:1], None, ALU.mult),
                    deps=[dr, U["gat"][tt]])
                U["mskb"] = ga
                U["graw"] = ga
                gb = S.op("V", lambda e, tt=tt: e.tensor_copy(gbf[:, :], gating[:, tt, :]),
                          deps=[ga, U["gbf"]])
                def mk3(pap, tst, tt=tt):
                    return S.op("V", lambda e, pap=pap: e.tensor_copy(gT[:, tt, :], pap),
                                deps=[tst, U["gT"][tt]])
                t3, e3 = TRP(gbf[:, :], 16, 128, ident[:, :], mk3, [gb])
                U["gbf"] = t3
                gat_steps.append(ga)
                gT_steps.append(e3)
                U["gat"][tt] = ga
                U["gT"][tt] = e3

            U["sct"] = [c2]
            if dbg:
                dgat = DMA("dY", dbg_gat[:, half * 8:(half + 1) * 8, :], gating[:, :, :],
                           deps=gat_steps)
                for tt in range(8):
                    U["gat"][tt] = dgat

            def expert_pass(jconv, b1s, b1col, mode, gcol):
                w1_src = sw1[jconv - 16] if jconv >= 16 else rw1[jconv]
                w2_src = sw2[jconv - 16] if jconv >= 16 else rw2[jconv]
                h_evs = []
                for fs in range(NFC):
                    wb = w1b[fs % 2]
                    if half == 0:
                        w1f = w1_src[:, fs * 128:(fs + 1) * 128].rearrange(
                            "(dc p) f -> p dc f", p=128)
                        dma1 = DMA("dG", wb[:, 0:8, :], w1f[:, 0:8, :],
                                   deps=U["w1b"][fs % 2])
                        dma2 = DMA("dG", wb[:, 8:16, :], w1f[:, 8:16, :],
                                   deps=U["w1b"][fs % 2])
                        scrw = DMA("dS",
                                   scr_w1[jconv, fs].rearrange("p (dc f) -> p dc f", f=128),
                                   wb[:, :, :], deps=[dma1, dma2])
                        scrw1_done[(jconv, fs)] = scrw
                    else:
                        w1sl = scr_w1[jconv, fs].rearrange("p (dc f) -> p dc f", f=128)
                        q = "dG" if fs % 2 == 0 else "dY"
                        dma1 = DMA(q, wb[:, :, :], w1sl,
                                   deps=U["w1b"][fs % 2] + [scrw1_done[(jconv, fs)]])
                        dma2 = None
                        scrw = None
                    musers = [] if scrw is None else [scrw]
                    for sub in range(2):
                        g = (fs * 2 + sub) % 2
                        ps = [psA, psB][g]

                        def hfn(e, wb=wb, sub=sub, ps=ps):
                            for dc in range(NDC):
                                ins = e.matmul(ps[:, :], wb[:, dc, :],
                                               xnqTh[:, dc, sub * 512:(sub + 1) * 512],
                                               start=(dc == 0), stop=(dc == NDC - 1))
                            return ins
                        mmst = S.op("T", hfn,
                                    deps=[dma1, dma2, U["psH"][g]] + trh + U["hT"])
                        ev = S.op("S", lambda e, fs=fs, sub=sub, ps=ps, b1s=b1s, b1col=b1col: e.activation(
                            hT[:, fs, sub * 512:(sub + 1) * 512], ps[:, :], AF.Relu,
                            bias=b1s[:, fs, b1col:b1col + 1]),
                            deps=[mmst, ld_sb1, ld_rb1])
                        U["psH"][g] = ev
                        h_evs.append(ev)
                        musers.append(mmst)
                    U["w1b"][fs % 2] = musers
                y_last_mm = None
                for qd in range(4):
                    qb = qd % 2
                    if half == 0:
                        w2f = w2_src[:, qd * 512:(qd + 1) * 512].rearrange(
                            "(fc p) d -> p fc d", p=128)
                        wd1 = DMA("dG", w2b[:, 0:6, qb, :], w2f[:, 0:6, :],
                                  deps=U["w2b"][qb])
                        wd2 = DMA("dG", w2b[:, 6:11, qb, :], w2f[:, 6:11, :],
                                  deps=U["w2b"][qb])
                        sc1 = DMA("dS",
                                  scr_w2[jconv, 2 * qd].rearrange("p (fc x) -> p fc x", x=256),
                                  w2b[:, :, qb, 0:256], deps=[wd1, wd2])
                        sc2 = DMA("dY",
                                  scr_w2[jconv, 2 * qd + 1].rearrange("p (fc x) -> p fc x", x=256),
                                  w2b[:, :, qb, 256:512], deps=[wd1, wd2])
                        scrw2_done[(jconv, 2 * qd)] = sc1
                        scrw2_done[(jconv, 2 * qd + 1)] = sc2
                        yusers = [sc1, sc2]
                    else:
                        wd1 = DMA("dG", w2b[:, :, qb, 0:256],
                                  scr_w2[jconv, 2 * qd].rearrange("p (fc x) -> p fc x", x=256),
                                  deps=U["w2b"][qb] + [scrw2_done[(jconv, 2 * qd)]])
                        wd2 = DMA("dY", w2b[:, :, qb, 256:512],
                                  scr_w2[jconv, 2 * qd + 1].rearrange("p (fc x) -> p fc x", x=256),
                                  deps=U["w2b"][qb] + [scrw2_done[(jconv, 2 * qd + 1)]])
                        yusers = []
                    for tt in range(8):
                        bk = (qd * 8 + tt) % 4
                        ps = psY[bk]
                        extras = (mode == "init")
                        if extras:
                            rl1 = DMA("dY", xnqrl[tt % 2][:, :],
                                      scr_q[hoff + tt * 128:hoff + (tt + 1) * 128, :],
                                      deps=[U["xnqrl"][tt % 2]] + q_hi)
                            rl2 = DMA("dY", xlorl[:, :],
                                      scr_qlo[hoff + tt * 128:hoff + (tt + 1) * 128, :],
                                      deps=[U["xlorl"]] + q_lo)

                        def yfn(e, qd=qd, qb=qb, tt=tt, ps=ps, extras=extras):
                            for fs in range(NFC):
                                ins = e.matmul(ps[:, :], hT[:, fs, tt * 128:(tt + 1) * 128],
                                               w2b[:, fs, qb, :],
                                               start=(fs == 0), stop=(fs == NFC - 1 and not extras))
                            if extras:
                                e.matmul(ps[:, :], gT[:, tt, :], b2ext[:, qd * 512:(qd + 1) * 512],
                                         start=False, stop=False)
                                e.matmul(ps[:, :], ones2[:, :], sb2n[:, qd * 512:(qd + 1) * 512],
                                         start=False, stop=False)
                                e.matmul(ps[:, :], ident[:, :], xnqrl[tt % 2][:, qd * 512:(qd + 1) * 512],
                                         start=False, stop=False)
                                ins = e.matmul(ps[:, :], ident[:, :], xlorl[:, qd * 512:(qd + 1) * 512],
                                               start=False, stop=True)
                            return ins
                        deps = h_evs + [wd1, wd2, U["psY"][bk]]
                        if extras:
                            deps += [gT_steps[tt], ld_rb2, ld_sb2, ones2_set, rl1, rl2, ld_ident]
                        mmst = S.op("T", yfn, deps=deps)
                        if extras:
                            U["xnqrl"][tt % 2] = mmst
                            U["xlorl"] = mmst
                        accdep = U["acc"].get((tt, qd))
                        if mode == "init":
                            ev = S.op("V", lambda e, tt=tt, qd=qd, ps=ps: e.tensor_copy(
                                acc[:, tt, qd * 512:(qd + 1) * 512], ps[:, :]),
                                deps=[mmst, accdep])
                        elif mode == "add":
                            ev = S.op("V", lambda e, tt=tt, qd=qd, ps=ps: e.scalar_tensor_tensor(
                                acc[:, tt, qd * 512:(qd + 1) * 512], ps[:, :], 1.0,
                                acc[:, tt, qd * 512:(qd + 1) * 512], ALU.mult, ALU.add),
                                deps=[mmst, accdep])
                        else:
                            ev = S.op("V", lambda e, tt=tt, qd=qd, ps=ps, gcol=gcol: e.scalar_tensor_tensor(
                                acc[:, tt, qd * 512:(qd + 1) * 512], ps[:, :],
                                gating[:, tt, gcol:gcol + 1],
                                acc[:, tt, qd * 512:(qd + 1) * 512], ALU.mult, ALU.add),
                                deps=[mmst, accdep, gat_steps[tt]])
                        U["acc"][(tt, qd)] = ev
                        U["psY"][bk] = ev
                        yusers.append(mmst)
                        y_last_mm = mmst
                    U["w2b"][qb] = yusers
                U["hT"] = [y_last_mm]
                return y_last_mm

            init_last = expert_pass(16, sb1s, 0, "init", None)
            for tt in range(8):
                U["gT"][tt] = init_last
            expert_pass(17, sb1s, 1, "add", None)
            for j in range(NEXP):
                expert_pass(j, rb1s, j, "gated", j)
            U["psR"] = [U["psY"][2], U["psY"][3]]

            if dbg:
                dacc = DMA("dY", dbg_acc[:, half * 8:(half + 1) * 8, :], acc[:, :, :],
                           deps=[U["acc"][(tt, qd)] for tt in range(8) for qd in range(4)])
                dbg_acc_dep = [dacc]
            else:
                dbg_acc_dep = []

            for tt in range(8):
                acc_done = [U["acc"][(tt, qd)] for qd in range(4)] + dbg_acc_dep
                sil = S.op("S", lambda e, tt=tt: e.activation(gsil[:, :], acc[:, tt, :], AF.Silu),
                           deps=acc_done + [U["gsil"]])
                srload = DMA("dY", srl[:, :], s_scr[hoff + tt * 128:hoff + (tt + 1) * 128, :],
                             deps=[s_store_list[half * 8 + tt], U["srl"]])
                xqload = DMA("dY", xqrl[:, :], xq[hoff + tt * 128:hoff + (tt + 1) * 128, :],
                             deps=[U["xqrl"]])
                mu = S.op("V", lambda e: e.tensor_mul(fin[:, :], srl[:, :], gsil[:, :]),
                          deps=[sil, srload, U["fin"]])
                ad = S.op("V", lambda e: e.tensor_add(xqrl[:, :], fin[:, :], xqrl[:, :]),
                          deps=[mu, xqload])
                ost = DMA("dY", out[hoff + tt * 128:hoff + (tt + 1) * 128, :], xqrl[:, :],
                          deps=[ad])
                out_stores.append(ost)
                U["gsil"] = mu
                U["fin"] = ad
                U["srl"] = mu
                U["xqrl"] = ost
                for qd in range(4):
                    U["acc"][(tt, qd)] = sil

            U["xnqTh"] = [U["hT"][0], rt_steps[-1]]

        S.op("Y", lambda e: e.nop(), deps=out_stores)

        S.emit(nc, sems)
        P5.close()
    return nc


_PROG_CACHE = {}


def _get_prog(dbg=0, serialize=False):
    key = (dbg, serialize)
    if key not in _PROG_CACHE:
        _PROG_CACHE[key] = build_program(dbg=dbg, serialize=serialize)
    return _PROG_CACHE[key]


def make_in_maps(inputs, dbg=0):
    f32 = np.float32
    g = lambda k: np.ascontiguousarray(np.asarray(inputs[k], dtype=f32))
    x = g("L0_attn_x")
    WQ = g("L0_attn_WQ"); bQ = g("L0_attn_bQ")
    WK = g("L0_attn_WK"); bK = g("L0_attn_bK")
    WV = g("L0_attn_WV"); bV = g("L0_attn_bV")
    PJ = g("L0_attn_proj")
    sw1 = g("L0_ffn_shared_w1"); sb1 = g("L0_ffn_shared_b1")
    sw2 = g("L0_ffn_shared_w2"); sb2 = g("L0_ffn_shared_b2")
    rw1 = g("L0_ffn_routed_w1"); rb1 = g("L0_ffn_routed_b1")
    rw2 = g("L0_ffn_routed_w2"); rb2 = g("L0_ffn_routed_b2")
    cent = g("L0_ffn_centroids"); cb = g("L0_ffn_bias")

    common = {
        "ident": np.eye(128, dtype=BF),
        "wq": WQ,
        "bq_r": np.ascontiguousarray(bQ.reshape(16, 128).T),
        "wk": WK,
        "bk_r": np.ascontiguousarray(bK.reshape(4, 128).T),
        "wv": WV,
        "bv": bV.reshape(1, DK),
        "proj": PJ,
        "sw1": sw1,
        "sb1_r": np.ascontiguousarray(sb1.reshape(2, NFC, 128).transpose(2, 1, 0)),
        "sw2": sw2,
        "sb2": sb2,
        "rw1": rw1,
        "rb1_r": np.ascontiguousarray(rb1.reshape(NEXP, NFC, 128).transpose(2, 1, 0)),
        "rw2": rw2,
        "rb2": rb2,
        "cent_r": np.ascontiguousarray(cent.T),
        "cbias": cb.reshape(NEXP, 1),
    }
    masks = [build_masks(0), build_masks(1)]
    in_maps = []
    for core in range(8):
        b, p = core // 2, core % 2
        J = own_tile_ids(p)
        xq_host = np.ascontiguousarray(
            np.concatenate([x[b, 128 * j:128 * (j + 1), :] for j in J], axis=0))
        m = dict(common)
        m["xkv"] = x[b]
        m["xq"] = xq_host
        m["masks"] = masks[p]
        in_maps.append(m)
    return in_maps


def assemble(results):
    y = np.empty((4, TKV, D), dtype=np.float32)
    for core in range(8):
        b, p = core // 2, core % 2
        J = own_tile_ids(p)
        o = results[core]["out"]
        for i, j in enumerate(J):
            y[b, 128 * j:128 * (j + 1), :] = o[128 * i:128 * (i + 1), :]
    return y


def kernel(**inputs):
    nc = _get_prog(dbg=0)
    in_maps = make_in_maps(inputs)
    res = run_bass_kernel_spmd(nc, in_maps, core_ids=list(range(8)))
    return assemble(res.results)
